# version 01
# baseline (speedup 1.0000x reference)
import os
import numpy as np
import ml_dtypes
from contextlib import ExitStack

import concourse.bass as bass
import concourse.mybir as mybir
from concourse.bass_utils import run_bass_kernel_spmd

F32 = mybir.dt.float32
BF16 = mybir.dt.bfloat16
AF = mybir.ActivationFunctionType
ALU = mybir.AluOpType
BF = ml_dtypes.bfloat16

NCORES = 8
S, HD, DSSM, NHEAD = 8192, 4096, 8192, 64
NSTATE = 128
CONV_DIM = DSSM + 2 * NSTATE
PROJ_DIM = DSSM + CONV_DIM + NHEAD
T = S // NCORES
NCH, CK = 8, 128
EPS = 1e-5
NEGBIG = -1.0e30
SM_SHIFT = -32.0
SM_SCALE = 1.0 / float(np.sqrt(128.0))

ENGS = ("sync", "gpsimd", "tensor", "vector", "scalar")


class Prog:

    def __init__(self, nc, ctx):
        self.nc = nc
        self.ctx = ctx
        self.cmds = {e: [] for e in ENGS}
        self.ctrs = {}
        self.waits = {e: {} for e in ENGS}

    def ctr(self, name):
        if name not in self.ctrs:
            sem = self.ctx.enter_context(self.nc.semaphore(name))
            self.ctrs[name] = [sem, 0]
        return name

    def n(self, name):
        return self.ctrs[name][1]

    def wait(self, eng, name, val=None):
        if name not in self.ctrs:
            return
        sem, cnt = self.ctrs[name]
        v = cnt if val is None else val
        if v <= 0 or self.waits[eng].get(name, 0) >= v:
            return
        self.waits[eng][name] = v
        self.cmds[eng].append(lambda e, s=sem, vv=v: e.wait_ge(s, vv))

    def run(self, eng, fn, inc=None, k=1):
        if inc is None:
            self.cmds[eng].append(fn)
            return None
        self.ctr(inc)
        ent = self.ctrs[inc]
        self.cmds[eng].append(lambda e, f=fn, s=ent[0], kk=k: f(e).then_inc(s, kk))
        ent[1] += k
        return ent[1]

    def barrier(self, queue_map=None):
        queue_map = queue_map or {}
        for e, names in queue_map.items():
            for nm in names:
                self.wait(e, nm)
        for e in ENGS:
            for nm in list(self.ctrs):
                if not nm.startswith("bar_"):
                    self.wait(e, nm)
        vals = []
        for e in ENGS:
            nm = self.ctr(f"bar_{e}")
            ent = self.ctrs[nm]
            ent[1] += 1
            self.cmds[e].append(lambda en, s=ent[0]: en.nop().then_inc(s, 1))
            vals.append((ent[0], ent[1]))
        for e in ENGS:
            for s, v in vals:
                self.cmds[e].append(lambda en, ss=s, vv=v: en.wait_ge(ss, vv))
            for nm2, ent in self.ctrs.items():
                self.waits[e][nm2] = ent[1]

    def replay(self):
        with self.nc.Block() as block:
            @block.sync
            def _(e):
                for f in self.cmds["sync"]:
                    f(e)

            @block.gpsimd
            def _(e):
                for f in self.cmds["gpsimd"]:
                    f(e)

            @block.tensor
            def _(e):
                for f in self.cmds["tensor"]:
                    f(e)

            @block.vector
            def _(e):
                for f in self.cmds["vector"]:
                    f(e)

            @block.scalar
            def _(e):
                for f in self.cmds["scalar"]:
                    f(e)


def _blocked(w):
    Kt, Mt = w.shape
    return np.ascontiguousarray(
        w.reshape(Kt // 128, 128, Mt // 128, 128).transpose(2, 1, 0, 3).reshape(
            Mt // 128, 128, Kt)).astype(np.float32)


def build_A():
    nc = bass.Bass("TRN2", target_bir_lowering=False, debug=False, detect_race_conditions=False)
    i_xt = nc.dram_tensor("i_xt", [HD, 128 + T], F32, kind="ExternalInput")
    i_win = nc.dram_tensor("i_win", [131, 128, 4096], F32, kind="ExternalInput")
    i_kern = nc.dram_tensor("i_kern", [32, 128, 4096], F32, kind="ExternalInput")
    i_bias = nc.dram_tensor("i_bias", [128, 32], F32, kind="ExternalInput")
    i_cw = nc.dram_tensor("i_cw", [128, 264], F32, kind="ExternalInput")
    i_dtb = nc.dram_tensor("i_dtb", [64, 1], F32, kind="ExternalInput")
    i_alog = nc.dram_tensor("i_alog", [64, 1], F32, kind="ExternalInput")
    i_drow = nc.dram_tensor("i_drow", [1, 64], F32, kind="ExternalInput")
    i_ge = nc.dram_tensor("i_ge", [128, 128], F32, kind="ExternalInput")
    i_la = nc.dram_tensor("i_la", [4, 128, 4096], F32, kind="ExternalInput")
    i_lbt = nc.dram_tensor("i_lbt", [4, 128, 4096], F32, kind="ExternalInput")
    i_lat = nc.dram_tensor("i_lat", [128, 1024], F32, kind="ExternalInput")
    i_lbst = nc.dram_tensor("i_lbst", [32, 4096], F32, kind="ExternalInput")
    i_ident = nc.dram_tensor("i_ident", [128, 128], BF16, kind="ExternalInput")
    i_identf = nc.dram_tensor("i_identf", [128, 128], F32, kind="ExternalInput")
    i_mp = nc.dram_tensor("i_mp", [128, 128], F32, kind="ExternalInput")
    i_ones = nc.dram_tensor("i_ones", [128, 128], F32, kind="ExternalInput")
    i_onescol = nc.dram_tensor("i_onescol", [128, 1], F32, kind="ExternalInput")
    i_misc = nc.dram_tensor("i_misc", [1, 4], F32, kind="ExternalInput")

    o_siluz = nc.dram_tensor("o_siluz", [64, 128, T], BF16, kind="ExternalOutput")
    o_xbcc = nc.dram_tensor("o_xbcc", [66, 128, T], BF16, kind="ExternalOutput")
    o_ylc = nc.dram_tensor("o_ylc", [NCH, 128, 8192], BF16, kind="ExternalOutput")
    o_g = nc.dram_tensor("o_g", [32, 128, T], F32, kind="ExternalOutput")
    o_lamg = nc.dram_tensor("o_lamg", [64, T], F32, kind="ExternalOutput")
    o_scor = nc.dram_tensor("o_scor", [128, 8192], BF16, kind="ExternalOutput")

    d_row1 = nc.dram_tensor("d_row1", [8192], F32, kind="Internal")
    d_row2 = nc.dram_tensor("d_row2", [64], F32, kind="Internal")

    ctx = ExitStack()
    with ctx:
        pr = Prog(nc, ctx)
        sbt = lambda nm, sh, dt: ctx.enter_context(nc.sbuf_tensor(nm, sh, dt))
        pst = lambda nm, sh, dt=F32: ctx.enter_context(nc.psum_tensor(nm, sh, dt))

        bk = [pst(f"bk{i}", [128, 512]) for i in range(5)]
        pT = [pst(f"pT{i}", [128, 1024], BF16) for i in range(2)]

        ident = sbt("ident", [128, 128], BF16)
        identf = sbt("identf", [128, 128], F32)
        mp = sbt("mp", [128, 128], F32)
        onescol = sbt("onescol", [128, 1], F32)
        ones1r = sbt("ones1r", [1, 128], F32)
        cw = sbt("cw", [128, 264], F32)
        dtb = sbt("dtb", [64, 1], F32)
        nega = sbt("nega", [64, 1], F32)
        alog = sbt("alog", [64, 1], F32)
        drow = sbt("drow", [1, 64], F32)
        drowbc = sbt("drowbc", [128, 64], F32)
        biasfm = sbt("biasfm", [128, 32], F32)
        epsc = sbt("epsc", [1, 1], F32)
        onec = sbt("onec", [1, 1], F32)
        halobuf = sbt("halobuf", [128, 66 * 4], BF16)
        dtt = sbt("dtt", [64, T], F32)
        lamg = sbt("lamg", [64, T], F32)
        hst = sbt("hst", [128, 64 * 128], BF16)


        c_ld = pr.ctr("c_ld")
        c_gld = pr.ctr("c_gld")
        c_st = pr.ctr("c_st")
        c_pe = pr.ctr("c_pe")
        c_dve = pr.ctr("c_dve")
        c_act = pr.ctr("c_act")
        QMAP = {"sync": [c_ld], "gpsimd": [c_gld, c_st]}

        for o, i in ((ident[:], i_ident[:]), (identf[:], i_identf[:]), (mp[:], i_mp[:]),
                     (onescol[:], i_onescol[:]), (ones1r[:], i_ones[0:1, :]),
                     (epsc[:], i_misc[0:1, 0:1]), (onec[:], i_misc[0:1, 1:2]),
                     (cw[:], i_cw[:]), (dtb[:], i_dtb[:]), (alog[:], i_alog[:]),
                     (drow[:], i_drow[:]), (biasfm[:], i_bias[:])):
            pr.run("sync", lambda e, oo=o, ii=i: e.dma_start(out=oo, in_=ii), inc=c_ld, k=16)
        pr.run("gpsimd", lambda e: e.dma_start(out=gecol[:], in_=i_ge[:]), inc=c_gld, k=16)
        pr.run("gpsimd", lambda e: e.dma_start(out=asht[:], in_=i_lat[:]), inc=c_gld, k=16)
        pr.run("gpsimd", lambda e: e.dma_start(out=bsht[:], in_=i_lbst[:]), inc=c_gld, k=16)

        pr.wait("scalar", c_ld)
        pr.run("scalar", lambda e: e.activation(nega[:], alog[:], AF.Exp), inc=c_act)
        pr.wait("vector", c_act)
        pr.run("vector", lambda e: e.tensor_scalar_mul(nega[:], nega[:], -1.0), inc=c_dve)
        pr.wait("tensor", c_ld)
        pr.run("tensor", lambda e: e.matmul(bk[4][:, 0:64], ones1r[:], drow[:], start=True, stop=True), inc=c_pe)
        pr.wait("vector", c_pe)
        pr.run("vector", lambda e: e.tensor_copy(drowbc[:], bk[4][:, 0:64]), inc=c_dve)
        pr.run("vector", lambda e: e.memset(hst[:], 0.0), inc=c_dve)

        A_LORA = int(os.environ.get("A_LORA", "1"))
        lsc = ExitStack()
        aeff = lsc.enter_context(nc.sbuf_tensor("aeff", [128, 32 * 32], BF16))
        asht = lsc.enter_context(nc.sbuf_tensor("asht", [128, 32 * 32], BF16))
        beff = lsc.enter_context(nc.sbuf_tensor("beff", [32, 4096], BF16))
        bsht = lsc.enter_context(nc.sbuf_tensor("bsht", [32, 4096], BF16))
        gecol = lsc.enter_context(nc.sbuf_tensor("gecol", [128, 4 * 32], BF16))
        with ExitStack() as lctx:
          if A_LORA:
            lab = [lctx.enter_context(nc.sbuf_tensor(f"lab{kt}", [128, 4096], BF16)) for kt in range(4)]
            pr.wait("gpsimd", c_dve)
            for kt in range(4):
                pr.run("gpsimd", lambda e, kt=kt: e.dma_start(out=lab[kt][:], in_=i_la[kt]), inc=c_gld, k=16)
            pr.wait("tensor", c_gld)
            pr.wait("tensor", c_dve)
            for kc in range(32):
                for kt in range(4):
                    pr.run("tensor", lambda e, kc=kc, kt=kt: e.matmul(
                        bk[0][:, 0:32], lab[kt][:, kc * 128:(kc + 1) * 128],
                        gecol[:, kt * 32:(kt + 1) * 32],
                        start=(kt == 0), stop=(kt == 3)), inc=(c_pe if kt == 3 else None))
                pr.wait("vector", c_pe)
                pr.run("vector", lambda e, kc=kc: e.tensor_copy(aeff[:, kc * 32:(kc + 1) * 32], bk[0][:, 0:32]), inc=c_dve)
                pr.wait("tensor", c_dve)
            pr.wait("gpsimd", c_pe)
            for kt in range(4):
                pr.run("gpsimd", lambda e, kt=kt: e.dma_start(out=lab[kt][:], in_=i_lbt[kt]), inc=c_gld, k=16)
            pr.wait("tensor", c_gld)
            for pc in range(8):
                for kt in range(4):
                    pr.run("tensor", lambda e, pc=pc, kt=kt: e.matmul(
                        bk[0][0:32, :], gecol[:, kt * 32:(kt + 1) * 32],
                        lab[kt][:, pc * 512:(pc + 1) * 512],
                        start=(kt == 0), stop=(kt == 3)), inc=(c_pe if kt == 3 else None))
                pr.wait("vector", c_pe)
                pr.run("vector", lambda e, pc=pc: e.tensor_copy(beff[:, pc * 512:(pc + 1) * 512], bk[0][0:32, :]), inc=c_dve)
                pr.wait("tensor", c_dve)

        with ExitStack() as p1:
            x_sb = p1.enter_context(nc.sbuf_tensor("x_sb", [128, 32 * 512], F32))
            xn_sb = p1.enter_context(nc.sbuf_tensor("xn_sb", [128, 32 * 512], BF16))
            wblk = [p1.enter_context(nc.sbuf_tensor(f"wblk{i}", [128, 32 * 128], BF16)) for i in range(2)]
            kblk = wblk
            zbuf = [p1.enter_context(nc.sbuf_tensor(f"zbuf{i}", [128, 512], BF16)) for i in range(2)]
            xcb = [p1.enter_context(nc.sbuf_tensor(f"xcb{i}", [128, 512], BF16)) for i in range(2)]
            gb = [p1.enter_context(nc.sbuf_tensor(f"gb{i}", [128, 512], F32)) for i in range(2)]
            xwin = p1.enter_context(nc.sbuf_tensor("xwin", [128, 516], BF16))
            diag = p1.enter_context(nc.sbuf_tensor("diag", [128, 4 * 128], BF16))
            x2t = [p1.enter_context(nc.sbuf_tensor(f"x2t{i}", [128, 512], F32)) for i in range(2)]
            mrow = p1.enter_context(nc.sbuf_tensor("mrow", [1, 512], F32))
            rrow = p1.enter_context(nc.sbuf_tensor("rrow", [1, 512], F32))
            srow = p1.enter_context(nc.sbuf_tensor("srow", [1, 512], F32))
            mbc = p1.enter_context(nc.sbuf_tensor("mbc", [128, 512], F32))
            rbc = p1.enter_context(nc.sbuf_tensor("rbc", [128, 512], F32))
            tmpf = p1.enter_context(nc.sbuf_tensor("tmpf", [128, 512], F32))
            axb = p1.enter_context(nc.sbuf_tensor("axb", [32, 512], BF16))
            axsb = p1.enter_context(nc.sbuf_tensor("axsb", [32, 512], BF16))
            spA = p1.enter_context(nc.sbuf_tensor("spA", [64, 512], F32))
            spB = p1.enter_context(nc.sbuf_tensor("spB", [64, 512], F32))
            spC = p1.enter_context(nc.sbuf_tensor("spC", [64, 512], F32))
            dtsc = p1.enter_context(nc.sbuf_tensor("dtsc", [64, T], F32))

            def ln_xn(ncols, xsrc, xn_out):
                pr.wait("sync", c_dve)
                pr.wait("sync", c_pe)
                for kc in range(32):
                    pr.run("sync", lambda e, kc=kc: e.dma_start(
                        out=x_sb[:, kc * 512:kc * 512 + ncols],
                        in_=i_xt[kc * 128:(kc + 1) * 128, xsrc:xsrc + ncols]), inc=c_ld, k=16)
                pr.wait("scalar", c_ld)
                pr.wait("scalar", c_pe)
                pr.wait("tensor", c_ld)
                pr.wait("tensor", c_act)
                pr.wait("tensor", c_dve)
                for kc in range(32):
                    pr.run("tensor", lambda e, kc=kc: e.matmul(
                        bk[3][0:1, 0:ncols], onescol[:], x_sb[:, kc * 512:kc * 512 + ncols],
                        start=(kc == 0), stop=(kc == 31)), inc=None)
                sq, x2v = {}, {}
                for kc in range(32):
                    if kc >= 2:
                        pr.wait("scalar", c_pe, x2v[kc - 2])
                    sq[kc] = pr.run("scalar", lambda e, kc=kc: e.activation(
                        x2t[kc % 2][:, 0:ncols], x_sb[:, kc * 512:kc * 512 + ncols], AF.Square), inc=c_act)
                    pr.wait("tensor", c_act, sq[kc])
                    x2v[kc] = pr.run("tensor", lambda e, kc=kc: e.matmul(
                        bk[4][0:1, 0:ncols], onescol[:], x2t[kc % 2][:, 0:ncols],
                        start=(kc == 0), stop=(kc == 31)), inc=c_pe)
                pr.wait("vector", c_pe)
                pr.run("vector", lambda e: e.tensor_scalar_mul(mrow[0:1, 0:ncols], bk[3][0:1, 0:ncols], 1.0 / HD), inc=c_dve)
                pr.run("vector", lambda e: e.tensor_scalar_mul(srow[0:1, 0:ncols], bk[4][0:1, 0:ncols], 1.0 / HD), inc=c_dve)
                pr.wait("vector", c_dve)
                pr.run("vector", lambda e: e.tensor_mul(rrow[0:1, 0:ncols], mrow[0:1, 0:ncols], mrow[0:1, 0:ncols]), inc=c_dve)
                pr.wait("vector", c_dve)
                pr.run("vector", lambda e: e.tensor_sub(srow[0:1, 0:ncols], srow[0:1, 0:ncols], rrow[0:1, 0:ncols]), inc=c_dve)
                pr.wait("scalar", c_dve)
                pr.run("scalar", lambda e: e.activation(srow[0:1, 0:ncols], srow[0:1, 0:ncols], AF.Sqrt, bias=epsc[:]), inc=c_act)
                pr.wait("vector", c_act)
                pr.run("vector", lambda e: e.reciprocal(rrow[0:1, 0:ncols], srow[0:1, 0:ncols]), inc=c_dve)
                pr.wait("tensor", c_dve)
                pr.run("tensor", lambda e: e.matmul(bk[3][:, 0:ncols], ones1r[:], mrow[0:1, 0:ncols], start=True, stop=True), inc=None)
                pr.run("tensor", lambda e: e.matmul(bk[4][:, 0:ncols], ones1r[:], rrow[0:1, 0:ncols], start=True, stop=True), inc=c_pe)
                pr.wait("vector", c_pe)
                pr.run("vector", lambda e: e.tensor_copy(mbc[:, 0:ncols], bk[3][:, 0:ncols]), inc=c_dve)
                pr.run("vector", lambda e: e.tensor_copy(rbc[:, 0:ncols], bk[4][:, 0:ncols]), inc=c_dve)
                pr.wait("vector", c_dve)
                for kc in range(32):
                    pr.run("vector", lambda e, kc=kc: e.tensor_sub(
                        tmpf[:, 0:ncols], x_sb[:, kc * 512:kc * 512 + ncols], mbc[:, 0:ncols]), inc=c_dve)
                    pr.wait("vector", c_dve)
                    pr.run("vector", lambda e, kc=kc: e.tensor_mul(
                        xn_out(kc)[:, 0:ncols], tmpf[:, 0:ncols], rbc[:, 0:ncols]), inc=c_dve)

            A_NG = int(os.environ.get("A_NG", "2"))
            A_NMC = int(os.environ.get("A_NMC", "131"))
            A_FFN = int(os.environ.get("A_FFN", "1"))
            passes = [(128, None, 0)] + [(512, g, 128 + g * 512) for g in range(A_NG)]
            if A_NG == 0:
                passes = []
            for (ncols, g, xoff) in passes:
                halo_pass = g is None
                ln_xn(ncols, xoff, lambda kc: xn_sb[:, kc * 512:kc * 512 + ncols])

                for Mc in range(A_NMC if int(os.environ.get("A_PROJ", "1")) else 0):
                    wb = wblk[Mc % 2]
                    pr.wait("gpsimd", c_pe)
                    pr.run("gpsimd", lambda e, Mc=Mc, wb=wb: e.dma_start(
                        out=wb[:], in_=i_win[Mc]), inc=c_gld, k=16)
                    MCPART = int(os.environ.get("A_MCPART", "3"))
                    if MCPART < 2:
                        continue
                    pr.wait("tensor", c_gld)
                    pr.wait("tensor", c_dve)
                    pr.wait("tensor", c_act)
                    pb = bk[Mc % 3]
                    for kc in range(32):
                        pr.run("tensor", lambda e, kc=kc, wb=wb, pb=pb, ncols=ncols: e.matmul(
                            pb[:, 0:ncols], wb[:, kc * 128:(kc + 1) * 128],
                            xn_sb[:, kc * 512:kc * 512 + ncols],
                            start=(kc == 0), stop=(kc == 31)),
                            inc=(None if kc < 31 else c_pe))
                    pev = pr.n(c_pe)
                    if MCPART < 3:
                        continue

                    if Mc < 64:
                        if halo_pass:
                            continue
                        zb = zbuf[Mc % 2]
                        pr.wait("scalar", c_pe, pev)
                        pr.wait("scalar", c_st)
                        pr.run("scalar", lambda e, zb=zb, pb=pb: e.activation(zb[:], pb[:], AF.Silu), inc=c_act)
                        pr.wait("gpsimd", c_act)
                        pr.run("gpsimd", lambda e, Mc=Mc, g=g, zb=zb: e.dma_start(
                            out=o_siluz[Mc, :, g * 512:(g + 1) * 512], in_=zb[:]), inc=c_st, k=16)
                    elif Mc < 130:
                        j = Mc - 64
                        pr.wait("vector", c_act)
                        pr.wait("vector", c_pe, pev)
                        pr.run("vector", lambda e, j=j: e.tensor_copy(xwin[:, 0:4], halobuf[:, j * 4:(j + 1) * 4]), inc=None)
                        pr.run("vector", lambda e, pb=pb, ncols=ncols: e.tensor_copy(
                            xwin[:, 4:4 + ncols], pb[:, 0:ncols]), inc=None)
                        pr.run("vector", lambda e, j=j, pb=pb, ncols=ncols: e.tensor_copy(
                            halobuf[:, j * 4:(j + 1) * 4], pb[:, ncols - 4:ncols]), inc=None)
                        for k in range(4):
                            pr.run("vector", lambda e, j=j, k=k: e.tensor_scalar_mul(
                                diag[:, k * 128:(k + 1) * 128], ident[:], cw[:, j * 4 + k:j * 4 + k + 1]),
                                inc=(None if k < 3 else c_dve))
                        dvev = pr.n(c_dve)
                        if halo_pass:
                            continue
                        cb = bk[3 + (j % 2)]
                        pr.wait("tensor", c_dve, dvev)
                        pr.wait("tensor", c_act)
                        for k in range(4):
                            pr.run("tensor", lambda e, k=k, cb=cb, ncols=ncols: e.matmul(
                                cb[:, 0:ncols], diag[:, k * 128:(k + 1) * 128], xwin[:, k + 1:k + 1 + ncols],
                                start=(k == 0), stop=(k == 3)), inc=(c_pe if k == 3 else None))
                        xc = xcb[j % 2]
                        pr.wait("scalar", c_pe)
                        pr.wait("scalar", c_st)
                        pr.run("scalar", lambda e, xc=xc, cb=cb: e.activation(xc[:], cb[:], AF.Silu), inc=c_act)
                        pr.wait("gpsimd", c_act)
                        pr.run("gpsimd", lambda e, j=j, g=g, xc=xc: e.dma_start(
                            out=o_xbcc[j, :, g * 512:(g + 1) * 512], in_=xc[:]), inc=c_st, k=16)
                    else:
                        if halo_pass:
                            continue
                        pr.wait("scalar", c_pe, pev)
                        pr.run("scalar", lambda e, pb=pb: e.activation(spA[:], pb[0:64, :], AF.Abs, bias=dtb[:]), inc=c_act)
                        pr.run("scalar", lambda e, pb=pb: e.activation(spB[:], pb[0:64, :], AF.Relu, bias=dtb[:]), inc=c_act)
                        pr.wait("scalar", c_act)
                        pr.run("scalar", lambda e: e.activation(spC[:], spA[:], AF.Exp, scale=-1.0), inc=c_act)
                        pr.wait("scalar", c_act)
                        pr.run("scalar", lambda e: e.activation(spA[:], spC[:], AF.Ln, bias=onescol[0:64, :]), inc=c_act)
                        pr.wait("vector", c_act)
                        pr.run("vector", lambda e, g=g: e.tensor_add(
                            dtt[:, g * 512:(g + 1) * 512], spB[:], spA[:]), inc=c_dve)

                if not A_FFN or halo_pass:
                    continue
                pr.wait("tensor", c_dve)
                pr.wait("tensor", c_act)
                for kc in range(32):
                    pr.run("tensor", lambda e, kc=kc: e.matmul(
                        bk[3][0:32, :], aeff[:, kc * 32:(kc + 1) * 32], xn_sb[:, kc * 512:(kc + 1) * 512],
                        start=(kc == 0), stop=(kc == 31)), inc=None)
                for kc in range(32):
                    pr.run("tensor", lambda e, kc=kc: e.matmul(
                        bk[4][0:32, :], asht[:, kc * 32:(kc + 1) * 32], xn_sb[:, kc * 512:(kc + 1) * 512],
                        start=(kc == 0), stop=(kc == 31)), inc=(None if kc < 31 else c_pe))
                pr.wait("vector", c_pe)
                pr.run("vector", lambda e: e.tensor_copy(axb[:], bk[3][0:32, :]), inc=None)
                pr.run("vector", lambda e: e.tensor_copy(axsb[:], bk[4][0:32, :]), inc=c_dve)
                for Mc in range(32):
                    kb = kblk[Mc % 2]
                    pr.wait("gpsimd", c_pe)
                    pr.run("gpsimd", lambda e, Mc=Mc, kb=kb: e.dma_start(
                        out=kb[:], in_=i_kern[Mc]), inc=c_gld, k=16)
                    pr.wait("tensor", c_gld)
                    pr.wait("tensor", c_dve)
                    pr.wait("tensor", c_act)
                    pb = bk[Mc % 3]
                    for kc in range(32):
                        pr.run("tensor", lambda e, kc=kc, kb=kb, pb=pb: e.matmul(
                            pb[:], kb[:, kc * 128:(kc + 1) * 128], xn_sb[:, kc * 512:(kc + 1) * 512],
                            start=(kc == 0), stop=False), inc=None)
                    pr.run("tensor", lambda e, Mc=Mc, pb=pb: e.matmul(
                        pb[:], beff[:, Mc * 128:(Mc + 1) * 128], axb[:], start=False, stop=False), inc=None)
                    pr.run("tensor", lambda e, Mc=Mc, pb=pb: e.matmul(
                        pb[:], bsht[:, Mc * 128:(Mc + 1) * 128], axsb[:], start=False, stop=True), inc=c_pe)
                    pev = pr.n(c_pe)
                    gbx = gb[Mc % 2]
                    pr.wait("scalar", c_pe, pev)
                    pr.wait("scalar", c_st)
                    pr.run("scalar", lambda e, Mc=Mc, gbx=gbx, pb=pb: e.activation(
                        gbx[:], pb[:], AF.Silu, bias=biasfm[:, Mc:Mc + 1]), inc=c_act)
                    pr.wait("gpsimd", c_act)
                    pr.run("gpsimd", lambda e, Mc=Mc, g=g, gbx=gbx: e.dma_start(
                        out=o_g[Mc, :, g * 512:(g + 1) * 512], in_=gbx[:]), inc=c_st, k=16)

            pr.wait("vector", c_dve)
            pr.wait("vector", c_act)
            pr.run("vector", lambda e: e.tensor_scalar_mul(dtsc[:], dtt[:], nega[:]), inc=c_dve)
            pr.wait("vector", c_dve)
            pr.run("vector", lambda e: e.tensor_tensor_scan(lamg[:], dtsc[:], dtsc[:], 0.0, ALU.add, ALU.bypass), inc=c_dve)
            pr.wait("gpsimd", c_dve)
            pr.run("gpsimd", lambda e: e.dma_start(out=o_lamg[:], in_=lamg[:]), inc=c_st, k=16)
        lsc.close()
        pr.barrier(QMAP)

        with ExitStack() as p2:
            xcc = [p2.enter_context(nc.sbuf_tensor(f"xcc{i}", [128, 64 * 128], BF16)) for i in range(2)]
            Bt = p2.enter_context(nc.sbuf_tensor("Bt", [128, T], BF16))
            Ct = p2.enter_context(nc.sbuf_tensor("Ct", [128, T], BF16))
            dtx = p2.enter_context(nc.sbuf_tensor("dtx", [128, 8192], BF16))
            mask = p2.enter_context(nc.sbuf_tensor("mask", [128, 8192], BF16))
            cbtm = p2.enter_context(nc.sbuf_tensor("cbtm", [128, 8192], BF16))
            cwt = p2.enter_context(nc.sbuf_tensor("cwt", [128, 8192], BF16))
            ylcb = p2.enter_context(nc.sbuf_tensor("ylcb", [128, 8192], BF16))
            lrowp = [p2.enter_context(nc.sbuf_tensor(f"lrowp{i}", [1, 512], F32)) for i in range(2)]
            ltrow = p2.enter_context(nc.sbuf_tensor("ltrow", [1, 64], F32))
            argp = [p2.enter_context(nc.sbuf_tensor(f"argp{i}", [128, 512], F32)) for i in range(2)]
            expp = p2.enter_context(nc.sbuf_tensor("expp", [128, 512], BF16))
            lamfm = p2.enter_context(nc.sbuf_tensor("lamfm", [64, 128], F32))
            lamTM = p2.enter_context(nc.sbuf_tensor("lamTM", [128, 64], F32))
            dttTM = p2.enter_context(nc.sbuf_tensor("dttTM", [128, 64], F32))
            ltotbc = p2.enter_context(nc.sbuf_tensor("ltotbc", [128, 64], F32))
            endw = p2.enter_context(nc.sbuf_tensor("endw", [128, 64], F32))
            atotbc = p2.enter_context(nc.sbuf_tensor("atotbc", [128, 64], F32))
            cbtraw = p2.enter_context(nc.sbuf_tensor("cbtraw", [128, 128], BF16))
            btm = p2.enter_context(nc.sbuf_tensor("btm", [128, 128], BF16))
            bwb = [p2.enter_context(nc.sbuf_tensor(f"bwb{i}", [128, 128], BF16)) for i in range(2)]

            pr.run("sync", lambda e: e.dma_start(out=Bt[:], in_=o_xbcc[64]), inc=c_ld, k=16)
            pr.run("sync", lambda e: e.dma_start(out=Ct[:], in_=o_xbcc[65]), inc=c_ld, k=16)
            btct_v = pr.n(c_ld)

            def load_xcc(c):
                buf = xcc[c % 2]
                pr.wait("sync", c_dve)
                pr.wait("sync", c_pe)
                pr.wait("gpsimd", c_dve)
                pr.wait("gpsimd", c_pe)
                for m in range(64):
                    eng, cc = ("sync", c_ld) if m % 2 == 0 else ("gpsimd", c_gld)
                    pr.run(eng, lambda e, m=m, c=c, buf=buf: e.dma_start(
                        out=buf[:, m * 128:(m + 1) * 128], in_=o_xbcc[m, :, c * 128:(c + 1) * 128]),
                        inc=cc, k=16)

            A_NCH = int(os.environ.get("A_NCH", str(NCH)))
            if A_NCH > 0:
                load_xcc(0)
            for c in range(A_NCH):
                buf = xcc[c % 2]
                ld_mark = (pr.n(c_ld), pr.n(c_gld))
                if c + 1 < NCH:
                    load_xcc(c + 1)

                if c == 0:
                    pr.run("vector", lambda e: e.tensor_copy(lamfm[:], lamg[:, 0:128]), inc=c_dve)
                else:
                    pr.run("vector", lambda e, c=c: e.tensor_scalar(
                        lamfm[:], lamg[:, c * 128:(c + 1) * 128], lamg[:, c * 128 - 1:c * 128], None,
                        ALU.subtract), inc=c_dve)
                pr.wait("gpsimd", c_dve)
                pr.run("gpsimd", lambda e: e.dma_start(
                    out=d_row1[:].rearrange("(h t) -> h t", h=64), in_=lamfm[:]), inc=c_st, k=16)
                pr.run("gpsimd", lambda e: e.dma_start(
                    out=d_row2[:].rearrange("(h o) -> h o", h=64),
                    in_=lamfm[:, 127:128]), inc=c_st, k=16)
                pr.wait("sync", c_st)
                pr.run("sync", lambda e: e.dma_start(out=ltrow[:], in_=d_row2[:].unsqueeze(0)), inc=c_ld, k=16)
                row_v = pr.n(c_ld)

                A_P2 = int(os.environ.get("A_P2", "9"))
                if A_P2 < 2:
                    continue
                pr.wait("tensor", c_dve)
                pr.run("tensor", lambda e: e.transpose(bk[4][0:128, 0:64], lamfm[:], identf[0:64, 0:64]), inc=c_pe)
                pr.wait("vector", c_pe)
                pr.run("vector", lambda e: e.tensor_copy(lamTM[:], bk[4][0:128, 0:64]), inc=c_dve)
                pr.wait("tensor", c_dve)
                pr.run("tensor", lambda e, c=c: e.transpose(bk[4][0:128, 0:64], dtt[:, c * 128:(c + 1) * 128], identf[0:64, 0:64]), inc=c_pe)
                pr.wait("vector", c_pe)
                pr.run("vector", lambda e: e.tensor_copy(dttTM[:], bk[4][0:128, 0:64]), inc=c_dve)
                pr.wait("tensor", c_ld, row_v)
                pr.wait("tensor", c_dve)
                pr.run("tensor", lambda e: e.matmul(bk[4][:, 0:64], ones1r[:], ltrow[:], start=True, stop=True), inc=c_pe)
                pr.wait("vector", c_pe)
                pr.run("vector", lambda e: e.tensor_copy(ltotbc[:], bk[4][:, 0:64]), inc=c_dve)
                pr.wait("vector", c_dve)
                pr.run("vector", lambda e: e.scalar_tensor_tensor(
                    endw[:], lamTM[:], -1.0, ltotbc[:], ALU.mult, ALU.add), inc=c_dve)
                pr.wait("scalar", c_dve)
                pr.run("scalar", lambda e: e.activation(endw[:], endw[:], AF.Exp), inc=None)
                pr.run("scalar", lambda e: e.activation(atotbc[:], ltotbc[:], AF.Exp), inc=c_act)
                pr.wait("tensor", c_ld, btct_v)
                pr.wait("tensor", c_dve)
                pr.run("tensor", lambda e, c=c: e.matmul(
                    bk[4][:, 0:128], Bt[:, c * 128:(c + 1) * 128], Ct[:, c * 128:(c + 1) * 128],
                    start=True, stop=True), inc=c_pe)
                pr.wait("vector", c_pe)
                pr.run("vector", lambda e: e.tensor_copy(cbtraw[:], bk[4][:, 0:128]), inc=c_dve)

                if A_P2 < 3:
                    continue
                for p in range(16):
                    pb = bk[p % 2]
                    lrp = lrowp[p % 2]
                    pr.wait("sync", c_pe)
                    pr.run("sync", lambda e, p=p, lrp=lrp: e.dma_start(
                        out=lrp[:], in_=d_row1[p * 512:(p + 1) * 512].unsqueeze(0)), inc=c_ld, k=16)
                    pr.wait("tensor", c_dve)
                    pr.wait("tensor", c_act)
                    pr.wait("tensor", c_ld)
                    pr.run("tensor", lambda e, lrp=lrp, pb=pb: e.matmul(
                        pb[:], ones1r[:], lrp[:], start=True, stop=True), inc=c_pe)
                    pev = pr.n(c_pe)
                    ap = argp[p % 2]
                    pr.wait("vector", c_pe, pev)
                    for j in range(4):
                        h = p * 4 + j
                        pr.run("vector", lambda e, j=j, h=h, ap=ap, pb=pb: e.scalar_tensor_tensor(
                            ap[:, j * 128:(j + 1) * 128], pb[:, j * 128:(j + 1) * 128],
                            lamTM[:, h:h + 1], mp[:], ALU.subtract, ALU.add),
                            inc=(None if j < 3 else c_dve))
                    pr.wait("scalar", c_pe, pev)
                    pr.wait("scalar", c_dve)
                    pr.run("scalar", lambda e, pb=pb: e.activation(expp[:], pb[:], AF.Exp), inc=None)
                    pr.run("scalar", lambda e, p=p, ap=ap: e.activation(
                        mask[:, p * 512:(p + 1) * 512], ap[:], AF.Exp), inc=c_act)
                    pr.wait("vector", c_act)
                    pr.run("vector", lambda e, p=p, c=c: e.tensor_tensor(
                        cwt[:, p * 512:(p + 1) * 512].rearrange("q (a b) -> q a b", a=4),
                        expp[:].rearrange("q (a b) -> q a b", a=4),
                        Ct[:, c * 128:(c + 1) * 128].unsqueeze(1).broadcast_to([128, 4, 128]),
                        ALU.mult), inc=c_dve)

                pr.wait("vector", c_dve)
                pr.run("vector", lambda e: e.tensor_tensor(
                    cbtm[:].rearrange("q (a b) -> q a b", a=64),
                    mask[:].rearrange("q (a b) -> q a b", a=64),
                    cbtraw[:].unsqueeze(1).broadcast_to([128, 64, 128]), ALU.mult), inc=c_dve)

                if A_P2 < 4:
                    continue
                pr.wait("tensor", c_ld, ld_mark[0])
                pr.wait("tensor", c_gld, ld_mark[1])
                pr.wait("tensor", c_dve)
                for m in range(64):
                    pt = pT[m % 2]
                    if m >= 2:
                        pr.wait("tensor", c_dve, dtx_ev[m - 2])
                    pr.run("tensor", lambda e, m=m, buf=buf, pt=pt: e.transpose(
                        pt[:, 0:128], buf[:, m * 128:(m + 1) * 128], ident[:]), inc=c_pe)
                    pr.wait("vector", c_pe)
                    if m == 0:
                        dtx_ev = {}
                    dtx_ev[m] = pr.run("vector", lambda e, m=m, pt=pt: e.tensor_scalar_mul(
                        dtx[:, m * 128:(m + 1) * 128], pt[:, 0:128], dttTM[:, m:m + 1]), inc=c_dve)
                pr.wait("tensor", c_dve)
                pr.run("tensor", lambda e, c=c: e.transpose(pT[0][:, 0:128], Bt[:, c * 128:(c + 1) * 128], ident[:]), inc=c_pe)
                pr.wait("vector", c_pe)
                pr.run("vector", lambda e: e.tensor_copy(btm[:], pT[0][:, 0:128]), inc=c_dve)

                if A_P2 < 5:
                    continue
                pr.wait("tensor", c_dve)
                pr.wait("tensor", c_act)
                pr.wait("vector", c_dve)
                pe_h, dve_h = {}, {}
                for h in range(64):
                    pyb = bk[h % 2]
                    psb = bk[2 + (h % 2)]
                    pr.wait("vector", c_pe, pe_h.get(h - 2, 0))
                    pr.run("vector", lambda e, h=h: e.tensor_scalar_mul(
                        bwb[h % 2][:], btm[:], endw[:, h:h + 1]), inc=c_dve)
                    bwv = pr.n(c_dve)
                    pr.wait("tensor", c_dve, bwv)
                    pr.run("tensor", lambda e, h=h, pyb=pyb: e.matmul(
                        pyb[:, 0:128], dtx[:, h * 128:(h + 1) * 128], cbtm[:, h * 128:(h + 1) * 128],
                        start=True, stop=False), inc=None)
                    pr.run("tensor", lambda e, h=h, pyb=pyb: e.matmul(
                        pyb[:, 0:128], hst[:, h * 128:(h + 1) * 128], cwt[:, h * 128:(h + 1) * 128],
                        start=False, stop=True), inc=None)
                    pr.run("tensor", lambda e, h=h, psb=psb: e.matmul(
                        psb[:, 0:128], bwb[h % 2][:], dtx[:, h * 128:(h + 1) * 128],
                        start=True, stop=True), inc=c_pe)
                    pe_h[h] = pr.n(c_pe)
                    pr.wait("vector", c_pe, pe_h[h])
                    pr.run("vector", lambda e, h=h, psb=psb: e.scalar_tensor_tensor(
                        hst[:, h * 128:(h + 1) * 128], hst[:, h * 128:(h + 1) * 128],
                        atotbc[:, h:h + 1], psb[:, 0:128], ALU.mult, ALU.add), inc=None)
                    pr.run("vector", lambda e, h=h, buf=buf, pyb=pyb: e.scalar_tensor_tensor(
                        ylcb[:, h * 128:(h + 1) * 128], buf[:, h * 128:(h + 1) * 128],
                        drowbc[:, h:h + 1], pyb[:, 0:128], ALU.mult, ALU.add), inc=c_dve)
                    dve_h[h] = pr.n(c_dve)

                if A_P2 < 6:
                    continue
                pr.wait("gpsimd", c_dve)
                pr.run("gpsimd", lambda e, c=c: e.dma_start(out=o_ylc[c], in_=ylcb[:]), inc=c_st, k=16)
                pr.wait("vector", c_st)

            pr.wait("gpsimd", c_dve)
            pr.run("gpsimd", lambda e: e.dma_start(out=o_scor[:], in_=hst[:]), inc=c_st, k=16)
            pr.wait("gpsimd", c_st)
            pr.wait("sync", c_ld)

        pr.replay()
    return nc


def stage_A(inp):
    x = np.asarray(inp["L0_attn_hidden_states"], np.float32).reshape(S, HD)
    xT = np.ascontiguousarray(x.T)
    w_in = np.asarray(inp["L0_attn_W_in"], np.float32)
    w_pad = np.zeros((HD, 131 * 128), np.float32)
    w_pad[:, :PROJ_DIM] = w_in
    win_b = _blocked(w_pad)
    kern_b = _blocked(np.asarray(inp["L0_ffn_kernel"], np.float32))
    bias_fm = np.ascontiguousarray(np.asarray(inp["L0_ffn_bias"], np.float32).reshape(32, 128).T)
    cwr = np.ascontiguousarray(
        np.asarray(inp["L0_attn_conv_w"], np.float32).reshape(66, 128, 4).transpose(1, 0, 2).reshape(128, 264))
    dtb = np.asarray(inp["L0_attn_dt_bias"], np.float32).reshape(64, 1)
    alog = np.asarray(inp["L0_attn_A_log"], np.float32).reshape(64, 1)
    drow = np.asarray(inp["L0_attn_D_param"], np.float32).reshape(1, 64)
    la = np.ascontiguousarray(np.asarray(inp["L0_ffn_lora_A"], np.float32).reshape(512, 4096).reshape(4, 128, 4096))
    lbt = np.ascontiguousarray(
        np.asarray(inp["L0_ffn_lora_B"], np.float32).transpose(0, 2, 1).reshape(512, 4096).reshape(4, 128, 4096))
    lat = np.ascontiguousarray(
        np.asarray(inp["L0_ffn_lora_A_shared"], np.float32).T.reshape(32, 128, 32)
        .transpose(1, 0, 2).reshape(128, 1024))
    lbst = np.ascontiguousarray(np.asarray(inp["L0_ffn_lora_B_shared"], np.float32).T)
    ident = np.eye(128).astype(BF)
    identf = np.eye(128).astype(np.float32)
    mpat = np.where(np.arange(128)[None, :] >= np.arange(128)[:, None], 0.0, NEGBIG).astype(np.float32)
    ones = np.ones((128, 128), np.float32)
    gate = np.asarray(inp["L0_ffn_gate"], np.float32)

    shared = dict(i_win=win_b, i_kern=kern_b, i_bias=bias_fm, i_cw=cwr, i_dtb=dtb,
                  i_alog=alog, i_drow=drow, i_la=la, i_lbt=lbt, i_lat=lat, i_lbst=lbst,
                  i_ident=ident, i_identf=identf, i_mp=mpat, i_ones=ones,
                  i_onescol=np.ones((128, 1), np.float32),
                  i_misc=np.array([[EPS, 1.0, 0.0, 0.0]], np.float32))
    per_core = []
    for c in range(NCORES):
        t0 = c * T
        xt = np.zeros((HD, 128 + T), np.float32)
        lo = max(t0 - 128, 0)
        xt[:, 128 - (t0 - lo):] = xT[:, lo:t0 + T]
        ge = np.zeros((512, 32), np.float32)
        for nl in range(16):
            for r in range(32):
                ge[nl * 32 + r, r] = gate[c, nl]
        per_core.append(dict(i_xt=np.ascontiguousarray(xt),
                             i_ge=np.ascontiguousarray(
                                 ge.reshape(4, 128, 32).transpose(1, 0, 2).reshape(128, 128))))
    return shared, per_core


def run_A(inp):
    nc = build_A()
    shared, per_core = stage_A(inp)
    in_maps = [dict(shared, **pc) for pc in per_core]
    res = run_bass_kernel_spmd(nc, in_maps, core_ids=list(range(NCORES)))
    return res.results




def build_B():
    nc = bass.Bass("TRN2", target_bir_lowering=False, debug=False, detect_race_conditions=False)
    i_xtb = nc.dram_tensor("i_xtb", [32, 128, T], F32, kind="ExternalInput")
    i_ylc = nc.dram_tensor("i_ylc", [NCH, 128, 8192], BF16, kind="ExternalInput")
    i_siluz = nc.dram_tensor("i_siluz", [64, 128, T], BF16, kind="ExternalInput")
    i_ct = nc.dram_tensor("i_ct", [128, T], BF16, kind="ExternalInput")
    i_lamg = nc.dram_tensor("i_lamg", [64, T], F32, kind="ExternalInput")
    i_sall = nc.dram_tensor("i_sall", [7, 128, 8192], BF16, kind="ExternalInput")
    i_ltv = nc.dram_tensor("i_ltv", [64, 7], F32, kind="ExternalInput")
    i_g = nc.dram_tensor("i_g", [32, 128, T], F32, kind="ExternalInput")
    i_wout = nc.dram_tensor("i_wout", [32, 128, 8192], F32, kind="ExternalInput")
    i_nw = nc.dram_tensor("i_nw", [128, 64], F32, kind="ExternalInput")
    i_qkb = nc.dram_tensor("i_qkb", [128, 64], F32, kind="ExternalInput")
    i_qvb = nc.dram_tensor("i_qvb", [1, 4096], F32, kind="ExternalInput")
    i_wqk = nc.dram_tensor("i_wqk", [64, 128, 4096], F32, kind="ExternalInput")
    i_wv = nc.dram_tensor("i_wv", [8, 128, 32 * 512], F32, kind="ExternalInput")
    i_ones = nc.dram_tensor("i_ones", [128, 128], F32, kind="ExternalInput")
    i_onescol = nc.dram_tensor("i_onescol", [128, 1], F32, kind="ExternalInput")
    i_oc_bf = nc.dram_tensor("i_oc_bf", [128, 1], BF16, kind="ExternalInput")
    i_o1r_bf = nc.dram_tensor("i_o1r_bf", [1, 128], BF16, kind="ExternalInput")
    i_misc = nc.dram_tensor("i_misc", [1, 4], F32, kind="ExternalInput")

    o_x1 = nc.dram_tensor("o_x1", [32, 128, T], F32, kind="ExternalOutput")
    o_xn2 = nc.dram_tensor("o_xn2", [32, 128, T], BF16, kind="ExternalOutput")
    o_q = nc.dram_tensor("o_q", [32, 128, T], BF16, kind="ExternalOutput")
    o_k = nc.dram_tensor("o_k", [32, 128, T], BF16, kind="ExternalOutput")
    o_v = nc.dram_tensor("o_v", [8, 128, 4096], BF16, kind="ExternalOutput")

    d_row3 = nc.dram_tensor("d_row3", [448], F32, kind="Internal")
    d_row1 = nc.dram_tensor("d_row1", [8192], F32, kind="Internal")

    ctx = ExitStack()
    with ctx:
        pr = Prog(nc, ctx)
        sbt = lambda nm, sh, dt: ctx.enter_context(nc.sbuf_tensor(nm, sh, dt))
        pst = lambda nm, sh, dt=F32: ctx.enter_context(nc.psum_tensor(nm, sh, dt))

        bkA = pst("bkA", [128, 512])
        bkB = pst("bkB", [128, 512])
        bkC = pst("bkC", [128, 512])
        bkD = pst("bkD", [128, 512])
        bkE = pst("bkE", [128, 512])
        bkS1 = pst("bkS1", [1, 512])
        bkS2 = pst("bkS2", [1, 512])

        ones1r = sbt("ones1r", [1, 128], F32)
        onescol = sbt("onescol", [128, 1], F32)
        oc_bf = sbt("oc_bf", [128, 1], BF16)
        o1r_bf = sbt("o1r_bf", [1, 128], BF16)
        epsc = sbt("epsc", [1, 1], F32)
        nwc = sbt("nwc", [128, 64], F32)
        qkbc = sbt("qkbc", [128, 64], F32)
        Ct = sbt("Ct", [128, T], BF16)
        lamg = sbt("lamg", [64, T], F32)
        hinb = sbt("hinb", [128, 8192], BF16)
        m2row = sbt("m2row", [1, T], F32)
        r2row = sbt("r2row", [1, T], F32)
        trow = sbt("trow", [1, 512], F32)
        urow = sbt("urow", [1, 512], F32)

        c_ld = pr.ctr("c_ld")
        c_gld = pr.ctr("c_gld")
        c_st = pr.ctr("c_st")
        c_pe = pr.ctr("c_pe")
        c_dve = pr.ctr("c_dve")
        c_act = pr.ctr("c_act")
        QMAP = {"sync": [c_ld], "gpsimd": [c_gld, c_st]}

        for o, i in ((ones1r[:], i_ones[0:1, :]), (onescol[:], i_onescol[:]),
                     (oc_bf[:], i_oc_bf[:]), (o1r_bf[:], i_o1r_bf[:]),
                     (epsc[:], i_misc[0:1, 0:1]), (nwc[:], i_nw[:]),
                     (qkbc[:], i_qkb[:]), (Ct[:], i_ct[:]), (lamg[:], i_lamg[:])):
            pr.run("sync", lambda e, oo=o, ii=i: e.dma_start(out=oo, in_=ii), inc=c_ld, k=16)

        with ExitStack() as b0:
            acc = b0.enter_context(nc.sbuf_tensor("acc", [128, 8192], F32))
            sjb = b0.enter_context(nc.sbuf_tensor("sjb", [128, 8192], BF16))
            ltv = b0.enter_context(nc.sbuf_tensor("ltv", [64, 7], F32))
            rr = b0.enter_context(nc.sbuf_tensor("rr", [64, 7], F32))
            coefr = b0.enter_context(nc.sbuf_tensor("coefr", [1, 448], F32))
            coefbc = b0.enter_context(nc.sbuf_tensor("coefbc", [128, 448], F32))
            pr.run("sync", lambda e: e.dma_start(out=ltv[:], in_=i_ltv[:]), inc=c_ld, k=16)
            pr.wait("vector", c_ld)
            pr.run("vector", lambda e: e.memset(rr[:, 6:7], 0.0), inc=c_dve)
            for j in range(5, -1, -1):
                pr.wait("vector", c_dve)
                pr.run("vector", lambda e, j=j: e.tensor_add(rr[:, j:j + 1], rr[:, j + 1:j + 2], ltv[:, j + 1:j + 2]), inc=c_dve)
            pr.wait("scalar", c_dve)
            pr.run("scalar", lambda e: e.activation(rr[:], rr[:], AF.Exp), inc=c_act)
            pr.wait("gpsimd", c_act)
            pr.run("gpsimd", lambda e: e.dma_start(out=d_row3[:].rearrange("(h j) -> h j", h=64), in_=rr[:]), inc=c_st, k=16)
            pr.wait("sync", c_st)
            pr.run("sync", lambda e: e.dma_start(out=coefr[:], in_=d_row3[:].unsqueeze(0)), inc=c_ld, k=16)
            pr.wait("tensor", c_ld)
            pr.run("tensor", lambda e: e.matmul(bkE[:, 0:448], ones1r[:], coefr[:], start=True, stop=True), inc=c_pe)
            pr.wait("vector", c_pe)
            pr.run("vector", lambda e: e.tensor_copy(coefbc[:], bkE[:, 0:448]), inc=c_dve)
            pr.run("vector", lambda e: e.memset(acc[:], 0.0), inc=c_dve)
            pr.wait("vector", c_dve)
            for j in range(7):
                pr.wait("sync", c_dve, pr.n(c_dve) if j else 0)
                pr.run("sync", lambda e, j=j: e.dma_start(out=sjb[:], in_=i_sall[j]), inc=c_ld, k=16)
                pr.wait("vector", c_ld)
                for h in range(64):
                    pr.run("vector", lambda e, j=j, h=h: e.scalar_tensor_tensor(
                        acc[:, h * 128:(h + 1) * 128], sjb[:, h * 128:(h + 1) * 128],
                        coefbc[:, h * 7 + j:h * 7 + j + 1], acc[:, h * 128:(h + 1) * 128],
                        ALU.mult, ALU.add), inc=(c_dve if h == 63 else None))
                pr.wait("vector", c_dve)
            pr.run("vector", lambda e: e.tensor_copy(hinb[:], acc[:]), inc=c_dve)

        pr.barrier(QMAP)

        with ExitStack() as b1:
            ylc = b1.enter_context(nc.sbuf_tensor("ylc", [128, 8192], BF16))
            szc = b1.enter_context(nc.sbuf_tensor("szc", [128, 8192], BF16))
            cwg = b1.enter_context(nc.sbuf_tensor("cwg", [128, 8192], BF16))
            ybuf = b1.enter_context(nc.sbuf_tensor("ybuf", [128, 8192], BF16))
            ynspan = b1.enter_context(nc.sbuf_tensor("ynspan", [128, 64 * 512], BF16))
            elgfm = b1.enter_context(nc.sbuf_tensor("elgfm", [64, 128], F32))
            elrp = [b1.enter_context(nc.sbuf_tensor(f"elrp{i}", [1, 512], F32)) for i in range(2)]
            tmp32 = b1.enter_context(nc.sbuf_tensor("tmp32", [128, 128], F32))
            ysq = [b1.enter_context(nc.sbuf_tensor(f"ysq{i}", [128, 128], BF16)) for i in range(2)]
            vrowp = b1.enter_context(nc.sbuf_tensor("vrowp", [1, 128], F32))
            rstd = b1.enter_context(nc.sbuf_tensor("rstd", [1, 128], F32))
            rstdbc = b1.enter_context(nc.sbuf_tensor("rstdbc", [128, 128], F32))
            wob = b1.enter_context(nc.sbuf_tensor("wob", [128, 8192], BF16))
            gsl = [b1.enter_context(nc.sbuf_tensor(f"gsl{i}", [128, 512], F32)) for i in range(2)]
            xsl = [b1.enter_context(nc.sbuf_tensor(f"xsl{i}", [128, 512], F32)) for i in range(2)]
            x1b = [b1.enter_context(nc.sbuf_tensor(f"x1b{i}", [128, 512], F32)) for i in range(2)]
            t1b = b1.enter_context(nc.sbuf_tensor("t1b", [128, 512], F32))
            x1sq = [b1.enter_context(nc.sbuf_tensor(f"x1sq{i}", [128, 512], BF16)) for i in range(2)]

            for c in range(NCH):
                pr.wait("sync", c_dve)
                pr.run("sync", lambda e, c=c: e.dma_start(out=ylc[:], in_=i_ylc[c]), inc=c_ld, k=16)
                for m in range(64):
                    eng, cc = ("sync", c_ld) if m % 2 == 0 else ("gpsimd", c_gld)
                    pr.wait(eng, c_dve)
                    pr.run(eng, lambda e, m=m, c=c: e.dma_start(
                        out=szc[:, m * 128:(m + 1) * 128], in_=i_siluz[m, :, c * 128:(c + 1) * 128]), inc=cc, k=16)
                ld_mark = (pr.n(c_ld), pr.n(c_gld))
                pr.wait("scalar", c_dve)
                pr.run("scalar", lambda e, c=c: e.activation(elgfm[:], lamg[:, c * 128:(c + 1) * 128], AF.Exp), inc=c_act)
                pr.wait("gpsimd", c_act)
                pr.run("gpsimd", lambda e: e.dma_start(out=d_row1[:].rearrange("(h t) -> h t", h=64), in_=elgfm[:]), inc=c_st, k=16)
                for p in range(16):
                    lrp = elrp[p % 2]
                    pr.wait("sync", c_pe)
                    pr.wait("sync", c_st)
                    pr.run("sync", lambda e, p=p, lrp=lrp: e.dma_start(
                        out=lrp[:], in_=d_row1[p * 512:(p + 1) * 512].unsqueeze(0)), inc=c_ld, k=16)
                    pb = bkE
                    pr.wait("tensor", c_ld)
                    pr.wait("tensor", c_dve)
                    pr.run("tensor", lambda e, lrp=lrp, pb=pb: e.matmul(
                        pb[:], ones1r[:], lrp[:], start=True, stop=True), inc=c_pe)
                    pr.wait("vector", c_pe)
                    pr.run("vector", lambda e, p=p, c=c, pb=pb: e.tensor_tensor(
                        cwg[:, p * 512:(p + 1) * 512].rearrange("q (a b) -> q a b", a=4),
                        pb[:].rearrange("q (a b) -> q a b", a=4),
                        Ct[:, c * 128:(c + 1) * 128].unsqueeze(1).broadcast_to([128, 4, 128]),
                        ALU.mult), inc=c_dve)
                pr.wait("tensor", c_ld, ld_mark[0])
                pr.wait("tensor", c_gld, ld_mark[1])
                pr.wait("tensor", c_dve)
                pe_h, dve_h, act_h = {}, {}, {}
                for h in range(64):
                    pyb = (bkA, bkB)[h % 2]
                    pr.run("tensor", lambda e, h=h, pyb=pyb: e.matmul(
                        pyb[:, 0:128], hinb[:, h * 128:(h + 1) * 128], cwg[:, h * 128:(h + 1) * 128],
                        start=True, stop=True), inc=c_pe)
                    pe_h[h] = pr.n(c_pe)
                    pr.wait("vector", c_pe, pe_h[h])
                    pr.run("vector", lambda e, h=h, pyb=pyb: e.tensor_add(
                        tmp32[:], ylc[:, h * 128:(h + 1) * 128], pyb[:, 0:128]), inc=c_dve)
                    pr.wait("vector", c_dve)
                    pr.run("vector", lambda e, h=h: e.tensor_mul(
                        ybuf[:, h * 128:(h + 1) * 128], tmp32[:], szc[:, h * 128:(h + 1) * 128]), inc=c_dve)
                    dve_h[h] = pr.n(c_dve)
                    pr.wait("scalar", c_dve, dve_h[h])
                    pr.wait("scalar", c_pe, act_h.get(h - 2, 0))
                    pr.run("scalar", lambda e, h=h: e.activation(
                        ysq[h % 2][:], ybuf[:, h * 128:(h + 1) * 128], AF.Square), inc=c_act)
                    pr.wait("tensor", c_act)
                    vv = pr.run("tensor", lambda e, h=h: e.matmul(
                        bkS1[0:1, 0:128], oc_bf[:], ysq[h % 2][:],
                        start=(h == 0), stop=(h == 63)), inc=c_pe)
                    act_h[h] = vv
                    if h >= 1:
                        pr.wait("tensor", c_dve, dve_h[h - 1])
                pr.wait("scalar", c_pe)
                pr.run("scalar", lambda e: e.activation(vrowp[:], bkS1[0:1, 0:128], AF.Ln, bias=epsc[:], scale=1.0 / 8192.0), inc=c_act)
                pr.wait("vector", c_act)
                pr.run("vector", lambda e: e.tensor_scalar_mul(vrowp[:], vrowp[:], -0.5), inc=c_dve)
                pr.wait("scalar", c_dve)
                pr.run("scalar", lambda e: e.activation(rstd[:], vrowp[:], AF.Exp), inc=c_act)
                pr.wait("tensor", c_act)
                pr.run("tensor", lambda e: e.matmul(bkE[:, 0:128], ones1r[:], rstd[:], start=True, stop=True), inc=c_pe)
                pr.wait("vector", c_pe)
                pr.run("vector", lambda e: e.tensor_copy(rstdbc[:], bkE[:, 0:128]), inc=c_dve)
                pr.wait("vector", c_dve)
                for h in range(64):
                    pr.run("vector", lambda e, h=h, c=c: e.scalar_tensor_tensor(
                        ynspan[:, h * 512 + (c % 4) * 128:h * 512 + (c % 4) * 128 + 128],
                        ybuf[:, h * 128:(h + 1) * 128], nwc[:, h:h + 1], rstdbc[:],
                        ALU.mult, ALU.mult), inc=(c_dve if h == 63 else None))

                if c % 4 == 3:
                    s = c // 4
                    for Mc in range(32):
                        pr.wait("gpsimd", c_pe)
                        pr.run("gpsimd", lambda e, Mc=Mc: e.dma_start(out=wob[:], in_=i_wout[Mc]), inc=c_gld, k=16)
                        pr.wait("sync", c_dve)
                        pr.run("sync", lambda e, Mc=Mc, s=s: e.dma_start(
                            out=gsl[Mc % 2][:], in_=i_g[Mc, :, s * 512:(s + 1) * 512]), inc=c_ld, k=16)
                        pr.run("sync", lambda e, Mc=Mc, s=s: e.dma_start(
                            out=xsl[Mc % 2][:], in_=i_xtb[Mc, :, s * 512:(s + 1) * 512]), inc=c_ld, k=16)
                        pw = (bkC, bkD)[Mc % 2]
                        pr.wait("tensor", c_gld)
                        pr.wait("tensor", c_dve)
                        pr.wait("tensor", c_act)
                        for kc in range(64):
                            pr.run("tensor", lambda e, kc=kc, pw=pw: e.matmul(
                                pw[:], wob[:, kc * 128:(kc + 1) * 128], ynspan[:, kc * 512:(kc + 1) * 512],
                                start=(kc == 0), stop=(kc == 63)), inc=(None if kc < 63 else c_pe))
                        pev = pr.n(c_pe)
                        pr.wait("vector", c_pe, pev)
                        pr.wait("vector", c_ld)
                        pr.run("vector", lambda e, Mc=Mc, pw=pw: e.tensor_mul(t1b[:], pw[:], gsl[Mc % 2][:]), inc=c_dve)
                        pr.wait("vector", c_dve)
                        pr.run("vector", lambda e, Mc=Mc: e.tensor_add(x1b[Mc % 2][:], t1b[:], xsl[Mc % 2][:]), inc=c_dve)
                        x1v = pr.n(c_dve)
                        pr.wait("gpsimd", c_dve, x1v)
                        pr.run("gpsimd", lambda e, Mc=Mc, s=s: e.dma_start(
                            out=o_x1[Mc, :, s * 512:(s + 1) * 512], in_=x1b[Mc % 2][:]), inc=c_st, k=16)
                        pr.wait("scalar", c_dve, x1v)
                        pr.wait("scalar", c_pe)
                        pr.run("scalar", lambda e, Mc=Mc: e.activation(x1sq[Mc % 2][:], x1b[Mc % 2][:], AF.Square), inc=c_act)
                        pr.wait("tensor", c_dve, x1v)
                        pr.run("tensor", lambda e, Mc=Mc: e.matmul(
                            bkS1[0:1, 0:512], onescol[:], x1b[Mc % 2][:],
                            start=(Mc == 0), stop=(Mc == 31)), inc=None)
                        pr.wait("tensor", c_act)
                        pr.run("tensor", lambda e, Mc=Mc: e.matmul(
                            bkS2[0:1, 0:512], oc_bf[:], x1sq[Mc % 2][:],
                            start=(Mc == 0), stop=(Mc == 31)), inc=(None if Mc < 31 else c_pe))
                    pr.wait("vector", c_pe)
                    pr.run("vector", lambda e, s=s: e.tensor_scalar_mul(m2row[0:1, s * 512:(s + 1) * 512], bkS1[0:1, 0:512], 1.0 / HD), inc=c_dve)
                    pr.run("vector", lambda e: e.tensor_scalar_mul(trow[:], bkS2[0:1, 0:512], 1.0 / HD), inc=c_dve)
                    pr.wait("vector", c_dve)
                    pr.run("vector", lambda e, s=s: e.tensor_mul(urow[:], m2row[0:1, s * 512:(s + 1) * 512], m2row[0:1, s * 512:(s + 1) * 512]), inc=c_dve)
                    pr.wait("vector", c_dve)
                    pr.run("vector", lambda e: e.tensor_sub(trow[:], trow[:], urow[:]), inc=c_dve)
                    pr.wait("scalar", c_dve)
                    pr.run("scalar", lambda e: e.activation(trow[:], trow[:], AF.Sqrt, bias=epsc[:]), inc=c_act)
                    pr.wait("vector", c_act)
                    pr.run("vector", lambda e, s=s: e.reciprocal(r2row[0:1, s * 512:(s + 1) * 512], trow[:]), inc=c_dve)

        pr.barrier(QMAP)

        with ExitStack() as b2:
            xn2sb = b2.enter_context(nc.sbuf_tensor("xn2sb", [128, 32 * T], BF16))
            m2bc = b2.enter_context(nc.sbuf_tensor("m2bc", [128, T], F32))
            r2bc = b2.enter_context(nc.sbuf_tensor("r2bc", [128, T], F32))
            x1t = [b2.enter_context(nc.sbuf_tensor(f"x1t{i}", [128, T], F32)) for i in range(2)]
            t2b = b2.enter_context(nc.sbuf_tensor("t2b", [128, T], F32))
            qkblk = [b2.enter_context(nc.sbuf_tensor(f"qkblk{i}", [128, 4096], BF16)) for i in range(2)]
            qbuf = [b2.enter_context(nc.sbuf_tensor(f"qbuf{i}", [128, 512], BF16)) for i in range(2)]
            vwcol = b2.enter_context(nc.sbuf_tensor("vwcol", [128, 32 * 512], BF16))
            vbr = b2.enter_context(nc.sbuf_tensor("vbr", [1, 4096], BF16))
            vbuf = [b2.enter_context(nc.sbuf_tensor(f"vbuf{i}", [128, 512], BF16)) for i in range(2)]

            for half in range(2):
                pr.wait("tensor", c_dve)
                pr.run("tensor", lambda e, half=half: e.matmul(
                    bkE[:], ones1r[:], m2row[0:1, half * 512:(half + 1) * 512], start=True, stop=True), inc=c_pe)
                pr.wait("vector", c_pe)
                pr.run("vector", lambda e, half=half: e.tensor_copy(m2bc[:, half * 512:(half + 1) * 512], bkE[:]), inc=c_dve)
                pr.wait("tensor", c_dve)
                pr.run("tensor", lambda e, half=half: e.matmul(
                    bkE[:], ones1r[:], r2row[0:1, half * 512:(half + 1) * 512], start=True, stop=True), inc=c_pe)
                pr.wait("vector", c_pe)
                pr.run("vector", lambda e, half=half: e.tensor_copy(r2bc[:, half * 512:(half + 1) * 512], bkE[:]), inc=c_dve)
            pr.wait("vector", c_dve)
            for kc in range(32):
                pr.wait("sync", c_dve)
                pr.run("sync", lambda e, kc=kc: e.dma_start(out=x1t[kc % 2][:], in_=o_x1[kc]), inc=c_ld, k=16)
                pr.wait("vector", c_ld)
                pr.run("vector", lambda e, kc=kc: e.tensor_sub(t2b[:], x1t[kc % 2][:], m2bc[:]), inc=c_dve)
                pr.wait("vector", c_dve)
                pr.run("vector", lambda e, kc=kc: e.tensor_mul(xn2sb[:, kc * T:(kc + 1) * T], t2b[:], r2bc[:]), inc=c_dve)
                pr.wait("gpsimd", c_dve)
                pr.run("gpsimd", lambda e, kc=kc: e.dma_start(out=o_xn2[kc], in_=xn2sb[:, kc * T:(kc + 1) * T]), inc=c_st, k=16)
            for Mc in range(64):
                qk = qkblk[Mc % 2]
                pr.wait("gpsimd", c_pe)
                pr.run("gpsimd", lambda e, Mc=Mc, qk=qk: e.dma_start(out=qk[:], in_=i_wqk[Mc]), inc=c_gld, k=16)
                pr.wait("tensor", c_gld)
                for half in range(2):
                    pq = (bkC, bkD)[half]
                    pr.wait("tensor", c_act)
                    for kc in range(32):
                        pr.run("tensor", lambda e, kc=kc, qk=qk, pq=pq, half=half: e.matmul(
                            pq[:], qk[:, kc * 128:(kc + 1) * 128],
                            xn2sb[:, kc * T + half * 512:kc * T + half * 512 + 512],
                            start=(kc == 0), stop=(kc == 31)), inc=(None if kc < 31 else c_pe))
                    pev = pr.n(c_pe)
                    qb = qbuf[half]
                    pr.wait("vector", c_pe, pev)
                    pr.wait("vector", c_st)
                    pr.run("vector", lambda e, Mc=Mc, qb=qb, pq=pq: e.tensor_scalar_add(
                        qb[:], pq[:], qkbc[:, Mc:Mc + 1]), inc=c_act)
                    pr.wait("gpsimd", c_act)
                    if Mc < 32:
                        pr.run("gpsimd", lambda e, Mc=Mc, half=half, qb=qb: e.dma_start(
                            out=o_q[Mc, :, half * 512:(half + 1) * 512], in_=qb[:]), inc=c_st, k=16)
                    else:
                        pr.run("gpsimd", lambda e, Mc=Mc, half=half, qb=qb: e.dma_start(
                            out=o_k[Mc - 32, :, half * 512:(half + 1) * 512], in_=qb[:]), inc=c_st, k=16)
            pr.run("gpsimd", lambda e: e.dma_start(out=vbr[:], in_=i_qvb[:]), inc=c_gld, k=16)
            for vc in range(8):
                pr.wait("gpsimd", c_pe)
                pr.run("gpsimd", lambda e, vc=vc: e.dma_start(out=vwcol[:], in_=i_wv[vc]), inc=c_gld, k=16)
                pr.wait("tensor", c_gld)
                for tc in range(8):
                    pv = (bkC, bkD)[tc % 2]
                    pr.wait("tensor", c_act)
                    for kc in range(32):
                        pr.run("tensor", lambda e, kc=kc, tc=tc, pv=pv: e.matmul(
                            pv[:], xn2sb[:, kc * T + tc * 128:kc * T + tc * 128 + 128],
                            vwcol[:, kc * 512:(kc + 1) * 512],
                            start=(kc == 0), stop=False), inc=None)
                    pr.run("tensor", lambda e, vc=vc, pv=pv: e.matmul(
                        pv[:], o1r_bf[:], vbr[0:1, vc * 512:(vc + 1) * 512],
                        start=False, stop=True), inc=c_pe)
                    pev = pr.n(c_pe)
                    vb = vbuf[tc % 2]
                    pr.wait("scalar", c_pe, pev)
                    pr.wait("scalar", c_st)
                    pr.run("scalar", lambda e, vb=vb, pv=pv: e.activation(vb[:], pv[:], AF.Copy), inc=c_act)
                    pr.wait("gpsimd", c_act)
                    pr.run("gpsimd", lambda e, tc=tc, vc=vc, vb=vb: e.dma_start(
                        out=o_v[tc, :, vc * 512:(vc + 1) * 512], in_=vb[:]), inc=c_st, k=16)
            pr.wait("gpsimd", c_st)
            pr.wait("sync", c_ld)
        pr.replay()
    return nc


def stage_B(inp, a_outs):
    xT = np.ascontiguousarray(np.asarray(inp["L0_attn_hidden_states"], np.float32).reshape(S, HD).T)
    wout_b = _blocked(np.asarray(inp["L0_attn_W_out"], np.float32))
    qkvw = np.asarray(inp["L1_attn_attn_qkvw"], np.float32)
    wqk_b = _blocked(qkvw[:, 0:8192])
    wv = np.ascontiguousarray(
        qkvw[:, 8192:12288].reshape(32, 128, 8, 512).transpose(2, 1, 0, 3).reshape(8, 128, 32 * 512))
    qkvb = np.asarray(inp["L1_attn_attn_qkvb"], np.float32)
    qkb = np.ascontiguousarray(qkvb[0:8192].reshape(64, 128).T)
    qvb = np.ascontiguousarray(qkvb[8192:12288].reshape(1, 4096))
    nw = np.ascontiguousarray(np.asarray(inp["L0_attn_norm_weight"], np.float32).reshape(64, 128).T)
    ones = np.ones((128, 128), np.float32)
    shared = dict(i_wout=wout_b, i_nw=nw, i_qkb=qkb, i_qvb=qvb, i_wqk=wqk_b, i_wv=wv,
                  i_ones=ones, i_onescol=np.ones((128, 1), np.float32),
                  i_oc_bf=np.ones((128, 1), BF), i_o1r_bf=np.ones((1, 128), BF),
                  i_misc=np.array([[EPS, 1.0, 0.0, 0.0]], np.float32))
    per_core = []
    ltots = [np.asarray(a_outs[c]["o_lamg"], np.float32)[:, T - 1] for c in range(NCORES)]
    for c in range(NCORES):
        sall = np.zeros((7, 128, 8192), BF)
        ltv = np.zeros((64, 7), np.float32)
        for j in range(min(c, 7)):
            sall[j] = np.asarray(a_outs[j]["o_scor"])
        for m in range(1, 7):
            if m <= c - 1:
                ltv[:, m] = ltots[m]
        per_core.append(dict(
            i_xtb=np.ascontiguousarray(xT[:, c * T:(c + 1) * T].reshape(32, 128, T)),
            i_ylc=np.asarray(a_outs[c]["o_ylc"]),
            i_siluz=np.asarray(a_outs[c]["o_siluz"]),
            i_ct=np.asarray(a_outs[c]["o_xbcc"])[65],
            i_lamg=np.asarray(a_outs[c]["o_lamg"], np.float32),
            i_sall=sall, i_ltv=ltv,
            i_g=np.asarray(a_outs[c]["o_g"], np.float32)))
    return shared, per_core


def build_C():
    nc = bass.Bass("TRN2", target_bir_lowering=False, debug=False, detect_race_conditions=False)
    i_q = nc.dram_tensor("i_q", [32, 128, T], BF16, kind="ExternalInput")
    i_kf = nc.dram_tensor("i_kf", [32, 128, 8192], BF16, kind="ExternalInput")
    i_vf = nc.dram_tensor("i_vf", [32, 128, 8192], BF16, kind="ExternalInput")
    i_x1 = nc.dram_tensor("i_x1", [32, 128, T], F32, kind="ExternalInput")
    i_xn2 = nc.dram_tensor("i_xn2", [32, 128, T], BF16, kind="ExternalInput")
    i_wo = nc.dram_tensor("i_wo", [32, 128, 4096], F32, kind="ExternalInput")
    i_w1t = nc.dram_tensor("i_w1t", [32, 128, 4096], F32, kind="ExternalInput")
    i_ob = nc.dram_tensor("i_ob", [128, 32], F32, kind="ExternalInput")
    i_wsc = nc.dram_tensor("i_wsc", [128, 32], F32, kind="ExternalInput")
    i_wbc = nc.dram_tensor("i_wbc", [128, 32], F32, kind="ExternalInput")
    i_ones = nc.dram_tensor("i_ones", [128, 128], F32, kind="ExternalInput")
    i_oc_bf = nc.dram_tensor("i_oc_bf", [128, 1], BF16, kind="ExternalInput")
    i_smsh = nc.dram_tensor("i_smsh", [128, 1], F32, kind="ExternalInput")

    o_out = nc.dram_tensor("o_out", [32, 128, T], F32, kind="ExternalOutput")

    ctx = ExitStack()
    with ctx:
        pr = Prog(nc, ctx)
        c_ld = pr.ctr("c_ld")
        c_gld = pr.ctr("c_gld")
        c_st = pr.ctr("c_st")
        c_pe = pr.ctr("c_pe")
        c_dve = pr.ctr("c_dve")
        c_act = pr.ctr("c_act")
        QMAP = {"sync": [c_ld], "gpsimd": [c_gld, c_st]}
        sbt = lambda nm, sh, dt: ctx.enter_context(nc.sbuf_tensor(nm, sh, dt))

        ones1r = sbt("ones1r", [1, 128], F32)
        oc_bf = sbt("oc_bf", [128, 1], BF16)
        smsh = sbt("smsh", [128, 1], F32)
        obc = sbt("obc", [128, 32], F32)
        wscc = sbt("wscc", [128, 32], F32)
        wbcc = sbt("wbcc", [128, 32], F32)
        ctxb = sbt("ctxb", [128, 32 * T], BF16)

        for o, i in ((ones1r[:], i_ones[0:1, :]), (oc_bf[:], i_oc_bf[:]), (smsh[:], i_smsh[:]),
                     (obc[:], i_ob[:]), (wscc[:], i_wsc[:]), (wbcc[:], i_wbc[:])):
            pr.run("sync", lambda e, oo=o, ii=i: e.dma_start(out=oo, in_=ii), inc=c_ld, k=16)

        with ExitStack() as ca:
            pSc = [ca.enter_context(nc.psum_tensor(f"pSc{i}", [128, 1024], F32)) for i in range(2)]
            pCx = ca.enter_context(nc.psum_tensor("pCx", [128, 1024], F32))
            pZ = ca.enter_context(nc.psum_tensor("pZ", [1, 1024], F32))
            khb = [ca.enter_context(nc.sbuf_tensor(f"khb{i}", [128, 8192], BF16)) for i in range(2)]
            vhb = [ca.enter_context(nc.sbuf_tensor(f"vhb{i}", [128, 8192], BF16)) for i in range(2)]
            qhb = [ca.enter_context(nc.sbuf_tensor(f"qhb{i}", [128, T], BF16)) for i in range(2)]
            probs = [ca.enter_context(nc.sbuf_tensor(f"probs{i}", [128, 1024], BF16)) for i in range(2)]
            zr = ca.enter_context(nc.sbuf_tensor("zr", [1, 1024], F32))
            zbc = ca.enter_context(nc.sbuf_tensor("zbc", [128, 1024], F32))

            def load_head(h):
                b = h % 2
                pr.wait("sync", c_pe)
                pr.wait("gpsimd", c_pe)
                pr.run("sync", lambda e, h=h, b=b: e.dma_start(out=khb[b][:], in_=i_kf[h]), inc=c_ld, k=16)
                pr.run("gpsimd", lambda e, h=h, b=b: e.dma_start(out=vhb[b][:], in_=i_vf[h]), inc=c_gld, k=16)
                pr.run("sync", lambda e, h=h, b=b: e.dma_start(out=qhb[b][:], in_=i_q[h]), inc=c_ld, k=16)

            load_head(0)
            for h in range(32):
                b = h % 2
                marks = (pr.n(c_ld), pr.n(c_gld))
                if h + 1 < 32:
                    load_head(h + 1)
                pr.wait("tensor", c_ld, marks[0])
                pr.wait("tensor", c_gld, marks[1])
                act_k, pe_k = {}, {}
                for kc in range(64):
                    sc = pSc[kc % 2]
                    pr.wait("tensor", c_act, act_k.get(kc - 2, 0))
                    for half in range(2):
                        pr.run("tensor", lambda e, kc=kc, b=b, sc=sc, half=half: e.matmul(
                            sc[:, half * 512:(half + 1) * 512],
                            khb[b][:, kc * 128:(kc + 1) * 128],
                            qhb[b][:, half * 512:(half + 1) * 512],
                            start=True, stop=True), inc=(None if half == 0 else c_pe))
                    scv = pr.n(c_pe)
                    pr.wait("scalar", c_pe, scv)
                    pr.wait("scalar", c_pe, pe_k.get(kc - 2, 0))
                    for half in range(2):
                        pr.run("scalar", lambda e, kc=kc, sc=sc, half=half: e.activation(
                            probs[kc % 2][:, half * 512:(half + 1) * 512],
                            sc[:, half * 512:(half + 1) * 512],
                            AF.Exp, bias=smsh[:], scale=SM_SCALE), inc=(None if half == 0 else c_act))
                    act_k[kc] = pr.n(c_act)
                    pr.wait("tensor", c_act, act_k[kc])
                    for half in range(2):
                        pr.run("tensor", lambda e, kc=kc, b=b, half=half: e.matmul(
                            pCx[:, half * 512:(half + 1) * 512],
                            vhb[b][:, kc * 128:(kc + 1) * 128],
                            probs[kc % 2][:, half * 512:(half + 1) * 512],
                            start=(kc == 0), stop=(kc == 63)), inc=None)
                        pr.run("tensor", lambda e, kc=kc, half=half: e.matmul(
                            pZ[0:1, half * 512:(half + 1) * 512], oc_bf[:],
                            probs[kc % 2][:, half * 512:(half + 1) * 512],
                            start=(kc == 0), stop=(kc == 63)), inc=(None if half == 0 else c_pe))
                    pe_k[kc] = pr.n(c_pe)
                pr.wait("vector", c_pe)
                pr.run("vector", lambda e: e.reciprocal(zr[:], pZ[0:1, :]), inc=c_dve)
                pr.wait("tensor", c_dve)
                for half in range(2):
                    pr.run("tensor", lambda e, half=half: e.matmul(
                        pSc[0][:, half * 512:(half + 1) * 512], ones1r[:],
                        zr[0:1, half * 512:(half + 1) * 512], start=True, stop=True),
                        inc=(None if half == 0 else c_pe))
                pr.wait("vector", c_pe)
                pr.run("vector", lambda e: e.tensor_copy(zbc[:], pSc[0][:]), inc=c_dve)
                pr.wait("vector", c_dve)
                pr.run("vector", lambda e, h=h: e.tensor_mul(
                    ctxb[:, h * T:(h + 1) * T], pCx[:], zbc[:]), inc=c_dve)
                pr.wait("tensor", c_dve)

        pr.barrier(QMAP)

        with ExitStack() as cb:
            pA = cb.enter_context(nc.psum_tensor("pA", [128, 512], F32))
            pB = cb.enter_context(nc.psum_tensor("pB", [128, 512], F32))
            pX = cb.enter_context(nc.psum_tensor("pX", [1, 1024], F32))
            pBc = cb.enter_context(nc.psum_tensor("pBc", [128, 1024], F32))
            xn2sb = cb.enter_context(nc.sbuf_tensor("xn2sb", [128, 32 * T], BF16))
            woblk = [cb.enter_context(nc.sbuf_tensor(f"woblk{i}", [128, 4096], BF16)) for i in range(2)]
            w1blk = [cb.enter_context(nc.sbuf_tensor(f"w1blk{i}", [128, 4096], BF16)) for i in range(2)]
            sgnb = cb.enter_context(nc.sbuf_tensor("sgnb", [128, 4096], BF16))
            xsumr = cb.enter_context(nc.sbuf_tensor("xsumr", [1, 1024], F32))
            xsbc = cb.enter_context(nc.sbuf_tensor("xsbc", [128, 1024], F32))
            x1sl = [cb.enter_context(nc.sbuf_tensor(f"x1sl{i}", [128, 512], F32)) for i in range(2)]
            g1b = cb.enter_context(nc.sbuf_tensor("g1b", [128, 512], F32))
            t1c = cb.enter_context(nc.sbuf_tensor("t1c", [128, 512], F32))
            t2c = cb.enter_context(nc.sbuf_tensor("t2c", [128, 512], F32))
            outb = [cb.enter_context(nc.sbuf_tensor(f"outb{i}", [128, 512], F32)) for i in range(2)]

            for kc in range(32):
                pr.run("sync", lambda e, kc=kc: e.dma_start(
                    out=xn2sb[:, kc * T:(kc + 1) * T], in_=i_xn2[kc]), inc=c_ld, k=16)
            pr.wait("tensor", c_ld)
            for kc in range(32):
                for half in range(2):
                    pr.run("tensor", lambda e, kc=kc, half=half: e.matmul(
                        pX[0:1, half * 512:(half + 1) * 512], oc_bf[:],
                        xn2sb[:, kc * T + half * 512:kc * T + half * 512 + 512],
                        start=(kc == 0), stop=(kc == 31)), inc=(None if (kc < 31 or half == 0) else c_pe))
            pr.wait("vector", c_pe)
            pr.run("vector", lambda e: e.tensor_copy(xsumr[:], pX[0:1, :]), inc=c_dve)
            pr.wait("tensor", c_dve)
            for half in range(2):
                pr.run("tensor", lambda e, half=half: e.matmul(
                    pBc[:, half * 512:(half + 1) * 512], ones1r[:],
                    xsumr[0:1, half * 512:(half + 1) * 512], start=True, stop=True),
                    inc=(None if half == 0 else c_pe))
            pr.wait("vector", c_pe)
            pr.run("vector", lambda e: e.tensor_copy(xsbc[:], pBc[:]), inc=c_dve)
            pr.wait("vector", c_dve)

            for th in range(2):
                for nc_ in range(32):
                    wo = woblk[nc_ % 2]
                    w1 = w1blk[nc_ % 2]
                    pr.wait("gpsimd", c_pe)
                    pr.run("gpsimd", lambda e, nc_=nc_, wo=wo: e.dma_start(out=wo[:], in_=i_wo[nc_]), inc=c_gld, k=16)
                    pr.run("gpsimd", lambda e, nc_=nc_, w1=w1: e.dma_start(out=w1[:], in_=i_w1t[nc_]), inc=c_gld, k=16)
                    pr.wait("sync", c_dve)
                    pr.run("sync", lambda e, nc_=nc_, th=th: e.dma_start(
                        out=x1sl[nc_ % 2][:], in_=i_x1[nc_, :, th * 512:(th + 1) * 512]), inc=c_ld, k=16)
                    pr.wait("scalar", c_gld)
                    pr.wait("scalar", c_pe)
                    pr.run("scalar", lambda e, w1=w1: e.activation(sgnb[:], w1[:], AF.Sign), inc=c_act)
                    pr.wait("tensor", c_gld)
                    pr.wait("tensor", c_dve)
                    for kc in range(32):
                        pr.run("tensor", lambda e, kc=kc, wo=wo, th=th: e.matmul(
                            pA[:], wo[:, kc * 128:(kc + 1) * 128],
                            ctxb[:, kc * T + th * 512:kc * T + th * 512 + 512],
                            start=(kc == 0), stop=(kc == 31)), inc=(None if kc < 31 else c_pe))
                    pr.wait("tensor", c_act)
                    for kc in range(32):
                        pr.run("tensor", lambda e, kc=kc, th=th: e.matmul(
                            pB[:], sgnb[:, kc * 128:(kc + 1) * 128],
                            xn2sb[:, kc * T + th * 512:kc * T + th * 512 + 512],
                            start=(kc == 0), stop=(kc == 31)), inc=(None if kc < 31 else c_pe))
                    pev = pr.n(c_pe)
                    pr.wait("vector", c_pe, pev)
                    pr.wait("vector", c_ld)
                    pr.run("vector", lambda e, nc_=nc_: e.tensor_scalar_mul(t1c[:], pB[:], wscc[:, nc_:nc_ + 1]), inc=c_dve)
                    pr.wait("vector", c_dve)
                    pr.run("vector", lambda e, nc_=nc_, th=th: e.scalar_tensor_tensor(
                        t2c[:], xsbc[:, th * 512:(th + 1) * 512], wbcc[:, nc_:nc_ + 1], t1c[:],
                        ALU.mult, ALU.add), inc=c_dve)
                    pr.wait("scalar", c_dve)
                    pr.run("scalar", lambda e: e.activation(g1b[:], t2c[:], AF.Silu), inc=c_act)
                    pr.wait("vector", c_act)
                    pr.run("vector", lambda e, nc_=nc_: e.tensor_scalar_add(t1c[:], pA[:], obc[:, nc_:nc_ + 1]), inc=c_dve)
                    pr.wait("vector", c_dve)
                    pr.run("vector", lambda e: e.tensor_mul(t2c[:], t1c[:], g1b[:]), inc=c_dve)
                    pr.wait("vector", c_dve)
                    pr.run("vector", lambda e, nc_=nc_: e.tensor_add(outb[nc_ % 2][:], t2c[:], x1sl[nc_ % 2][:]), inc=c_dve)
                    pr.wait("gpsimd", c_dve)
                    pr.run("gpsimd", lambda e, nc_=nc_, th=th: e.dma_start(
                        out=o_out[nc_, :, th * 512:(th + 1) * 512], in_=outb[nc_ % 2][:]), inc=c_st, k=16)
            pr.wait("gpsimd", c_st)
            pr.wait("sync", c_ld)
        pr.replay()
    return nc


def stage_C(inp, b_outs):
    wo_b = _blocked(np.asarray(inp["L1_attn_attn_ow"], np.float32))
    w1t = np.ascontiguousarray(np.asarray(inp["L1_ffn_weight"], np.float32).T)
    w1t_b = _blocked(w1t)
    ob = np.ascontiguousarray(np.asarray(inp["L1_attn_attn_ob"], np.float32).reshape(32, 128).T)
    wsc = np.ascontiguousarray(np.asarray(inp["L1_ffn_wscale"], np.float32).reshape(32, 128).T)
    wbc = np.ascontiguousarray(np.asarray(inp["L1_ffn_wbias"], np.float32).reshape(32, 128).T)
    kf = np.concatenate([np.asarray(b_outs[c]["o_k"]) for c in range(NCORES)], axis=2)
    vg = np.concatenate([np.asarray(b_outs[c]["o_v"]) for c in range(NCORES)], axis=0)
    vf = np.ascontiguousarray(vg.reshape(64, 128, 32, 128).transpose(2, 1, 0, 3).reshape(32, 128, 8192))
    shared = dict(i_kf=kf, i_vf=vf, i_wo=wo_b, i_w1t=w1t_b, i_ob=ob, i_wsc=wsc, i_wbc=wbc,
                  i_ones=np.ones((128, 128), np.float32),
                  i_oc_bf=np.ones((128, 1), BF),
                  i_smsh=np.full((128, 1), SM_SHIFT, np.float32))
    per_core = []
    for c in range(NCORES):
        per_core.append(dict(i_q=np.asarray(b_outs[c]["o_q"]),
                             i_x1=np.asarray(b_outs[c]["o_x1"], np.float32),
                             i_xn2=np.asarray(b_outs[c]["o_xn2"])))
    return shared, per_core


def run_B(inp, a_outs):
    nc = build_B()
    shared, per_core = stage_B(inp, a_outs)
    res = run_bass_kernel_spmd(nc, [dict(shared, **pc) for pc in per_core], core_ids=list(range(NCORES)))
    return res.results


def run_C(inp, b_outs):
    nc = build_C()
    shared, per_core = stage_C(inp, b_outs)
    res = run_bass_kernel_spmd(nc, [dict(shared, **pc) for pc in per_core], core_ids=list(range(NCORES)))
    return res.results


def kernel(**inputs):
    a_outs = run_A(inputs)
    b_outs = run_B(inputs, a_outs)
    c_outs = run_C(inputs, b_outs)
    out = np.concatenate(
        [np.asarray(c_outs[c]["o_out"], np.float32).reshape(HD, T) for c in range(NCORES)], axis=1)
    return np.ascontiguousarray(out.T).reshape(1, S, HD).astype(np.float32)


# version 06
# speedup vs baseline: 1.1778x; 1.1778x over previous
import os
import numpy as np
import ml_dtypes
from contextlib import ExitStack

import concourse.bass as bass
import concourse.mybir as mybir
from concourse.bass_utils import run_bass_kernel_spmd

F32 = mybir.dt.float32
BF16 = mybir.dt.bfloat16
AF = mybir.ActivationFunctionType
ALU = mybir.AluOpType
BF = ml_dtypes.bfloat16

NCORES = 8
S, HD, DSSM, NHEAD = 8192, 4096, 8192, 64
NSTATE = 128
CONV_DIM = DSSM + 2 * NSTATE
PROJ_DIM = DSSM + CONV_DIM + NHEAD
T = S // NCORES
NCH, CK = 8, 128
EPS = 1e-5
NEGBIG = -1.0e30
SM_SHIFT = -32.0
SM_SCALE = 1.0 / float(np.sqrt(128.0))

ENGS = ("sync", "gpsimd", "tensor", "vector", "scalar")


class Prog:

    def __init__(self, nc, ctx):
        self.nc = nc
        self.ctx = ctx
        self.cmds = {e: [] for e in ENGS}
        self.ctrs = {}
        self.waits = {e: {} for e in ENGS}

    def ctr(self, name):
        if name not in self.ctrs:
            sem = self.ctx.enter_context(self.nc.semaphore(name))
            self.ctrs[name] = [sem, 0]
        return name

    def n(self, name):
        return self.ctrs[name][1]

    def wait(self, eng, name, val=None):
        if name not in self.ctrs:
            return
        sem, cnt = self.ctrs[name]
        v = cnt if val is None else val
        if v <= 0 or self.waits[eng].get(name, 0) >= v:
            return
        self.waits[eng][name] = v
        self.cmds[eng].append(lambda e, s=sem, vv=v: e.wait_ge(s, vv))

    def run(self, eng, fn, inc=None, k=1):
        if inc is None:
            self.cmds[eng].append(fn)
            return None
        self.ctr(inc)
        ent = self.ctrs[inc]
        self.cmds[eng].append(lambda e, f=fn, s=ent[0], kk=k: f(e).then_inc(s, kk))
        ent[1] += k
        return ent[1]

    def barrier(self, queue_map=None):
        queue_map = queue_map or {}
        for e, names in queue_map.items():
            for nm in names:
                self.wait(e, nm)
        for e in ENGS:
            for nm in list(self.ctrs):
                if not nm.startswith("bar_"):
                    self.wait(e, nm)
        vals = []
        for e in ENGS:
            nm = self.ctr(f"bar_{e}")
            ent = self.ctrs[nm]
            ent[1] += 1
            self.cmds[e].append(lambda en, s=ent[0]: en.nop().then_inc(s, 1))
            vals.append((ent[0], ent[1]))
        for e in ENGS:
            for s, v in vals:
                self.cmds[e].append(lambda en, ss=s, vv=v: en.wait_ge(ss, vv))
            for nm2, ent in self.ctrs.items():
                self.waits[e][nm2] = ent[1]

    def replay(self):
        with self.nc.Block() as block:
            @block.sync
            def _(e):
                for f in self.cmds["sync"]:
                    f(e)

            @block.gpsimd
            def _(e):
                for f in self.cmds["gpsimd"]:
                    f(e)

            @block.tensor
            def _(e):
                for f in self.cmds["tensor"]:
                    f(e)

            @block.vector
            def _(e):
                for f in self.cmds["vector"]:
                    f(e)

            @block.scalar
            def _(e):
                for f in self.cmds["scalar"]:
                    f(e)


def _blocked(w):
    Kt, Mt = w.shape
    return np.ascontiguousarray(
        w.reshape(Kt // 128, 128, Mt // 128, 128).transpose(2, 1, 0, 3).reshape(
            Mt // 128, 128, Kt)).astype(np.float32)


def build_A():
    nc = bass.Bass("TRN2", target_bir_lowering=False, debug=False, detect_race_conditions=False)
    i_xt = nc.dram_tensor("i_xt", [HD, 128 + T], F32, kind="ExternalInput")
    i_win = nc.dram_tensor("i_win", [131, 128, 4096], F32, kind="ExternalInput")
    i_kern = nc.dram_tensor("i_kern", [32, 128, 4096], F32, kind="ExternalInput")
    i_bias = nc.dram_tensor("i_bias", [128, 32], F32, kind="ExternalInput")
    i_cw = nc.dram_tensor("i_cw", [128, 264], F32, kind="ExternalInput")
    i_dtb = nc.dram_tensor("i_dtb", [64, 1], F32, kind="ExternalInput")
    i_alog = nc.dram_tensor("i_alog", [64, 1], F32, kind="ExternalInput")
    i_drow = nc.dram_tensor("i_drow", [1, 64], F32, kind="ExternalInput")
    i_ge = nc.dram_tensor("i_ge", [128, 128], F32, kind="ExternalInput")
    i_la = nc.dram_tensor("i_la", [4, 128, 4096], F32, kind="ExternalInput")
    i_lbt = nc.dram_tensor("i_lbt", [4, 128, 4096], F32, kind="ExternalInput")
    i_lat = nc.dram_tensor("i_lat", [128, 1024], F32, kind="ExternalInput")
    i_lbst = nc.dram_tensor("i_lbst", [32, 4096], F32, kind="ExternalInput")
    i_ident = nc.dram_tensor("i_ident", [128, 128], BF16, kind="ExternalInput")
    i_identf = nc.dram_tensor("i_identf", [128, 128], F32, kind="ExternalInput")
    i_mp = nc.dram_tensor("i_mp", [128, 128], F32, kind="ExternalInput")
    i_ones = nc.dram_tensor("i_ones", [128, 128], F32, kind="ExternalInput")
    i_onescol = nc.dram_tensor("i_onescol", [128, 1], F32, kind="ExternalInput")
    i_misc = nc.dram_tensor("i_misc", [1, 4], F32, kind="ExternalInput")

    o_siluz = nc.dram_tensor("o_siluz", [64, 128, T], BF16, kind="ExternalOutput")
    o_xbcc = nc.dram_tensor("o_xbcc", [66, 128, T], BF16, kind="ExternalOutput")
    o_ylc = nc.dram_tensor("o_ylc", [NCH, 128, 8192], BF16, kind="ExternalOutput")
    o_g = nc.dram_tensor("o_g", [32, 128, T], F32, kind="ExternalOutput")
    o_lamg = nc.dram_tensor("o_lamg", [64, T], F32, kind="ExternalOutput")
    o_scor = nc.dram_tensor("o_scor", [128, 8192], BF16, kind="ExternalOutput")

    d_row1 = nc.dram_tensor("d_row1", [8192], F32, kind="Internal")
    d_row2 = nc.dram_tensor("d_row2", [64], F32, kind="Internal")

    ctx = ExitStack()
    with ctx:
        pr = Prog(nc, ctx)
        sbt = lambda nm, sh, dt: ctx.enter_context(nc.sbuf_tensor(nm, sh, dt))
        pst = lambda nm, sh, dt=F32: ctx.enter_context(nc.psum_tensor(nm, sh, dt))

        bk = [pst(f"bk{i}", [128, 512]) for i in range(5)]
        pT = [pst(f"pT{i}", [128, 1024], BF16) for i in range(2)]

        ident = sbt("ident", [128, 128], BF16)
        identf = sbt("identf", [128, 128], F32)
        mp = sbt("mp", [128, 128], F32)
        onescol = sbt("onescol", [128, 1], F32)
        ones1r = sbt("ones1r", [1, 128], F32)
        cw = sbt("cw", [128, 264], F32)
        dtb = sbt("dtb", [64, 1], F32)
        nega = sbt("nega", [64, 1], F32)
        alog = sbt("alog", [64, 1], F32)
        drow = sbt("drow", [1, 64], F32)
        drowbc = sbt("drowbc", [128, 64], F32)
        biasfm = sbt("biasfm", [128, 32], F32)
        epsc = sbt("epsc", [1, 1], F32)
        onec = sbt("onec", [1, 1], F32)
        halobuf = sbt("halobuf", [128, 66 * 4], BF16)
        dtt = sbt("dtt", [64, T], F32)
        lamg = sbt("lamg", [64, T], F32)
        hst = sbt("hst", [128, 64 * 128], BF16)


        c_ld = pr.ctr("c_ld")
        c_gld = pr.ctr("c_gld")
        c_st = pr.ctr("c_st")
        c_pe = pr.ctr("c_pe")
        c_dve = pr.ctr("c_dve")
        c_act = pr.ctr("c_act")
        QMAP = {"sync": [c_ld], "gpsimd": [c_gld, c_st]}

        for o, i in ((ident[:], i_ident[:]), (identf[:], i_identf[:]), (mp[:], i_mp[:]),
                     (onescol[:], i_onescol[:]), (ones1r[:], i_ones[0:1, :]),
                     (epsc[:], i_misc[0:1, 0:1]), (onec[:], i_misc[0:1, 1:2]),
                     (cw[:], i_cw[:]), (dtb[:], i_dtb[:]), (alog[:], i_alog[:]),
                     (drow[:], i_drow[:]), (biasfm[:], i_bias[:])):
            pr.run("sync", lambda e, oo=o, ii=i: e.dma_start(out=oo, in_=ii), inc=c_ld, k=16)
        pr.run("gpsimd", lambda e: e.dma_start(out=gecol[:], in_=i_ge[:]), inc=c_gld, k=16)
        pr.run("gpsimd", lambda e: e.dma_start(out=asht[:], in_=i_lat[:]), inc=c_gld, k=16)
        pr.run("gpsimd", lambda e: e.dma_start(out=bsht[:], in_=i_lbst[:]), inc=c_gld, k=16)

        pr.wait("scalar", c_ld)
        pr.run("scalar", lambda e: e.activation(nega[:], alog[:], AF.Exp), inc=c_act)
        pr.wait("vector", c_act)
        pr.run("vector", lambda e: e.tensor_scalar_mul(nega[:], nega[:], -1.0), inc=c_dve)
        pr.wait("tensor", c_ld)
        pr.run("tensor", lambda e: e.matmul(bk[4][:, 0:64], ones1r[:], drow[:], start=True, stop=True), inc=c_pe)
        pr.wait("vector", c_pe)
        pr.run("vector", lambda e: e.tensor_copy(drowbc[:], bk[4][:, 0:64]), inc=c_dve)
        pr.run("vector", lambda e: e.memset(hst[:], 0.0), inc=c_dve)

        A_LORA = int(os.environ.get("A_LORA", "1"))
        lsc = ExitStack()
        aeff = lsc.enter_context(nc.sbuf_tensor("aeff", [128, 32 * 32], BF16))
        asht = lsc.enter_context(nc.sbuf_tensor("asht", [128, 32 * 32], BF16))
        beff = lsc.enter_context(nc.sbuf_tensor("beff", [32, 4096], BF16))
        bsht = lsc.enter_context(nc.sbuf_tensor("bsht", [32, 4096], BF16))
        gecol = lsc.enter_context(nc.sbuf_tensor("gecol", [128, 4 * 32], BF16))
        with ExitStack() as lctx:
          if A_LORA:
            lab = [lctx.enter_context(nc.sbuf_tensor(f"lab{kt}", [128, 4096], BF16)) for kt in range(4)]
            pr.wait("gpsimd", c_dve)
            for kt in range(4):
                pr.run("gpsimd", lambda e, kt=kt: e.dma_start(out=lab[kt][:], in_=i_la[kt]), inc=c_gld, k=16)
            pr.wait("tensor", c_gld)
            pr.wait("tensor", c_dve)
            for kc in range(32):
                for kt in range(4):
                    pr.run("tensor", lambda e, kc=kc, kt=kt: e.matmul(
                        bk[0][:, 0:32], lab[kt][:, kc * 128:(kc + 1) * 128],
                        gecol[:, kt * 32:(kt + 1) * 32],
                        start=(kt == 0), stop=(kt == 3)), inc=(c_pe if kt == 3 else None))
                pr.wait("vector", c_pe)
                pr.run("vector", lambda e, kc=kc: e.tensor_copy(aeff[:, kc * 32:(kc + 1) * 32], bk[0][:, 0:32]), inc=c_dve)
                pr.wait("tensor", c_dve)
            pr.wait("gpsimd", c_pe)
            for kt in range(4):
                pr.run("gpsimd", lambda e, kt=kt: e.dma_start(out=lab[kt][:], in_=i_lbt[kt]), inc=c_gld, k=16)
            pr.wait("tensor", c_gld)
            for pc in range(8):
                for kt in range(4):
                    pr.run("tensor", lambda e, pc=pc, kt=kt: e.matmul(
                        bk[0][0:32, :], gecol[:, kt * 32:(kt + 1) * 32],
                        lab[kt][:, pc * 512:(pc + 1) * 512],
                        start=(kt == 0), stop=(kt == 3)), inc=(c_pe if kt == 3 else None))
                pr.wait("vector", c_pe)
                pr.run("vector", lambda e, pc=pc: e.tensor_copy(beff[:, pc * 512:(pc + 1) * 512], bk[0][0:32, :]), inc=c_dve)
                pr.wait("tensor", c_dve)

        with ExitStack() as p1:
            x_sb = p1.enter_context(nc.sbuf_tensor("x_sb", [128, 32 * 512], F32))
            xn_sb = p1.enter_context(nc.sbuf_tensor("xn_sb", [128, 32 * 512], BF16))
            wblk = [p1.enter_context(nc.sbuf_tensor(f"wblk{i}", [128, 32 * 128], BF16)) for i in range(2)]
            kblk = wblk
            zbuf = [p1.enter_context(nc.sbuf_tensor(f"zbuf{i}", [128, 512], BF16)) for i in range(2)]
            xcb = [p1.enter_context(nc.sbuf_tensor(f"xcb{i}", [128, 512], BF16)) for i in range(2)]
            gb = [p1.enter_context(nc.sbuf_tensor(f"gb{i}", [128, 512], F32)) for i in range(2)]
            xwin = p1.enter_context(nc.sbuf_tensor("xwin", [128, 516], BF16))
            diag = p1.enter_context(nc.sbuf_tensor("diag", [128, 4 * 128], BF16))
            x2t = [p1.enter_context(nc.sbuf_tensor(f"x2t{i}", [128, 512], F32)) for i in range(2)]
            mrow = p1.enter_context(nc.sbuf_tensor("mrow", [1, 512], F32))
            rrow = p1.enter_context(nc.sbuf_tensor("rrow", [1, 512], F32))
            srow = p1.enter_context(nc.sbuf_tensor("srow", [1, 512], F32))
            mbc = p1.enter_context(nc.sbuf_tensor("mbc", [128, 512], F32))
            rbc = p1.enter_context(nc.sbuf_tensor("rbc", [128, 512], F32))
            tmpf = p1.enter_context(nc.sbuf_tensor("tmpf", [128, 512], F32))
            axb = p1.enter_context(nc.sbuf_tensor("axb", [32, 512], BF16))
            axsb = p1.enter_context(nc.sbuf_tensor("axsb", [32, 512], BF16))
            spA = p1.enter_context(nc.sbuf_tensor("spA", [64, 512], F32))
            spB = p1.enter_context(nc.sbuf_tensor("spB", [64, 512], F32))
            spC = p1.enter_context(nc.sbuf_tensor("spC", [64, 512], F32))
            dtsc = p1.enter_context(nc.sbuf_tensor("dtsc", [64, T], F32))

            def ln_xn(ncols, xsrc, xn_out):
                pr.wait("sync", c_dve)
                pr.wait("sync", c_pe)
                for kc in range(32):
                    pr.run("sync", lambda e, kc=kc: e.dma_start(
                        out=x_sb[:, kc * 512:kc * 512 + ncols],
                        in_=i_xt[kc * 128:(kc + 1) * 128, xsrc:xsrc + ncols]), inc=c_ld, k=16)
                pr.wait("scalar", c_ld)
                pr.wait("scalar", c_pe)
                pr.wait("tensor", c_ld)
                pr.wait("tensor", c_act)
                pr.wait("tensor", c_dve)
                for kc in range(32):
                    pr.run("tensor", lambda e, kc=kc: e.matmul(
                        bk[3][0:1, 0:ncols], onescol[:], x_sb[:, kc * 512:kc * 512 + ncols],
                        start=(kc == 0), stop=(kc == 31)), inc=None)
                sq, x2v = {}, {}
                for kc in range(32):
                    if kc >= 2:
                        pr.wait("scalar", c_pe, x2v[kc - 2])
                    sq[kc] = pr.run("scalar", lambda e, kc=kc: e.activation(
                        x2t[kc % 2][:, 0:ncols], x_sb[:, kc * 512:kc * 512 + ncols], AF.Square), inc=c_act)
                    pr.wait("tensor", c_act, sq[kc])
                    x2v[kc] = pr.run("tensor", lambda e, kc=kc: e.matmul(
                        bk[4][0:1, 0:ncols], onescol[:], x2t[kc % 2][:, 0:ncols],
                        start=(kc == 0), stop=(kc == 31)), inc=c_pe)
                pr.wait("vector", c_pe)
                pr.run("vector", lambda e: e.tensor_scalar_mul(mrow[0:1, 0:ncols], bk[3][0:1, 0:ncols], 1.0 / HD), inc=c_dve)
                pr.run("vector", lambda e: e.tensor_scalar_mul(srow[0:1, 0:ncols], bk[4][0:1, 0:ncols], 1.0 / HD), inc=c_dve)
                pr.wait("vector", c_dve)
                pr.run("vector", lambda e: e.tensor_mul(rrow[0:1, 0:ncols], mrow[0:1, 0:ncols], mrow[0:1, 0:ncols]), inc=c_dve)
                pr.wait("vector", c_dve)
                pr.run("vector", lambda e: e.tensor_sub(srow[0:1, 0:ncols], srow[0:1, 0:ncols], rrow[0:1, 0:ncols]), inc=c_dve)
                pr.wait("scalar", c_dve)
                pr.run("scalar", lambda e: e.activation(srow[0:1, 0:ncols], srow[0:1, 0:ncols], AF.Sqrt, bias=epsc[:]), inc=c_act)
                pr.wait("vector", c_act)
                pr.run("vector", lambda e: e.reciprocal(rrow[0:1, 0:ncols], srow[0:1, 0:ncols]), inc=c_dve)
                pr.wait("tensor", c_dve)
                pr.run("tensor", lambda e: e.matmul(bk[3][:, 0:ncols], ones1r[:], mrow[0:1, 0:ncols], start=True, stop=True), inc=None)
                pr.run("tensor", lambda e: e.matmul(bk[4][:, 0:ncols], ones1r[:], rrow[0:1, 0:ncols], start=True, stop=True), inc=c_pe)
                pr.wait("vector", c_pe)
                pr.run("vector", lambda e: e.tensor_copy(mbc[:, 0:ncols], bk[3][:, 0:ncols]), inc=c_dve)
                pr.run("vector", lambda e: e.tensor_copy(rbc[:, 0:ncols], bk[4][:, 0:ncols]), inc=c_dve)
                pr.wait("vector", c_dve)
                for kc in range(32):
                    pr.run("vector", lambda e, kc=kc: e.tensor_sub(
                        tmpf[:, 0:ncols], x_sb[:, kc * 512:kc * 512 + ncols], mbc[:, 0:ncols]), inc=c_dve)
                    pr.wait("vector", c_dve)
                    pr.run("vector", lambda e, kc=kc: e.tensor_mul(
                        xn_out(kc)[:, 0:ncols], tmpf[:, 0:ncols], rbc[:, 0:ncols]), inc=c_dve)

            A_NG = int(os.environ.get("A_NG", "2"))
            A_NMC = int(os.environ.get("A_NMC", "131"))
            A_FFN = int(os.environ.get("A_FFN", "1"))
            passes = [(128, None, 0)] + [(512, g, 128 + g * 512) for g in range(A_NG)]
            if A_NG == 0:
                passes = []
            wblk_guard = {0: 0, 1: 0}
            for (ncols, g, xoff) in passes:
                halo_pass = g is None
                ln_xn(ncols, xoff, lambda kc: xn_sb[:, kc * 512:kc * 512 + ncols])

                pe_mc, ev_mc = {}, {}
                for Mc in range(A_NMC if int(os.environ.get("A_PROJ", "1")) else 0):
                    wb = wblk[Mc % 2]
                    pr.wait("gpsimd", c_pe, wblk_guard[Mc % 2])
                    gld_v = pr.run("gpsimd", lambda e, Mc=Mc, wb=wb: e.dma_start(
                        out=wb[:], in_=i_win[Mc]), inc=c_gld, k=16)
                    pr.wait("tensor", c_gld, gld_v)
                    ev = ev_mc.get(Mc - 3)
                    if ev is not None:
                        pr.wait("tensor", ev[0], ev[1])
                    elif Mc < 3:
                        pr.wait("tensor", c_dve)
                        pr.wait("tensor", c_act)
                    pb = bk[Mc % 3]
                    for kc in range(32):
                        pr.run("tensor", lambda e, kc=kc, wb=wb, pb=pb, ncols=ncols: e.matmul(
                            pb[:, 0:ncols], wb[:, kc * 128:(kc + 1) * 128],
                            xn_sb[:, kc * 512:kc * 512 + ncols],
                            start=(kc == 0), stop=(kc == 31)),
                            inc=(None if kc < 31 else c_pe))
                    pev = pr.n(c_pe)
                    pe_mc[Mc] = pev
                    wblk_guard[Mc % 2] = pev

                    if Mc < 64:
                        if halo_pass:
                            continue
                        zb = zbuf[Mc % 2]
                        pr.wait("scalar", c_pe, pev)
                        pr.wait("scalar", c_st, ev_mc.get(("st", Mc - 2), 0))
                        av = pr.run("scalar", lambda e, zb=zb, pb=pb: e.activation(zb[:], pb[:], AF.Silu), inc=c_act)
                        ev_mc[Mc] = (c_act, av)
                        pr.wait("gpsimd", c_act, av)
                        ev_mc[("st", Mc)] = pr.run("gpsimd", lambda e, Mc=Mc, g=g, zb=zb: e.dma_start(
                            out=o_siluz[Mc, :, g * 512:(g + 1) * 512], in_=zb[:]), inc=c_st, k=16)
                    elif Mc < 130:
                        j = Mc - 64
                        pr.wait("vector", c_pe, pev)
                        pr.run("vector", lambda e, j=j: e.tensor_copy(xwin[:, 0:4], halobuf[:, j * 4:(j + 1) * 4]), inc=None)
                        pr.run("vector", lambda e, pb=pb, ncols=ncols: e.tensor_copy(
                            xwin[:, 4:4 + ncols], pb[:, 0:ncols]), inc=None)
                        dvs = pr.run("vector", lambda e, j=j, pb=pb, ncols=ncols: e.tensor_copy(
                            halobuf[:, j * 4:(j + 1) * 4], pb[:, ncols - 4:ncols]), inc=c_dve)
                        ev_mc[Mc] = (c_dve, dvs)
                        for k in range(4):
                            pr.run("vector", lambda e, j=j, k=k: e.tensor_scalar_mul(
                                diag[:, k * 128:(k + 1) * 128], ident[:], cw[:, j * 4 + k:j * 4 + k + 1]),
                                inc=(None if k < 3 else c_dve))
                        dvev = pr.n(c_dve)
                        if halo_pass:
                            continue
                        cb = bk[3 + (j % 2)]
                        pr.wait("tensor", c_dve, dvev)
                        pr.wait("tensor", c_act, ev_mc.get(("cv", j - 2), 0))
                        for k in range(4):
                            pr.run("tensor", lambda e, k=k, cb=cb, ncols=ncols: e.matmul(
                                cb[:, 0:ncols], diag[:, k * 128:(k + 1) * 128], xwin[:, k + 1:k + 1 + ncols],
                                start=(k == 0), stop=(k == 3)), inc=(c_pe if k == 3 else None))
                        xc = xcb[j % 2]
                        pr.wait("scalar", c_pe)
                        pr.wait("scalar", c_st, ev_mc.get(("cst", j - 2), 0))
                        cv = pr.run("scalar", lambda e, xc=xc, cb=cb: e.activation(xc[:], cb[:], AF.Silu), inc=c_act)
                        ev_mc[("cv", j)] = cv
                        pr.wait("gpsimd", c_act, cv)
                        ev_mc[("cst", j)] = pr.run("gpsimd", lambda e, j=j, g=g, xc=xc: e.dma_start(
                            out=o_xbcc[j, :, g * 512:(g + 1) * 512], in_=xc[:]), inc=c_st, k=16)
                    else:
                        if halo_pass:
                            continue
                        ev_mc[Mc] = (c_act, pr.n(c_act) + 2)
                        pr.wait("scalar", c_pe, pev)
                        pr.run("scalar", lambda e, pb=pb: e.activation(spA[:], pb[0:64, :], AF.Abs, bias=dtb[:]), inc=c_act)
                        pr.run("scalar", lambda e, pb=pb: e.activation(spB[:], pb[0:64, :], AF.Relu, bias=dtb[:]), inc=c_act)
                        pr.wait("scalar", c_act)
                        pr.run("scalar", lambda e: e.activation(spC[:], spA[:], AF.Exp, scale=-1.0), inc=c_act)
                        pr.wait("scalar", c_act)
                        pr.run("scalar", lambda e: e.activation(spA[:], spC[:], AF.Ln, bias=onescol[0:64, :]), inc=c_act)
                        pr.wait("vector", c_act)
                        pr.run("vector", lambda e, g=g: e.tensor_add(
                            dtt[:, g * 512:(g + 1) * 512], spB[:], spA[:]), inc=c_dve)

                if not A_FFN or halo_pass:
                    continue
                pr.wait("tensor", c_dve)
                pr.wait("tensor", c_act)
                for kc in range(32):
                    pr.run("tensor", lambda e, kc=kc: e.matmul(
                        bk[3][0:32, :], aeff[:, kc * 32:(kc + 1) * 32], xn_sb[:, kc * 512:(kc + 1) * 512],
                        start=(kc == 0), stop=(kc == 31)), inc=None)
                for kc in range(32):
                    pr.run("tensor", lambda e, kc=kc: e.matmul(
                        bk[4][0:32, :], asht[:, kc * 32:(kc + 1) * 32], xn_sb[:, kc * 512:(kc + 1) * 512],
                        start=(kc == 0), stop=(kc == 31)), inc=(None if kc < 31 else c_pe))
                pr.wait("vector", c_pe)
                pr.run("vector", lambda e: e.tensor_copy(axb[:], bk[3][0:32, :]), inc=None)
                pr.run("vector", lambda e: e.tensor_copy(axsb[:], bk[4][0:32, :]), inc=c_dve)
                fpe, fev, fst = {}, {}, {}
                pr.wait("tensor", c_dve)
                pr.wait("tensor", c_act)
                for Mc in range(32):
                    kb = kblk[Mc % 2]
                    pr.wait("gpsimd", c_pe, wblk_guard[Mc % 2])
                    gv = pr.run("gpsimd", lambda e, Mc=Mc, kb=kb: e.dma_start(
                        out=kb[:], in_=i_kern[Mc]), inc=c_gld, k=16)
                    pr.wait("tensor", c_gld, gv)
                    if Mc >= 3:
                        pr.wait("tensor", fev[Mc - 3][0], fev[Mc - 3][1])
                    pb = bk[Mc % 3]
                    for kc in range(32):
                        pr.run("tensor", lambda e, kc=kc, kb=kb, pb=pb: e.matmul(
                            pb[:], kb[:, kc * 128:(kc + 1) * 128], xn_sb[:, kc * 512:(kc + 1) * 512],
                            start=(kc == 0), stop=False), inc=None)
                    pr.run("tensor", lambda e, Mc=Mc, pb=pb: e.matmul(
                        pb[:], beff[:, Mc * 128:(Mc + 1) * 128], axb[:], start=False, stop=False), inc=None)
                    pr.run("tensor", lambda e, Mc=Mc, pb=pb: e.matmul(
                        pb[:], bsht[:, Mc * 128:(Mc + 1) * 128], axsb[:], start=False, stop=True), inc=c_pe)
                    pev = pr.n(c_pe)
                    fpe[Mc] = pev
                    wblk_guard[Mc % 2] = pev
                    gbx = gb[Mc % 2]
                    pr.wait("scalar", c_pe, pev)
                    pr.wait("scalar", c_st, fst.get(Mc - 2, 0))
                    av = pr.run("scalar", lambda e, Mc=Mc, gbx=gbx, pb=pb: e.activation(
                        gbx[:], pb[:], AF.Silu, bias=biasfm[:, Mc:Mc + 1]), inc=c_act)
                    fev[Mc] = (c_act, av)
                    pr.wait("gpsimd", c_act, av)
                    fst[Mc] = pr.run("gpsimd", lambda e, Mc=Mc, g=g, gbx=gbx: e.dma_start(
                        out=o_g[Mc, :, g * 512:(g + 1) * 512], in_=gbx[:]), inc=c_st, k=16)

            pr.wait("vector", c_dve)
            pr.wait("vector", c_act)
            pr.run("vector", lambda e: e.tensor_scalar_mul(dtsc[:], dtt[:], nega[:]), inc=c_dve)
            pr.wait("vector", c_dve)
            pr.run("vector", lambda e: e.tensor_tensor_scan(lamg[:], dtsc[:], dtsc[:], 0.0, ALU.add, ALU.bypass), inc=c_dve)
            pr.wait("gpsimd", c_dve)
            pr.run("gpsimd", lambda e: e.dma_start(out=o_lamg[:], in_=lamg[:]), inc=c_st, k=16)
        lsc.close()
        pr.barrier(QMAP)

        with ExitStack() as p2:
            xcc = [p2.enter_context(nc.sbuf_tensor(f"xcc{i}", [128, 64 * 128], BF16)) for i in range(2)]
            Bt = p2.enter_context(nc.sbuf_tensor("Bt", [128, T], BF16))
            Ct = p2.enter_context(nc.sbuf_tensor("Ct", [128, T], BF16))
            dtx = p2.enter_context(nc.sbuf_tensor("dtx", [128, 8192], BF16))
            mask = p2.enter_context(nc.sbuf_tensor("mask", [128, 8192], BF16))
            cbtm = p2.enter_context(nc.sbuf_tensor("cbtm", [128, 8192], BF16))
            cwt = p2.enter_context(nc.sbuf_tensor("cwt", [128, 8192], BF16))
            ylcb = p2.enter_context(nc.sbuf_tensor("ylcb", [128, 8192], BF16))
            lrowp = [p2.enter_context(nc.sbuf_tensor(f"lrowp{i}", [1, 512], F32)) for i in range(2)]
            ltrow = p2.enter_context(nc.sbuf_tensor("ltrow", [1, 64], F32))
            argp = [p2.enter_context(nc.sbuf_tensor(f"argp{i}", [128, 512], F32)) for i in range(2)]
            expp = p2.enter_context(nc.sbuf_tensor("expp", [128, 512], BF16))
            lamfm = p2.enter_context(nc.sbuf_tensor("lamfm", [64, 128], F32))
            lamTM = p2.enter_context(nc.sbuf_tensor("lamTM", [128, 64], F32))
            dttTM = p2.enter_context(nc.sbuf_tensor("dttTM", [128, 64], F32))
            ltotbc = p2.enter_context(nc.sbuf_tensor("ltotbc", [128, 64], F32))
            endw = p2.enter_context(nc.sbuf_tensor("endw", [128, 64], F32))
            atotbc = p2.enter_context(nc.sbuf_tensor("atotbc", [128, 64], F32))
            cbtraw = p2.enter_context(nc.sbuf_tensor("cbtraw", [128, 128], BF16))
            btm = p2.enter_context(nc.sbuf_tensor("btm", [128, 128], BF16))
            bwb = [p2.enter_context(nc.sbuf_tensor(f"bwb{i}", [128, 128], BF16)) for i in range(2)]

            pr.run("sync", lambda e: e.dma_start(out=Bt[:], in_=o_xbcc[64]), inc=c_ld, k=16)
            pr.run("sync", lambda e: e.dma_start(out=Ct[:], in_=o_xbcc[65]), inc=c_ld, k=16)
            btct_v = pr.n(c_ld)

            def load_xcc(c):
                buf = xcc[c % 2]
                pr.wait("sync", c_dve)
                pr.wait("sync", c_pe)
                pr.wait("gpsimd", c_dve)
                pr.wait("gpsimd", c_pe)
                for m in range(64):
                    eng, cc = ("sync", c_ld) if m % 2 == 0 else ("gpsimd", c_gld)
                    pr.run(eng, lambda e, m=m, c=c, buf=buf: e.dma_start(
                        out=buf[:, m * 128:(m + 1) * 128], in_=o_xbcc[m, :, c * 128:(c + 1) * 128]),
                        inc=cc, k=16)

            A_NCH = int(os.environ.get("A_NCH", str(NCH)))
            if A_NCH > 0:
                load_xcc(0)
            for c in range(A_NCH):
                buf = xcc[c % 2]
                ld_mark = (pr.n(c_ld), pr.n(c_gld))
                if c + 1 < NCH:
                    load_xcc(c + 1)

                if c == 0:
                    pr.run("vector", lambda e: e.tensor_copy(lamfm[:], lamg[:, 0:128]), inc=c_dve)
                else:
                    pr.run("vector", lambda e, c=c: e.tensor_scalar(
                        lamfm[:], lamg[:, c * 128:(c + 1) * 128], lamg[:, c * 128 - 1:c * 128], None,
                        ALU.subtract), inc=c_dve)
                pr.wait("gpsimd", c_dve)
                pr.run("gpsimd", lambda e: e.dma_start(
                    out=d_row1[:].rearrange("(h t) -> h t", h=64), in_=lamfm[:]), inc=c_st, k=16)
                pr.run("gpsimd", lambda e: e.dma_start(
                    out=d_row2[:].rearrange("(h o) -> h o", h=64),
                    in_=lamfm[:, 127:128]), inc=c_st, k=16)
                pr.wait("sync", c_st)
                pr.run("sync", lambda e: e.dma_start(out=ltrow[:], in_=d_row2[:].unsqueeze(0)), inc=c_ld, k=16)
                row_v = pr.n(c_ld)

                A_P2 = int(os.environ.get("A_P2", "9"))
                if A_P2 < 2:
                    continue
                pr.wait("tensor", c_dve)
                pr.run("tensor", lambda e: e.transpose(bk[4][0:128, 0:64], lamfm[:], identf[0:64, 0:64]), inc=c_pe)
                pr.wait("vector", c_pe)
                pr.run("vector", lambda e: e.tensor_copy(lamTM[:], bk[4][0:128, 0:64]), inc=c_dve)
                pr.wait("tensor", c_dve)
                pr.run("tensor", lambda e, c=c: e.transpose(bk[4][0:128, 0:64], dtt[:, c * 128:(c + 1) * 128], identf[0:64, 0:64]), inc=c_pe)
                pr.wait("vector", c_pe)
                pr.run("vector", lambda e: e.tensor_copy(dttTM[:], bk[4][0:128, 0:64]), inc=c_dve)
                pr.wait("tensor", c_ld, row_v)
                pr.wait("tensor", c_dve)
                pr.run("tensor", lambda e: e.matmul(bk[4][:, 0:64], ones1r[:], ltrow[:], start=True, stop=True), inc=c_pe)
                pr.wait("vector", c_pe)
                pr.run("vector", lambda e: e.tensor_copy(ltotbc[:], bk[4][:, 0:64]), inc=c_dve)
                pr.wait("vector", c_dve)
                pr.run("vector", lambda e: e.scalar_tensor_tensor(
                    endw[:], lamTM[:], -1.0, ltotbc[:], ALU.mult, ALU.add), inc=c_dve)
                pr.wait("scalar", c_dve)
                pr.run("scalar", lambda e: e.activation(endw[:], endw[:], AF.Exp), inc=None)
                pr.run("scalar", lambda e: e.activation(atotbc[:], ltotbc[:], AF.Exp), inc=c_act)
                pr.wait("tensor", c_ld, btct_v)
                pr.wait("tensor", c_dve)
                pr.run("tensor", lambda e, c=c: e.matmul(
                    bk[4][:, 0:128], Bt[:, c * 128:(c + 1) * 128], Ct[:, c * 128:(c + 1) * 128],
                    start=True, stop=True), inc=c_pe)
                pr.wait("vector", c_pe)
                pr.run("vector", lambda e: e.tensor_copy(cbtraw[:], bk[4][:, 0:128]), inc=c_dve)

                if A_P2 < 3:
                    continue
                for p in range(16):
                    pb = bk[p % 2]
                    lrp = lrowp[p % 2]
                    pr.wait("sync", c_pe)
                    pr.run("sync", lambda e, p=p, lrp=lrp: e.dma_start(
                        out=lrp[:], in_=d_row1[p * 512:(p + 1) * 512].unsqueeze(0)), inc=c_ld, k=16)
                    pr.wait("tensor", c_dve)
                    pr.wait("tensor", c_act)
                    pr.wait("tensor", c_ld)
                    pr.run("tensor", lambda e, lrp=lrp, pb=pb: e.matmul(
                        pb[:], ones1r[:], lrp[:], start=True, stop=True), inc=c_pe)
                    pev = pr.n(c_pe)
                    ap = argp[p % 2]
                    pr.wait("vector", c_pe, pev)
                    for j in range(4):
                        h = p * 4 + j
                        pr.run("vector", lambda e, j=j, h=h, ap=ap, pb=pb: e.scalar_tensor_tensor(
                            ap[:, j * 128:(j + 1) * 128], pb[:, j * 128:(j + 1) * 128],
                            lamTM[:, h:h + 1], mp[:], ALU.subtract, ALU.add),
                            inc=(None if j < 3 else c_dve))
                    pr.wait("scalar", c_pe, pev)
                    pr.wait("scalar", c_dve)
                    pr.run("scalar", lambda e, pb=pb: e.activation(expp[:], pb[:], AF.Exp), inc=None)
                    pr.run("scalar", lambda e, p=p, ap=ap: e.activation(
                        mask[:, p * 512:(p + 1) * 512], ap[:], AF.Exp), inc=c_act)
                    pr.wait("vector", c_act)
                    pr.run("vector", lambda e, p=p, c=c: e.tensor_tensor(
                        cwt[:, p * 512:(p + 1) * 512].rearrange("q (a b) -> q a b", a=4),
                        expp[:].rearrange("q (a b) -> q a b", a=4),
                        Ct[:, c * 128:(c + 1) * 128].unsqueeze(1).broadcast_to([128, 4, 128]),
                        ALU.mult), inc=c_dve)

                pr.wait("vector", c_dve)
                pr.run("vector", lambda e: e.tensor_tensor(
                    cbtm[:].rearrange("q (a b) -> q a b", a=64),
                    mask[:].rearrange("q (a b) -> q a b", a=64),
                    cbtraw[:].unsqueeze(1).broadcast_to([128, 64, 128]), ALU.mult), inc=c_dve)

                if A_P2 < 4:
                    continue
                pr.wait("tensor", c_ld, ld_mark[0])
                pr.wait("tensor", c_gld, ld_mark[1])
                pr.wait("tensor", c_dve)
                for m in range(64):
                    pt = pT[m % 2]
                    if m >= 2:
                        pr.wait("tensor", c_dve, dtx_ev[m - 2])
                    pr.run("tensor", lambda e, m=m, buf=buf, pt=pt: e.transpose(
                        pt[:, 0:128], buf[:, m * 128:(m + 1) * 128], ident[:]), inc=c_pe)
                    pr.wait("vector", c_pe)
                    if m == 0:
                        dtx_ev = {}
                    dtx_ev[m] = pr.run("vector", lambda e, m=m, pt=pt: e.tensor_scalar_mul(
                        dtx[:, m * 128:(m + 1) * 128], pt[:, 0:128], dttTM[:, m:m + 1]), inc=c_dve)
                pr.wait("tensor", c_dve)
                pr.run("tensor", lambda e, c=c: e.transpose(pT[0][:, 0:128], Bt[:, c * 128:(c + 1) * 128], ident[:]), inc=c_pe)
                pr.wait("vector", c_pe)
                pr.run("vector", lambda e: e.tensor_copy(btm[:], pT[0][:, 0:128]), inc=c_dve)

                if A_P2 < 5:
                    continue
                pr.wait("tensor", c_dve)
                pr.wait("tensor", c_act)
                pr.wait("vector", c_dve)
                pe_h, dve_h = {}, {}
                for h in range(64):
                    pyb = bk[h % 2]
                    psb = bk[2 + (h % 2)]
                    pr.wait("vector", c_pe, pe_h.get(h - 2, 0))
                    pr.run("vector", lambda e, h=h: e.tensor_scalar_mul(
                        bwb[h % 2][:], btm[:], endw[:, h:h + 1]), inc=c_dve)
                    bwv = pr.n(c_dve)
                    pr.wait("tensor", c_dve, bwv)
                    pr.run("tensor", lambda e, h=h, pyb=pyb: e.matmul(
                        pyb[:, 0:128], dtx[:, h * 128:(h + 1) * 128], cbtm[:, h * 128:(h + 1) * 128],
                        start=True, stop=False), inc=None)
                    pr.run("tensor", lambda e, h=h, pyb=pyb: e.matmul(
                        pyb[:, 0:128], hst[:, h * 128:(h + 1) * 128], cwt[:, h * 128:(h + 1) * 128],
                        start=False, stop=True), inc=None)
                    pr.run("tensor", lambda e, h=h, psb=psb: e.matmul(
                        psb[:, 0:128], bwb[h % 2][:], dtx[:, h * 128:(h + 1) * 128],
                        start=True, stop=True), inc=c_pe)
                    pe_h[h] = pr.n(c_pe)
                    pr.wait("vector", c_pe, pe_h[h])
                    pr.run("vector", lambda e, h=h, psb=psb: e.scalar_tensor_tensor(
                        hst[:, h * 128:(h + 1) * 128], hst[:, h * 128:(h + 1) * 128],
                        atotbc[:, h:h + 1], psb[:, 0:128], ALU.mult, ALU.add), inc=None)
                    pr.run("vector", lambda e, h=h, buf=buf, pyb=pyb: e.scalar_tensor_tensor(
                        ylcb[:, h * 128:(h + 1) * 128], buf[:, h * 128:(h + 1) * 128],
                        drowbc[:, h:h + 1], pyb[:, 0:128], ALU.mult, ALU.add), inc=c_dve)
                    dve_h[h] = pr.n(c_dve)

                if A_P2 < 6:
                    continue
                pr.wait("gpsimd", c_dve)
                pr.run("gpsimd", lambda e, c=c: e.dma_start(out=o_ylc[c], in_=ylcb[:]), inc=c_st, k=16)
                pr.wait("vector", c_st)

            pr.wait("gpsimd", c_dve)
            pr.run("gpsimd", lambda e: e.dma_start(out=o_scor[:], in_=hst[:]), inc=c_st, k=16)
            pr.wait("gpsimd", c_st)
            pr.wait("sync", c_ld)

        pr.replay()
    return nc


def stage_A(inp):
    x = np.asarray(inp["L0_attn_hidden_states"], np.float32).reshape(S, HD)
    xT = np.ascontiguousarray(x.T)
    w_in = np.asarray(inp["L0_attn_W_in"], np.float32)
    w_pad = np.zeros((HD, 131 * 128), np.float32)
    w_pad[:, :PROJ_DIM] = w_in
    win_b = _blocked(w_pad)
    kern_b = _blocked(np.asarray(inp["L0_ffn_kernel"], np.float32))
    bias_fm = np.ascontiguousarray(np.asarray(inp["L0_ffn_bias"], np.float32).reshape(32, 128).T)
    cwr = np.ascontiguousarray(
        np.asarray(inp["L0_attn_conv_w"], np.float32).reshape(66, 128, 4).transpose(1, 0, 2).reshape(128, 264))
    dtb = np.asarray(inp["L0_attn_dt_bias"], np.float32).reshape(64, 1)
    alog = np.asarray(inp["L0_attn_A_log"], np.float32).reshape(64, 1)
    drow = np.asarray(inp["L0_attn_D_param"], np.float32).reshape(1, 64)
    la = np.ascontiguousarray(np.asarray(inp["L0_ffn_lora_A"], np.float32).reshape(512, 4096).reshape(4, 128, 4096))
    lbt = np.ascontiguousarray(
        np.asarray(inp["L0_ffn_lora_B"], np.float32).transpose(0, 2, 1).reshape(512, 4096).reshape(4, 128, 4096))
    lat = np.ascontiguousarray(
        np.asarray(inp["L0_ffn_lora_A_shared"], np.float32).T.reshape(32, 128, 32)
        .transpose(1, 0, 2).reshape(128, 1024))
    lbst = np.ascontiguousarray(np.asarray(inp["L0_ffn_lora_B_shared"], np.float32).T)
    ident = np.eye(128).astype(BF)
    identf = np.eye(128).astype(np.float32)
    mpat = np.where(np.arange(128)[None, :] >= np.arange(128)[:, None], 0.0, NEGBIG).astype(np.float32)
    ones = np.ones((128, 128), np.float32)
    gate = np.asarray(inp["L0_ffn_gate"], np.float32)

    shared = dict(i_win=win_b, i_kern=kern_b, i_bias=bias_fm, i_cw=cwr, i_dtb=dtb,
                  i_alog=alog, i_drow=drow, i_la=la, i_lbt=lbt, i_lat=lat, i_lbst=lbst,
                  i_ident=ident, i_identf=identf, i_mp=mpat, i_ones=ones,
                  i_onescol=np.ones((128, 1), np.float32),
                  i_misc=np.array([[EPS, 1.0, 0.0, 0.0]], np.float32))
    per_core = []
    for c in range(NCORES):
        t0 = c * T
        xt = np.zeros((HD, 128 + T), np.float32)
        lo = max(t0 - 128, 0)
        xt[:, 128 - (t0 - lo):] = xT[:, lo:t0 + T]
        ge = np.zeros((512, 32), np.float32)
        for nl in range(16):
            for r in range(32):
                ge[nl * 32 + r, r] = gate[c, nl]
        per_core.append(dict(i_xt=np.ascontiguousarray(xt),
                             i_ge=np.ascontiguousarray(
                                 ge.reshape(4, 128, 32).transpose(1, 0, 2).reshape(128, 128))))
    return shared, per_core


def run_A(inp):
    nc = build_A()
    shared, per_core = stage_A(inp)
    in_maps = [dict(shared, **pc) for pc in per_core]
    res = run_bass_kernel_spmd(nc, in_maps, core_ids=list(range(NCORES)))
    return res.results




def build_B():
    nc = bass.Bass("TRN2", target_bir_lowering=False, debug=False, detect_race_conditions=False)
    i_xtb = nc.dram_tensor("i_xtb", [32, 128, T], F32, kind="ExternalInput")
    i_ylc = nc.dram_tensor("i_ylc", [NCH, 128, 8192], BF16, kind="ExternalInput")
    i_siluz = nc.dram_tensor("i_siluz", [64, 128, T], BF16, kind="ExternalInput")
    i_ct = nc.dram_tensor("i_ct", [128, T], BF16, kind="ExternalInput")
    i_lamg = nc.dram_tensor("i_lamg", [64, T], F32, kind="ExternalInput")
    i_sall = nc.dram_tensor("i_sall", [7, 128, 8192], BF16, kind="ExternalInput")
    i_ltv = nc.dram_tensor("i_ltv", [64, 7], F32, kind="ExternalInput")
    i_g = nc.dram_tensor("i_g", [32, 128, T], F32, kind="ExternalInput")
    i_wout = nc.dram_tensor("i_wout", [32, 128, 8192], F32, kind="ExternalInput")
    i_nw = nc.dram_tensor("i_nw", [128, 64], F32, kind="ExternalInput")
    i_qkb = nc.dram_tensor("i_qkb", [128, 64], F32, kind="ExternalInput")
    i_qvb = nc.dram_tensor("i_qvb", [1, 4096], F32, kind="ExternalInput")
    i_wqk = nc.dram_tensor("i_wqk", [64, 128, 4096], F32, kind="ExternalInput")
    i_wv = nc.dram_tensor("i_wv", [8, 128, 32 * 512], F32, kind="ExternalInput")
    i_ones = nc.dram_tensor("i_ones", [128, 128], F32, kind="ExternalInput")
    i_onescol = nc.dram_tensor("i_onescol", [128, 1], F32, kind="ExternalInput")
    i_oc_bf = nc.dram_tensor("i_oc_bf", [128, 1], BF16, kind="ExternalInput")
    i_o1r_bf = nc.dram_tensor("i_o1r_bf", [1, 128], BF16, kind="ExternalInput")
    i_misc = nc.dram_tensor("i_misc", [1, 4], F32, kind="ExternalInput")

    o_x1 = nc.dram_tensor("o_x1", [32, 128, T], F32, kind="ExternalOutput")
    o_xn2 = nc.dram_tensor("o_xn2", [32, 128, T], BF16, kind="ExternalOutput")
    o_q = nc.dram_tensor("o_q", [32, 128, T], BF16, kind="ExternalOutput")
    o_k = nc.dram_tensor("o_k", [32, 128, T], BF16, kind="ExternalOutput")
    o_v = nc.dram_tensor("o_v", [8, 128, 4096], BF16, kind="ExternalOutput")

    d_row3 = nc.dram_tensor("d_row3", [448], F32, kind="Internal")
    d_row1 = nc.dram_tensor("d_row1", [8192], F32, kind="Internal")
    d_rowM = nc.dram_tensor("d_rowM", [T], F32, kind="Internal")
    d_rowR = nc.dram_tensor("d_rowR", [T], F32, kind="Internal")

    ctx = ExitStack()
    with ctx:
        pr = Prog(nc, ctx)
        sbt = lambda nm, sh, dt: ctx.enter_context(nc.sbuf_tensor(nm, sh, dt))
        pst = lambda nm, sh, dt=F32: ctx.enter_context(nc.psum_tensor(nm, sh, dt))

        bkA = pst("bkA", [128, 512])
        bkB = pst("bkB", [128, 512])
        bkC = pst("bkC", [128, 512])
        bkD = pst("bkD", [128, 512])
        bkE = pst("bkE", [128, 512])
        bkS1 = pst("bkS1", [1, 512])
        bkS2 = pst("bkS2", [1, 512])

        ones1r = sbt("ones1r", [1, 128], F32)
        onescol = sbt("onescol", [128, 1], F32)
        oc_bf = sbt("oc_bf", [128, 1], BF16)
        o1r_bf = sbt("o1r_bf", [1, 128], BF16)
        epsc = sbt("epsc", [1, 1], F32)
        nwc = sbt("nwc", [128, 64], F32)
        qkbc = sbt("qkbc", [128, 64], F32)
        Ct = sbt("Ct", [128, T], BF16)
        lamg = sbt("lamg", [64, T], F32)
        hinb = sbt("hinb", [128, 8192], BF16)

        c_ld = pr.ctr("c_ld")
        c_gld = pr.ctr("c_gld")
        c_st = pr.ctr("c_st")
        c_pe = pr.ctr("c_pe")
        c_dve = pr.ctr("c_dve")
        c_act = pr.ctr("c_act")
        QMAP = {"sync": [c_ld], "gpsimd": [c_gld, c_st]}

        for o, i in ((ones1r[:], i_ones[0:1, :]), (onescol[:], i_onescol[:]),
                     (oc_bf[:], i_oc_bf[:]), (o1r_bf[:], i_o1r_bf[:]),
                     (epsc[:], i_misc[0:1, 0:1]), (nwc[:], i_nw[:]),
                     (qkbc[:], i_qkb[:]), (Ct[:], i_ct[:]), (lamg[:], i_lamg[:])):
            pr.run("sync", lambda e, oo=o, ii=i: e.dma_start(out=oo, in_=ii), inc=c_ld, k=16)

        with ExitStack() as b0:
            acc = b0.enter_context(nc.sbuf_tensor("acc", [128, 8192], F32))
            sjb = b0.enter_context(nc.sbuf_tensor("sjb", [128, 8192], BF16))
            ltv = b0.enter_context(nc.sbuf_tensor("ltv", [64, 7], F32))
            rr = b0.enter_context(nc.sbuf_tensor("rr", [64, 7], F32))
            coefr = b0.enter_context(nc.sbuf_tensor("coefr", [1, 448], F32))
            coefbc = b0.enter_context(nc.sbuf_tensor("coefbc", [128, 448], F32))
            pr.run("sync", lambda e: e.dma_start(out=ltv[:], in_=i_ltv[:]), inc=c_ld, k=16)
            pr.wait("vector", c_ld)
            pr.run("vector", lambda e: e.memset(rr[:, 6:7], 0.0), inc=c_dve)
            for j in range(5, -1, -1):
                pr.wait("vector", c_dve)
                pr.run("vector", lambda e, j=j: e.tensor_add(rr[:, j:j + 1], rr[:, j + 1:j + 2], ltv[:, j + 1:j + 2]), inc=c_dve)
            pr.wait("scalar", c_dve)
            pr.run("scalar", lambda e: e.activation(rr[:], rr[:], AF.Exp), inc=c_act)
            pr.wait("gpsimd", c_act)
            pr.run("gpsimd", lambda e: e.dma_start(out=d_row3[:].rearrange("(h j) -> h j", h=64), in_=rr[:]), inc=c_st, k=16)
            pr.wait("sync", c_st)
            pr.run("sync", lambda e: e.dma_start(out=coefr[:], in_=d_row3[:].unsqueeze(0)), inc=c_ld, k=16)
            pr.wait("tensor", c_ld)
            pr.run("tensor", lambda e: e.matmul(bkE[:, 0:448], ones1r[:], coefr[:], start=True, stop=True), inc=c_pe)
            pr.wait("vector", c_pe)
            pr.run("vector", lambda e: e.tensor_copy(coefbc[:], bkE[:, 0:448]), inc=c_dve)
            pr.run("vector", lambda e: e.memset(acc[:], 0.0), inc=c_dve)
            pr.wait("vector", c_dve)
            for j in range(7):
                pr.wait("sync", c_dve, pr.n(c_dve) if j else 0)
                pr.run("sync", lambda e, j=j: e.dma_start(out=sjb[:], in_=i_sall[j]), inc=c_ld, k=16)
                pr.wait("vector", c_ld)
                for h in range(64):
                    pr.run("vector", lambda e, j=j, h=h: e.scalar_tensor_tensor(
                        acc[:, h * 128:(h + 1) * 128], sjb[:, h * 128:(h + 1) * 128],
                        coefbc[:, h * 7 + j:h * 7 + j + 1], acc[:, h * 128:(h + 1) * 128],
                        ALU.mult, ALU.add), inc=(c_dve if h == 63 else None))
                pr.wait("vector", c_dve)
            pr.run("vector", lambda e: e.tensor_copy(hinb[:], acc[:]), inc=c_dve)

        pr.barrier(QMAP)

        with ExitStack() as b1:
            ylc = b1.enter_context(nc.sbuf_tensor("ylc", [128, 8192], BF16))
            szc = b1.enter_context(nc.sbuf_tensor("szc", [128, 8192], BF16))
            cwg = b1.enter_context(nc.sbuf_tensor("cwg", [128, 8192], BF16))
            ybuf = b1.enter_context(nc.sbuf_tensor("ybuf", [128, 8192], BF16))
            ynspan = b1.enter_context(nc.sbuf_tensor("ynspan", [128, 64 * 512], BF16))
            elgfm = b1.enter_context(nc.sbuf_tensor("elgfm", [64, 128], F32))
            elrp = [b1.enter_context(nc.sbuf_tensor(f"elrp{i}", [1, 512], F32)) for i in range(2)]
            tmp32 = b1.enter_context(nc.sbuf_tensor("tmp32", [128, 128], F32))
            ysq = [b1.enter_context(nc.sbuf_tensor(f"ysq{i}", [128, 128], BF16)) for i in range(2)]
            vrowp = b1.enter_context(nc.sbuf_tensor("vrowp", [1, 128], F32))
            rstd = b1.enter_context(nc.sbuf_tensor("rstd", [1, 128], F32))
            rstdbc = b1.enter_context(nc.sbuf_tensor("rstdbc", [128, 128], F32))
            wob = [b1.enter_context(nc.sbuf_tensor(f"wob{i}", [128, 8192], BF16)) for i in range(2)]
            _gsl = b1.enter_context(nc.sbuf_tensor("gsl", [128, 512], F32))
            _xsl = b1.enter_context(nc.sbuf_tensor("xsl", [128, 512], F32))
            gsl = [_gsl, _gsl]
            xsl = [_xsl, _xsl]
            x1b = [b1.enter_context(nc.sbuf_tensor(f"x1b{i}", [128, 512], F32)) for i in range(2)]
            x1sq = [b1.enter_context(nc.sbuf_tensor(f"x1sq{i}", [128, 512], BF16)) for i in range(2)]
            m2loc = b1.enter_context(nc.sbuf_tensor("m2loc", [1, 512], F32))
            trow = b1.enter_context(nc.sbuf_tensor("trow", [1, 512], F32))
            urow = b1.enter_context(nc.sbuf_tensor("urow", [1, 512], F32))

            for c in range(NCH):
                pr.wait("sync", c_dve)
                pr.run("sync", lambda e, c=c: e.dma_start(out=ylc[:], in_=i_ylc[c]), inc=c_ld, k=16)
                for m in range(64):
                    eng, cc = ("sync", c_ld) if m % 2 == 0 else ("gpsimd", c_gld)
                    pr.wait(eng, c_dve)
                    pr.run(eng, lambda e, m=m, c=c: e.dma_start(
                        out=szc[:, m * 128:(m + 1) * 128], in_=i_siluz[m, :, c * 128:(c + 1) * 128]), inc=cc, k=16)
                ld_mark = (pr.n(c_ld), pr.n(c_gld))
                pr.wait("scalar", c_dve)
                pr.run("scalar", lambda e, c=c: e.activation(elgfm[:], lamg[:, c * 128:(c + 1) * 128], AF.Exp), inc=c_act)
                pr.wait("gpsimd", c_act)
                pr.run("gpsimd", lambda e: e.dma_start(out=d_row1[:].rearrange("(h t) -> h t", h=64), in_=elgfm[:]), inc=c_st, k=16)
                for p in range(16):
                    lrp = elrp[p % 2]
                    pr.wait("sync", c_pe)
                    pr.wait("sync", c_st)
                    pr.run("sync", lambda e, p=p, lrp=lrp: e.dma_start(
                        out=lrp[:], in_=d_row1[p * 512:(p + 1) * 512].unsqueeze(0)), inc=c_ld, k=16)
                    pb = bkE
                    pr.wait("tensor", c_ld)
                    pr.wait("tensor", c_dve)
                    pr.run("tensor", lambda e, lrp=lrp, pb=pb: e.matmul(
                        pb[:], ones1r[:], lrp[:], start=True, stop=True), inc=c_pe)
                    pr.wait("vector", c_pe)
                    pr.run("vector", lambda e, p=p, c=c, pb=pb: e.tensor_tensor(
                        cwg[:, p * 512:(p + 1) * 512].rearrange("q (a b) -> q a b", a=4),
                        pb[:].rearrange("q (a b) -> q a b", a=4),
                        Ct[:, c * 128:(c + 1) * 128].unsqueeze(1).broadcast_to([128, 4, 128]),
                        ALU.mult), inc=c_dve)
                pr.wait("tensor", c_ld, ld_mark[0])
                pr.wait("tensor", c_gld, ld_mark[1])
                pr.wait("tensor", c_dve)
                pe_h, dve_h, act_h = {}, {}, {}
                for h in range(64):
                    pyb = (bkA, bkB)[h % 2]
                    pr.run("tensor", lambda e, h=h, pyb=pyb: e.matmul(
                        pyb[:, 0:128], hinb[:, h * 128:(h + 1) * 128], cwg[:, h * 128:(h + 1) * 128],
                        start=True, stop=True), inc=c_pe)
                    pe_h[h] = pr.n(c_pe)
                    pr.wait("vector", c_pe, pe_h[h])
                    pr.run("vector", lambda e, h=h, pyb=pyb: e.tensor_add(
                        tmp32[:], ylc[:, h * 128:(h + 1) * 128], pyb[:, 0:128]), inc=c_dve)
                    pr.wait("vector", c_dve)
                    pr.run("vector", lambda e, h=h: e.tensor_mul(
                        ybuf[:, h * 128:(h + 1) * 128], tmp32[:], szc[:, h * 128:(h + 1) * 128]), inc=c_dve)
                    dve_h[h] = pr.n(c_dve)
                    pr.wait("scalar", c_dve, dve_h[h])
                    pr.wait("scalar", c_pe, act_h.get(h - 2, 0))
                    pr.run("scalar", lambda e, h=h: e.activation(
                        ysq[h % 2][:], ybuf[:, h * 128:(h + 1) * 128], AF.Square), inc=c_act)
                    pr.wait("tensor", c_act)
                    vv = pr.run("tensor", lambda e, h=h: e.matmul(
                        bkS1[0:1, 0:128], oc_bf[:], ysq[h % 2][:],
                        start=(h == 0), stop=(h == 63)), inc=c_pe)
                    act_h[h] = vv
                    if h >= 1:
                        pr.wait("tensor", c_dve, dve_h[h - 1])
                pr.wait("scalar", c_pe)
                pr.run("scalar", lambda e: e.activation(vrowp[:], bkS1[0:1, 0:128], AF.Ln, bias=epsc[:], scale=1.0 / 8192.0), inc=c_act)
                pr.wait("vector", c_act)
                pr.run("vector", lambda e: e.tensor_scalar_mul(vrowp[:], vrowp[:], -0.5), inc=c_dve)
                pr.wait("scalar", c_dve)
                pr.run("scalar", lambda e: e.activation(rstd[:], vrowp[:], AF.Exp), inc=c_act)
                pr.wait("tensor", c_act)
                pr.run("tensor", lambda e: e.matmul(bkE[:, 0:128], ones1r[:], rstd[:], start=True, stop=True), inc=c_pe)
                pr.wait("vector", c_pe)
                pr.run("vector", lambda e: e.tensor_copy(rstdbc[:], bkE[:, 0:128]), inc=c_dve)
                pr.wait("vector", c_dve)
                for h in range(64):
                    pr.run("vector", lambda e, h=h, c=c: e.scalar_tensor_tensor(
                        ynspan[:, h * 512 + (c % 4) * 128:h * 512 + (c % 4) * 128 + 128],
                        ybuf[:, h * 128:(h + 1) * 128], nwc[:, h:h + 1], rstdbc[:],
                        ALU.mult, ALU.mult), inc=(c_dve if h == 63 else None))

                if c % 4 == 3:
                    s = c // 4
                    wpe, wev, wld = {}, {}, {}
                    pr.wait("tensor", c_dve)
                    pr.wait("tensor", c_act)
                    for Mc in range(32):
                        wb = wob[Mc % 2]
                        pr.wait("gpsimd", c_pe, wpe.get(Mc - 2, 0))
                        gvv = pr.run("gpsimd", lambda e, Mc=Mc, wb=wb: e.dma_start(out=wb[:], in_=i_wout[Mc]), inc=c_gld, k=16)
                        pr.wait("sync", c_dve, wev.get(Mc - 1, 0))
                        pr.run("sync", lambda e, Mc=Mc, s=s: e.dma_start(
                            out=gsl[Mc % 2][:], in_=i_g[Mc, :, s * 512:(s + 1) * 512]), inc=c_ld, k=16)
                        wld[Mc] = pr.run("sync", lambda e, Mc=Mc, s=s: e.dma_start(
                            out=xsl[Mc % 2][:], in_=i_xtb[Mc, :, s * 512:(s + 1) * 512]), inc=c_ld, k=16)
                        pw = (bkC, bkD)[Mc % 2]
                        pr.wait("tensor", c_gld, gvv)
                        if Mc >= 2:
                            pr.wait("tensor", c_dve, wev[Mc - 2])
                        for kc in range(64):
                            pr.run("tensor", lambda e, kc=kc, wb=wb, pw=pw: e.matmul(
                                pw[:], wb[:, kc * 128:(kc + 1) * 128], ynspan[:, kc * 512:(kc + 1) * 512],
                                start=(kc == 0), stop=(kc == 63)), inc=(None if kc < 63 else c_pe))
                        pev = pr.n(c_pe)
                        wpe[Mc] = pev
                        pr.wait("vector", c_pe, pev)
                        pr.wait("vector", c_ld, wld[Mc])
                        pr.run("vector", lambda e, Mc=Mc, pw=pw: e.tensor_mul(x1b[Mc % 2][:], pw[:], gsl[Mc % 2][:]), inc=c_dve)
                        pr.wait("vector", c_dve)
                        pr.run("vector", lambda e, Mc=Mc: e.tensor_add(x1b[Mc % 2][:], x1b[Mc % 2][:], xsl[Mc % 2][:]), inc=c_dve)
                        x1v = pr.n(c_dve)
                        wev[Mc] = x1v
                        pr.wait("gpsimd", c_dve, x1v)
                        pr.run("gpsimd", lambda e, Mc=Mc, s=s: e.dma_start(
                            out=o_x1[Mc, :, s * 512:(s + 1) * 512], in_=x1b[Mc % 2][:]), inc=c_st, k=16)
                        pr.wait("scalar", c_dve, x1v)
                        pr.wait("scalar", c_pe)
                        pr.run("scalar", lambda e, Mc=Mc: e.activation(x1sq[Mc % 2][:], x1b[Mc % 2][:], AF.Square), inc=c_act)
                        pr.wait("tensor", c_dve, x1v)
                        pr.run("tensor", lambda e, Mc=Mc: e.matmul(
                            bkS1[0:1, 0:512], onescol[:], x1b[Mc % 2][:],
                            start=(Mc == 0), stop=(Mc == 31)), inc=None)
                        pr.wait("tensor", c_act)
                        pr.run("tensor", lambda e, Mc=Mc: e.matmul(
                            bkS2[0:1, 0:512], oc_bf[:], x1sq[Mc % 2][:],
                            start=(Mc == 0), stop=(Mc == 31)), inc=(None if Mc < 31 else c_pe))
                    pr.wait("vector", c_pe)
                    pr.run("vector", lambda e: e.tensor_scalar_mul(m2loc[:], bkS1[0:1, 0:512], 1.0 / HD), inc=c_dve)
                    pr.run("vector", lambda e: e.tensor_scalar_mul(trow[:], bkS2[0:1, 0:512], 1.0 / HD), inc=c_dve)
                    pr.wait("vector", c_dve)
                    pr.run("vector", lambda e: e.tensor_mul(urow[:], m2loc[:], m2loc[:]), inc=c_dve)
                    pr.wait("vector", c_dve)
                    pr.run("vector", lambda e: e.tensor_sub(trow[:], trow[:], urow[:]), inc=c_dve)
                    pr.wait("scalar", c_dve)
                    pr.run("scalar", lambda e: e.activation(trow[:], trow[:], AF.Sqrt, bias=epsc[:]), inc=c_act)
                    pr.wait("vector", c_act)
                    pr.run("vector", lambda e: e.reciprocal(urow[:], trow[:]), inc=c_dve)
                    pr.wait("gpsimd", c_dve)
                    pr.run("gpsimd", lambda e, s=s: e.dma_start(
                        out=d_rowM[s * 512:(s + 1) * 512].unsqueeze(0), in_=m2loc[:]), inc=c_st, k=16)
                    pr.run("gpsimd", lambda e, s=s: e.dma_start(
                        out=d_rowR[s * 512:(s + 1) * 512].unsqueeze(0), in_=urow[:]), inc=c_st, k=16)

        pr.barrier(QMAP)

        with ExitStack() as b2:
            xn2sb = b2.enter_context(nc.sbuf_tensor("xn2sb", [128, 32 * T], BF16))
            m2bc = b2.enter_context(nc.sbuf_tensor("m2bc", [128, T], F32))
            r2bc = b2.enter_context(nc.sbuf_tensor("r2bc", [128, T], F32))
            x1t = [b2.enter_context(nc.sbuf_tensor(f"x1t{i}", [128, T], F32)) for i in range(2)]
            t2b = b2.enter_context(nc.sbuf_tensor("t2b", [128, T], F32))
            qkblk = [b2.enter_context(nc.sbuf_tensor(f"qkblk{i}", [128, 4096], BF16)) for i in range(2)]
            qbuf = [b2.enter_context(nc.sbuf_tensor(f"qbuf{i}", [128, 512], BF16)) for i in range(2)]
            vwcol = b2.enter_context(nc.sbuf_tensor("vwcol", [128, 32 * 512], BF16))
            vbr = b2.enter_context(nc.sbuf_tensor("vbr", [1, 4096], BF16))
            vbuf = [b2.enter_context(nc.sbuf_tensor(f"vbuf{i}", [128, 512], BF16)) for i in range(2)]
            m2row = b2.enter_context(nc.sbuf_tensor("m2row", [1, T], F32))
            r2row = b2.enter_context(nc.sbuf_tensor("r2row", [1, T], F32))
            pr.run("sync", lambda e: e.dma_start(out=m2row[:], in_=d_rowM[:].unsqueeze(0)), inc=c_ld, k=16)
            pr.run("sync", lambda e: e.dma_start(out=r2row[:], in_=d_rowR[:].unsqueeze(0)), inc=c_ld, k=16)
            pr.wait("tensor", c_ld)

            for half in range(2):
                pr.wait("tensor", c_dve)
                pr.run("tensor", lambda e, half=half: e.matmul(
                    bkE[:], ones1r[:], m2row[0:1, half * 512:(half + 1) * 512], start=True, stop=True), inc=c_pe)
                pr.wait("vector", c_pe)
                pr.run("vector", lambda e, half=half: e.tensor_copy(m2bc[:, half * 512:(half + 1) * 512], bkE[:]), inc=c_dve)
                pr.wait("tensor", c_dve)
                pr.run("tensor", lambda e, half=half: e.matmul(
                    bkE[:], ones1r[:], r2row[0:1, half * 512:(half + 1) * 512], start=True, stop=True), inc=c_pe)
                pr.wait("vector", c_pe)
                pr.run("vector", lambda e, half=half: e.tensor_copy(r2bc[:, half * 512:(half + 1) * 512], bkE[:]), inc=c_dve)
            pr.wait("vector", c_dve)
            for kc in range(32):
                pr.wait("sync", c_dve)
                pr.run("sync", lambda e, kc=kc: e.dma_start(out=x1t[kc % 2][:], in_=o_x1[kc]), inc=c_ld, k=16)
                pr.wait("vector", c_ld)
                pr.run("vector", lambda e, kc=kc: e.tensor_sub(t2b[:], x1t[kc % 2][:], m2bc[:]), inc=c_dve)
                pr.wait("vector", c_dve)
                pr.run("vector", lambda e, kc=kc: e.tensor_mul(xn2sb[:, kc * T:(kc + 1) * T], t2b[:], r2bc[:]), inc=c_dve)
                pr.wait("gpsimd", c_dve)
                pr.run("gpsimd", lambda e, kc=kc: e.dma_start(out=o_xn2[kc], in_=xn2sb[:, kc * T:(kc + 1) * T]), inc=c_st, k=16)
            qpe, qev = {}, {}
            for Mc in range(64):
                qk = qkblk[Mc % 2]
                pr.wait("gpsimd", c_pe, qpe.get(Mc - 2, 0))
                gvv = pr.run("gpsimd", lambda e, Mc=Mc, qk=qk: e.dma_start(out=qk[:], in_=i_wqk[Mc]), inc=c_gld, k=16)
                pr.wait("tensor", c_gld, gvv)
                for half in range(2):
                    pq = (bkC, bkD)[half]
                    pr.wait("tensor", c_act, qev.get((Mc - 1, half), 0))
                    for kc in range(32):
                        pr.run("tensor", lambda e, kc=kc, qk=qk, pq=pq, half=half: e.matmul(
                            pq[:], qk[:, kc * 128:(kc + 1) * 128],
                            xn2sb[:, kc * T + half * 512:kc * T + half * 512 + 512],
                            start=(kc == 0), stop=(kc == 31)), inc=(None if kc < 31 else c_pe))
                    pev = pr.n(c_pe)
                    qpe[Mc] = pev
                    qb = qbuf[half]
                    pr.wait("vector", c_pe, pev)
                    pr.wait("vector", c_st, qev.get(("st", Mc - 1, half), 0))
                    qev[(Mc, half)] = pr.run("vector", lambda e, Mc=Mc, qb=qb, pq=pq: e.tensor_scalar_add(
                        qb[:], pq[:], qkbc[:, Mc:Mc + 1]), inc=c_act)
                    pr.wait("gpsimd", c_act, qev[(Mc, half)])
                    if Mc < 32:
                        qev[("st", Mc, half)] = pr.run("gpsimd", lambda e, Mc=Mc, half=half, qb=qb: e.dma_start(
                            out=o_q[Mc, :, half * 512:(half + 1) * 512], in_=qb[:]), inc=c_st, k=16)
                    else:
                        qev[("st", Mc, half)] = pr.run("gpsimd", lambda e, Mc=Mc, half=half, qb=qb: e.dma_start(
                            out=o_k[Mc - 32, :, half * 512:(half + 1) * 512], in_=qb[:]), inc=c_st, k=16)
            pr.run("gpsimd", lambda e: e.dma_start(out=vbr[:], in_=i_qvb[:]), inc=c_gld, k=16)
            for vc in range(8):
                pr.wait("gpsimd", c_pe)
                pr.run("gpsimd", lambda e, vc=vc: e.dma_start(out=vwcol[:], in_=i_wv[vc]), inc=c_gld, k=16)
                pr.wait("tensor", c_gld)
                for tc in range(8):
                    pv = (bkC, bkD)[tc % 2]
                    pr.wait("tensor", c_act)
                    for kc in range(32):
                        pr.run("tensor", lambda e, kc=kc, tc=tc, pv=pv: e.matmul(
                            pv[:], xn2sb[:, kc * T + tc * 128:kc * T + tc * 128 + 128],
                            vwcol[:, kc * 512:(kc + 1) * 512],
                            start=(kc == 0), stop=False), inc=None)
                    pr.run("tensor", lambda e, vc=vc, pv=pv: e.matmul(
                        pv[:], o1r_bf[:], vbr[0:1, vc * 512:(vc + 1) * 512],
                        start=False, stop=True), inc=c_pe)
                    pev = pr.n(c_pe)
                    vb = vbuf[tc % 2]
                    pr.wait("scalar", c_pe, pev)
                    pr.wait("scalar", c_st)
                    pr.run("scalar", lambda e, vb=vb, pv=pv: e.activation(vb[:], pv[:], AF.Copy), inc=c_act)
                    pr.wait("gpsimd", c_act)
                    pr.run("gpsimd", lambda e, tc=tc, vc=vc, vb=vb: e.dma_start(
                        out=o_v[tc, :, vc * 512:(vc + 1) * 512], in_=vb[:]), inc=c_st, k=16)
            pr.wait("gpsimd", c_st)
            pr.wait("sync", c_ld)
        pr.replay()
    return nc


def stage_B(inp, a_outs):
    xT = np.ascontiguousarray(np.asarray(inp["L0_attn_hidden_states"], np.float32).reshape(S, HD).T)
    wout_b = _blocked(np.asarray(inp["L0_attn_W_out"], np.float32))
    qkvw = np.asarray(inp["L1_attn_attn_qkvw"], np.float32)
    wqk_b = _blocked(qkvw[:, 0:8192])
    wv = np.ascontiguousarray(
        qkvw[:, 8192:12288].reshape(32, 128, 8, 512).transpose(2, 1, 0, 3).reshape(8, 128, 32 * 512))
    qkvb = np.asarray(inp["L1_attn_attn_qkvb"], np.float32)
    qkb = np.ascontiguousarray(qkvb[0:8192].reshape(64, 128).T)
    qvb = np.ascontiguousarray(qkvb[8192:12288].reshape(1, 4096))
    nw = np.ascontiguousarray(np.asarray(inp["L0_attn_norm_weight"], np.float32).reshape(64, 128).T)
    ones = np.ones((128, 128), np.float32)
    shared = dict(i_wout=wout_b, i_nw=nw, i_qkb=qkb, i_qvb=qvb, i_wqk=wqk_b, i_wv=wv,
                  i_ones=ones, i_onescol=np.ones((128, 1), np.float32),
                  i_oc_bf=np.ones((128, 1), BF), i_o1r_bf=np.ones((1, 128), BF),
                  i_misc=np.array([[EPS, 1.0, 0.0, 0.0]], np.float32))
    per_core = []
    ltots = [np.asarray(a_outs[c]["o_lamg"], np.float32)[:, T - 1] for c in range(NCORES)]
    for c in range(NCORES):
        sall = np.zeros((7, 128, 8192), BF)
        ltv = np.zeros((64, 7), np.float32)
        for j in range(min(c, 7)):
            sall[j] = np.asarray(a_outs[j]["o_scor"])
        for m in range(1, 7):
            if m <= c - 1:
                ltv[:, m] = ltots[m]
        per_core.append(dict(
            i_xtb=np.ascontiguousarray(xT[:, c * T:(c + 1) * T].reshape(32, 128, T)),
            i_ylc=np.asarray(a_outs[c]["o_ylc"]),
            i_siluz=np.asarray(a_outs[c]["o_siluz"]),
            i_ct=np.asarray(a_outs[c]["o_xbcc"])[65],
            i_lamg=np.asarray(a_outs[c]["o_lamg"], np.float32),
            i_sall=sall, i_ltv=ltv,
            i_g=np.asarray(a_outs[c]["o_g"], np.float32)))
    return shared, per_core


def build_C():
    nc = bass.Bass("TRN2", target_bir_lowering=False, debug=False, detect_race_conditions=False)
    i_q = nc.dram_tensor("i_q", [32, 128, T], BF16, kind="ExternalInput")
    i_kf = nc.dram_tensor("i_kf", [32, 128, 8192], BF16, kind="ExternalInput")
    i_vf = nc.dram_tensor("i_vf", [32, 128, 8192], BF16, kind="ExternalInput")
    i_x1 = nc.dram_tensor("i_x1", [32, 128, T], F32, kind="ExternalInput")
    i_xn2 = nc.dram_tensor("i_xn2", [32, 128, T], BF16, kind="ExternalInput")
    i_wo = nc.dram_tensor("i_wo", [32, 128, 4096], F32, kind="ExternalInput")
    i_w1t = nc.dram_tensor("i_w1t", [32, 128, 4096], F32, kind="ExternalInput")
    i_ob = nc.dram_tensor("i_ob", [128, 32], F32, kind="ExternalInput")
    i_wsc = nc.dram_tensor("i_wsc", [128, 32], F32, kind="ExternalInput")
    i_wbc = nc.dram_tensor("i_wbc", [128, 32], F32, kind="ExternalInput")
    i_ones = nc.dram_tensor("i_ones", [128, 128], F32, kind="ExternalInput")
    i_oc_bf = nc.dram_tensor("i_oc_bf", [128, 1], BF16, kind="ExternalInput")
    i_smsh = nc.dram_tensor("i_smsh", [128, 1], F32, kind="ExternalInput")

    o_out = nc.dram_tensor("o_out", [32, 128, T], F32, kind="ExternalOutput")

    ctx = ExitStack()
    with ctx:
        pr = Prog(nc, ctx)
        c_ld = pr.ctr("c_ld")
        c_gld = pr.ctr("c_gld")
        c_st = pr.ctr("c_st")
        c_pe = pr.ctr("c_pe")
        c_dve = pr.ctr("c_dve")
        c_act = pr.ctr("c_act")
        QMAP = {"sync": [c_ld], "gpsimd": [c_gld, c_st]}
        sbt = lambda nm, sh, dt: ctx.enter_context(nc.sbuf_tensor(nm, sh, dt))

        ones1r = sbt("ones1r", [1, 128], F32)
        oc_bf = sbt("oc_bf", [128, 1], BF16)
        smsh = sbt("smsh", [128, 1], F32)
        obc = sbt("obc", [128, 32], F32)
        wscc = sbt("wscc", [128, 32], F32)
        wbcc = sbt("wbcc", [128, 32], F32)
        ctxb = sbt("ctxb", [128, 32 * T], BF16)

        for o, i in ((ones1r[:], i_ones[0:1, :]), (oc_bf[:], i_oc_bf[:]), (smsh[:], i_smsh[:]),
                     (obc[:], i_ob[:]), (wscc[:], i_wsc[:]), (wbcc[:], i_wbc[:])):
            pr.run("sync", lambda e, oo=o, ii=i: e.dma_start(out=oo, in_=ii), inc=c_ld, k=16)

        with ExitStack() as ca:
            pSc = [ca.enter_context(nc.psum_tensor(f"pSc{i}", [128, 1024], F32)) for i in range(2)]
            pCx = ca.enter_context(nc.psum_tensor("pCx", [128, 1024], F32))
            pZ = ca.enter_context(nc.psum_tensor("pZ", [1, 1024], F32))
            khb = [ca.enter_context(nc.sbuf_tensor(f"khb{i}", [128, 8192], BF16)) for i in range(2)]
            vhb = [ca.enter_context(nc.sbuf_tensor(f"vhb{i}", [128, 8192], BF16)) for i in range(2)]
            qhb = [ca.enter_context(nc.sbuf_tensor(f"qhb{i}", [128, T], BF16)) for i in range(2)]
            probs = [ca.enter_context(nc.sbuf_tensor(f"probs{i}", [128, 1024], BF16)) for i in range(2)]
            zr = ca.enter_context(nc.sbuf_tensor("zr", [1, 1024], F32))
            zbc = ca.enter_context(nc.sbuf_tensor("zbc", [128, 1024], F32))

            def load_head(h):
                b = h % 2
                pr.wait("sync", c_pe)
                pr.wait("gpsimd", c_pe)
                pr.run("sync", lambda e, h=h, b=b: e.dma_start(out=khb[b][:], in_=i_kf[h]), inc=c_ld, k=16)
                pr.run("gpsimd", lambda e, h=h, b=b: e.dma_start(out=vhb[b][:], in_=i_vf[h]), inc=c_gld, k=16)
                pr.run("sync", lambda e, h=h, b=b: e.dma_start(out=qhb[b][:], in_=i_q[h]), inc=c_ld, k=16)

            load_head(0)
            for h in range(32):
                b = h % 2
                marks = (pr.n(c_ld), pr.n(c_gld))
                if h + 1 < 32:
                    load_head(h + 1)
                pr.wait("tensor", c_ld, marks[0])
                pr.wait("tensor", c_gld, marks[1])
                act_k, pe_k, sc_k = {}, {}, {}

                def emit_scores(kc):
                    sc = pSc[kc % 2]
                    pr.wait("tensor", c_act, act_k.get(kc - 2, 0))
                    for half in range(2):
                        pr.run("tensor", lambda e, kc=kc, b=b, sc=sc, half=half: e.matmul(
                            sc[:, half * 512:(half + 1) * 512],
                            khb[b][:, kc * 128:(kc + 1) * 128],
                            qhb[b][:, half * 512:(half + 1) * 512],
                            start=True, stop=True), inc=(None if half == 0 else c_pe))
                    sc_k[kc] = pr.n(c_pe)

                emit_scores(0)
                for kc in range(64):
                    sc = pSc[kc % 2]
                    if kc + 1 < 64:
                        emit_scores(kc + 1)
                    pr.wait("scalar", c_pe, sc_k[kc])
                    pr.wait("scalar", c_pe, pe_k.get(kc - 2, 0))
                    for half in range(2):
                        pr.run("scalar", lambda e, kc=kc, sc=sc, half=half: e.activation(
                            probs[kc % 2][:, half * 512:(half + 1) * 512],
                            sc[:, half * 512:(half + 1) * 512],
                            AF.Exp, bias=smsh[:], scale=SM_SCALE), inc=(None if half == 0 else c_act))
                    act_k[kc] = pr.n(c_act)
                    pr.wait("tensor", c_act, act_k[kc])
                    for half in range(2):
                        pr.run("tensor", lambda e, kc=kc, b=b, half=half: e.matmul(
                            pCx[:, half * 512:(half + 1) * 512],
                            vhb[b][:, kc * 128:(kc + 1) * 128],
                            probs[kc % 2][:, half * 512:(half + 1) * 512],
                            start=(kc == 0), stop=(kc == 63)), inc=None)
                        pr.run("tensor", lambda e, kc=kc, half=half: e.matmul(
                            pZ[0:1, half * 512:(half + 1) * 512], oc_bf[:],
                            probs[kc % 2][:, half * 512:(half + 1) * 512],
                            start=(kc == 0), stop=(kc == 63)), inc=(None if half == 0 else c_pe))
                    pe_k[kc] = pr.n(c_pe)
                pr.wait("vector", c_pe)
                pr.run("vector", lambda e: e.reciprocal(zr[:], pZ[0:1, :]), inc=c_dve)
                pr.wait("tensor", c_dve)
                for half in range(2):
                    pr.run("tensor", lambda e, half=half: e.matmul(
                        pSc[0][:, half * 512:(half + 1) * 512], ones1r[:],
                        zr[0:1, half * 512:(half + 1) * 512], start=True, stop=True),
                        inc=(None if half == 0 else c_pe))
                pr.wait("vector", c_pe)
                pr.run("vector", lambda e: e.tensor_copy(zbc[:], pSc[0][:]), inc=c_dve)
                pr.wait("vector", c_dve)
                pr.run("vector", lambda e, h=h: e.tensor_mul(
                    ctxb[:, h * T:(h + 1) * T], pCx[:], zbc[:]), inc=c_dve)
                pr.wait("tensor", c_dve)

        pr.barrier(QMAP)

        with ExitStack() as cb:
            pA = cb.enter_context(nc.psum_tensor("pA", [128, 512], F32))
            pB = cb.enter_context(nc.psum_tensor("pB", [128, 512], F32))
            pX = cb.enter_context(nc.psum_tensor("pX", [1, 1024], F32))
            pBc = cb.enter_context(nc.psum_tensor("pBc", [128, 1024], F32))
            xn2sb = cb.enter_context(nc.sbuf_tensor("xn2sb", [128, 32 * T], BF16))
            woblk = [cb.enter_context(nc.sbuf_tensor(f"woblk{i}", [128, 4096], BF16)) for i in range(2)]
            w1blk = [cb.enter_context(nc.sbuf_tensor(f"w1blk{i}", [128, 4096], BF16)) for i in range(2)]
            sgnb = cb.enter_context(nc.sbuf_tensor("sgnb", [128, 4096], BF16))
            xsumr = cb.enter_context(nc.sbuf_tensor("xsumr", [1, 1024], F32))
            xsbc = cb.enter_context(nc.sbuf_tensor("xsbc", [128, 1024], F32))
            x1sl = [cb.enter_context(nc.sbuf_tensor(f"x1sl{i}", [128, 512], F32)) for i in range(2)]
            g1b = cb.enter_context(nc.sbuf_tensor("g1b", [128, 512], F32))
            t1c = cb.enter_context(nc.sbuf_tensor("t1c", [128, 512], F32))
            t2c = cb.enter_context(nc.sbuf_tensor("t2c", [128, 512], F32))
            outb = [cb.enter_context(nc.sbuf_tensor(f"outb{i}", [128, 512], F32)) for i in range(2)]

            for kc in range(32):
                pr.run("sync", lambda e, kc=kc: e.dma_start(
                    out=xn2sb[:, kc * T:(kc + 1) * T], in_=i_xn2[kc]), inc=c_ld, k=16)
            pr.wait("tensor", c_ld)
            for kc in range(32):
                for half in range(2):
                    pr.run("tensor", lambda e, kc=kc, half=half: e.matmul(
                        pX[0:1, half * 512:(half + 1) * 512], oc_bf[:],
                        xn2sb[:, kc * T + half * 512:kc * T + half * 512 + 512],
                        start=(kc == 0), stop=(kc == 31)), inc=(None if (kc < 31 or half == 0) else c_pe))
            pr.wait("vector", c_pe)
            pr.run("vector", lambda e: e.tensor_copy(xsumr[:], pX[0:1, :]), inc=c_dve)
            pr.wait("tensor", c_dve)
            for half in range(2):
                pr.run("tensor", lambda e, half=half: e.matmul(
                    pBc[:, half * 512:(half + 1) * 512], ones1r[:],
                    xsumr[0:1, half * 512:(half + 1) * 512], start=True, stop=True),
                    inc=(None if half == 0 else c_pe))
            pr.wait("vector", c_pe)
            pr.run("vector", lambda e: e.tensor_copy(xsbc[:], pBc[:]), inc=c_dve)
            pr.wait("vector", c_dve)

            for th in range(2):
                for nc_ in range(32):
                    wo = woblk[nc_ % 2]
                    w1 = w1blk[nc_ % 2]
                    pr.wait("gpsimd", c_pe)
                    pr.run("gpsimd", lambda e, nc_=nc_, wo=wo: e.dma_start(out=wo[:], in_=i_wo[nc_]), inc=c_gld, k=16)
                    pr.run("gpsimd", lambda e, nc_=nc_, w1=w1: e.dma_start(out=w1[:], in_=i_w1t[nc_]), inc=c_gld, k=16)
                    pr.wait("sync", c_dve)
                    pr.run("sync", lambda e, nc_=nc_, th=th: e.dma_start(
                        out=x1sl[nc_ % 2][:], in_=i_x1[nc_, :, th * 512:(th + 1) * 512]), inc=c_ld, k=16)
                    pr.wait("scalar", c_gld)
                    pr.wait("scalar", c_pe)
                    pr.run("scalar", lambda e, w1=w1: e.activation(sgnb[:], w1[:], AF.Sign), inc=c_act)
                    pr.wait("tensor", c_gld)
                    pr.wait("tensor", c_dve)
                    for kc in range(32):
                        pr.run("tensor", lambda e, kc=kc, wo=wo, th=th: e.matmul(
                            pA[:], wo[:, kc * 128:(kc + 1) * 128],
                            ctxb[:, kc * T + th * 512:kc * T + th * 512 + 512],
                            start=(kc == 0), stop=(kc == 31)), inc=(None if kc < 31 else c_pe))
                    pr.wait("tensor", c_act)
                    for kc in range(32):
                        pr.run("tensor", lambda e, kc=kc, th=th: e.matmul(
                            pB[:], sgnb[:, kc * 128:(kc + 1) * 128],
                            xn2sb[:, kc * T + th * 512:kc * T + th * 512 + 512],
                            start=(kc == 0), stop=(kc == 31)), inc=(None if kc < 31 else c_pe))
                    pev = pr.n(c_pe)
                    pr.wait("vector", c_pe, pev)
                    pr.wait("vector", c_ld)
                    pr.run("vector", lambda e, nc_=nc_: e.tensor_scalar_mul(t1c[:], pB[:], wscc[:, nc_:nc_ + 1]), inc=c_dve)
                    pr.wait("vector", c_dve)
                    pr.run("vector", lambda e, nc_=nc_, th=th: e.scalar_tensor_tensor(
                        t2c[:], xsbc[:, th * 512:(th + 1) * 512], wbcc[:, nc_:nc_ + 1], t1c[:],
                        ALU.mult, ALU.add), inc=c_dve)
                    pr.wait("scalar", c_dve)
                    pr.run("scalar", lambda e: e.activation(g1b[:], t2c[:], AF.Silu), inc=c_act)
                    pr.wait("vector", c_act)
                    pr.run("vector", lambda e, nc_=nc_: e.tensor_scalar_add(t1c[:], pA[:], obc[:, nc_:nc_ + 1]), inc=c_dve)
                    pr.wait("vector", c_dve)
                    pr.run("vector", lambda e: e.tensor_mul(t2c[:], t1c[:], g1b[:]), inc=c_dve)
                    pr.wait("vector", c_dve)
                    pr.run("vector", lambda e, nc_=nc_: e.tensor_add(outb[nc_ % 2][:], t2c[:], x1sl[nc_ % 2][:]), inc=c_dve)
                    pr.wait("gpsimd", c_dve)
                    pr.run("gpsimd", lambda e, nc_=nc_, th=th: e.dma_start(
                        out=o_out[nc_, :, th * 512:(th + 1) * 512], in_=outb[nc_ % 2][:]), inc=c_st, k=16)
            pr.wait("gpsimd", c_st)
            pr.wait("sync", c_ld)
        pr.replay()
    return nc


def stage_C(inp, b_outs):
    wo_b = _blocked(np.asarray(inp["L1_attn_attn_ow"], np.float32))
    w1t = np.ascontiguousarray(np.asarray(inp["L1_ffn_weight"], np.float32).T)
    w1t_b = _blocked(w1t)
    ob = np.ascontiguousarray(np.asarray(inp["L1_attn_attn_ob"], np.float32).reshape(32, 128).T)
    wsc = np.ascontiguousarray(np.asarray(inp["L1_ffn_wscale"], np.float32).reshape(32, 128).T)
    wbc = np.ascontiguousarray(np.asarray(inp["L1_ffn_wbias"], np.float32).reshape(32, 128).T)
    kf = np.concatenate([np.asarray(b_outs[c]["o_k"]) for c in range(NCORES)], axis=2)
    vg = np.concatenate([np.asarray(b_outs[c]["o_v"]) for c in range(NCORES)], axis=0)
    vf = np.ascontiguousarray(vg.reshape(64, 128, 32, 128).transpose(2, 1, 0, 3).reshape(32, 128, 8192))
    shared = dict(i_kf=kf, i_vf=vf, i_wo=wo_b, i_w1t=w1t_b, i_ob=ob, i_wsc=wsc, i_wbc=wbc,
                  i_ones=np.ones((128, 128), np.float32),
                  i_oc_bf=np.ones((128, 1), BF),
                  i_smsh=np.full((128, 1), SM_SHIFT, np.float32))
    per_core = []
    for c in range(NCORES):
        per_core.append(dict(i_q=np.asarray(b_outs[c]["o_q"]),
                             i_x1=np.asarray(b_outs[c]["o_x1"], np.float32),
                             i_xn2=np.asarray(b_outs[c]["o_xn2"])))
    return shared, per_core


def run_B(inp, a_outs):
    nc = build_B()
    shared, per_core = stage_B(inp, a_outs)
    res = run_bass_kernel_spmd(nc, [dict(shared, **pc) for pc in per_core], core_ids=list(range(NCORES)))
    return res.results


def run_C(inp, b_outs):
    nc = build_C()
    shared, per_core = stage_C(inp, b_outs)
    res = run_bass_kernel_spmd(nc, [dict(shared, **pc) for pc in per_core], core_ids=list(range(NCORES)))
    return res.results


def kernel(**inputs):
    a_outs = run_A(inputs)
    b_outs = run_B(inputs, a_outs)
    c_outs = run_C(inputs, b_outs)
    out = np.concatenate(
        [np.asarray(c_outs[c]["o_out"], np.float32).reshape(HD, T) for c in range(NCORES)], axis=1)
    return np.ascontiguousarray(out.T).reshape(1, S, HD).astype(np.float32)


# version 10 (final)
# speedup vs baseline: 1.3017x; 1.1052x over previous
import os
import numpy as np
import ml_dtypes
from contextlib import ExitStack

import concourse.bass as bass
import concourse.mybir as mybir
from concourse.bass_utils import run_bass_kernel_spmd

F32 = mybir.dt.float32
BF16 = mybir.dt.bfloat16
AF = mybir.ActivationFunctionType
ALU = mybir.AluOpType
BF = ml_dtypes.bfloat16

NCORES = 8
S, HD, DSSM, NHEAD = 8192, 4096, 8192, 64
NSTATE = 128
CONV_DIM = DSSM + 2 * NSTATE
PROJ_DIM = DSSM + CONV_DIM + NHEAD
T = S // NCORES
NCH, CK = 8, 128
EPS = 1e-5
NEGBIG = -1.0e30
SM_SHIFT = -32.0
SM_SCALE = 1.0 / float(np.sqrt(128.0))

ENGS = ("sync", "gpsimd", "tensor", "vector", "scalar")


class Prog:

    def __init__(self, nc, ctx):
        self.nc = nc
        self.ctx = ctx
        self.cmds = {e: [] for e in ENGS}
        self.ctrs = {}
        self.waits = {e: {} for e in ENGS}

    def ctr(self, name):
        if name not in self.ctrs:
            sem = self.ctx.enter_context(self.nc.semaphore(name))
            self.ctrs[name] = [sem, 0]
        return name

    def n(self, name):
        return self.ctrs[name][1]

    def wait(self, eng, name, val=None):
        if name not in self.ctrs:
            return
        sem, cnt = self.ctrs[name]
        v = cnt if val is None else val
        if v <= 0 or self.waits[eng].get(name, 0) >= v:
            return
        self.waits[eng][name] = v
        self.cmds[eng].append(lambda e, s=sem, vv=v: e.wait_ge(s, vv))

    def run(self, eng, fn, inc=None, k=1):
        if inc is None:
            self.cmds[eng].append(fn)
            return None
        self.ctr(inc)
        ent = self.ctrs[inc]
        self.cmds[eng].append(lambda e, f=fn, s=ent[0], kk=k: f(e).then_inc(s, kk))
        ent[1] += k
        return ent[1]

    def barrier(self, queue_map=None):
        queue_map = queue_map or {}
        for e, names in queue_map.items():
            for nm in names:
                self.wait(e, nm)
        for e in ENGS:
            for nm in list(self.ctrs):
                if not nm.startswith("bar_"):
                    self.wait(e, nm)
        vals = []
        for e in ENGS:
            nm = self.ctr(f"bar_{e}")
            ent = self.ctrs[nm]
            ent[1] += 1
            self.cmds[e].append(lambda en, s=ent[0]: en.nop().then_inc(s, 1))
            vals.append((ent[0], ent[1]))
        for e in ENGS:
            for s, v in vals:
                self.cmds[e].append(lambda en, ss=s, vv=v: en.wait_ge(ss, vv))
            for nm2, ent in self.ctrs.items():
                self.waits[e][nm2] = ent[1]

    def replay(self):
        with self.nc.Block() as block:
            @block.sync
            def _(e):
                for f in self.cmds["sync"]:
                    f(e)

            @block.gpsimd
            def _(e):
                for f in self.cmds["gpsimd"]:
                    f(e)

            @block.tensor
            def _(e):
                for f in self.cmds["tensor"]:
                    f(e)

            @block.vector
            def _(e):
                for f in self.cmds["vector"]:
                    f(e)

            @block.scalar
            def _(e):
                for f in self.cmds["scalar"]:
                    f(e)


def _blocked(w):
    Kt, Mt = w.shape
    return np.ascontiguousarray(
        w.reshape(Kt // 128, 128, Mt // 128, 128).transpose(2, 1, 0, 3).reshape(
            Mt // 128, 128, Kt)).astype(np.float32)


def build_A():
    nc = bass.Bass("TRN2", target_bir_lowering=False, debug=False, detect_race_conditions=False)
    i_xt = nc.dram_tensor("i_xt", [HD, 128 + T], F32, kind="ExternalInput")
    i_win = nc.dram_tensor("i_win", [131, 128, 4096], BF16, kind="ExternalInput")
    i_kern = nc.dram_tensor("i_kern", [32, 128, 4096], BF16, kind="ExternalInput")
    i_bias = nc.dram_tensor("i_bias", [128, 32], F32, kind="ExternalInput")
    i_cw = nc.dram_tensor("i_cw", [128, 264], F32, kind="ExternalInput")
    i_dtb = nc.dram_tensor("i_dtb", [64, 1], F32, kind="ExternalInput")
    i_alog = nc.dram_tensor("i_alog", [64, 1], F32, kind="ExternalInput")
    i_drow = nc.dram_tensor("i_drow", [1, 64], F32, kind="ExternalInput")
    i_ge = nc.dram_tensor("i_ge", [128, 128], BF16, kind="ExternalInput")
    i_la = nc.dram_tensor("i_la", [4, 128, 4096], BF16, kind="ExternalInput")
    i_lbt = nc.dram_tensor("i_lbt", [4, 128, 4096], BF16, kind="ExternalInput")
    i_lat = nc.dram_tensor("i_lat", [128, 1024], BF16, kind="ExternalInput")
    i_lbst = nc.dram_tensor("i_lbst", [32, 4096], BF16, kind="ExternalInput")
    i_ident = nc.dram_tensor("i_ident", [128, 128], BF16, kind="ExternalInput")
    i_identf = nc.dram_tensor("i_identf", [128, 128], F32, kind="ExternalInput")
    i_mp = nc.dram_tensor("i_mp", [128, 128], F32, kind="ExternalInput")
    i_ones = nc.dram_tensor("i_ones", [128, 128], F32, kind="ExternalInput")
    i_onescol = nc.dram_tensor("i_onescol", [128, 1], F32, kind="ExternalInput")
    i_misc = nc.dram_tensor("i_misc", [1, 4], F32, kind="ExternalInput")

    o_siluz = nc.dram_tensor("o_siluz", [64, 128, T], BF16, kind="ExternalOutput")
    o_xbcc = nc.dram_tensor("o_xbcc", [66, 128, T], BF16, kind="ExternalOutput")
    o_ylc = nc.dram_tensor("o_ylc", [NCH, 128, 8192], BF16, kind="ExternalOutput")
    o_g = nc.dram_tensor("o_g", [32, 128, T], F32, kind="ExternalOutput")
    o_lamg = nc.dram_tensor("o_lamg", [64, T], F32, kind="ExternalOutput")
    o_scor = nc.dram_tensor("o_scor", [128, 8192], BF16, kind="ExternalOutput")

    d_row1 = nc.dram_tensor("d_row1", [8192], F32, kind="Internal")
    d_row2 = nc.dram_tensor("d_row2", [64], F32, kind="Internal")

    ctx = ExitStack()
    with ctx:
        pr = Prog(nc, ctx)
        sbt = lambda nm, sh, dt: ctx.enter_context(nc.sbuf_tensor(nm, sh, dt))
        pst = lambda nm, sh, dt=F32: ctx.enter_context(nc.psum_tensor(nm, sh, dt))

        bk = [pst(f"bk{i}", [128, 512]) for i in range(5)]
        pT = [pst(f"pT{i}", [128, 1024], BF16) for i in range(2)]

        ident = sbt("ident", [128, 128], BF16)
        identf = sbt("identf", [128, 128], F32)
        mp = sbt("mp", [128, 128], F32)
        onescol = sbt("onescol", [128, 1], F32)
        ones1r = sbt("ones1r", [1, 128], F32)
        cw = sbt("cw", [128, 264], F32)
        dtb = sbt("dtb", [64, 1], F32)
        nega = sbt("nega", [64, 1], F32)
        alog = sbt("alog", [64, 1], F32)
        drow = sbt("drow", [1, 64], F32)
        drowbc = sbt("drowbc", [128, 64], F32)
        biasfm = sbt("biasfm", [128, 32], F32)
        epsc = sbt("epsc", [1, 1], F32)
        onec = sbt("onec", [1, 1], F32)
        halobuf = sbt("halobuf", [128, 66 * 4], BF16)
        dtt = sbt("dtt", [64, T], F32)
        lamg = sbt("lamg", [64, T], F32)
        hst = sbt("hst", [128, 64 * 128], BF16)


        c_ld = pr.ctr("c_ld")
        c_gld = pr.ctr("c_gld")
        c_st = pr.ctr("c_st")
        c_pe = pr.ctr("c_pe")
        c_dve = pr.ctr("c_dve")
        c_act = pr.ctr("c_act")
        QMAP = {"sync": [c_ld], "gpsimd": [c_gld, c_st]}

        for o, i in ((ident[:], i_ident[:]), (identf[:], i_identf[:]), (mp[:], i_mp[:]),
                     (onescol[:], i_onescol[:]), (ones1r[:], i_ones[0:1, :]),
                     (epsc[:], i_misc[0:1, 0:1]), (onec[:], i_misc[0:1, 1:2]),
                     (cw[:], i_cw[:]), (dtb[:], i_dtb[:]), (alog[:], i_alog[:]),
                     (drow[:], i_drow[:]), (biasfm[:], i_bias[:])):
            pr.run("sync", lambda e, oo=o, ii=i: e.dma_start(out=oo, in_=ii), inc=c_ld, k=16)
        pr.run("gpsimd", lambda e: e.dma_start(out=gecol[:], in_=i_ge[:]), inc=c_gld, k=16)
        pr.run("gpsimd", lambda e: e.dma_start(out=asht[:], in_=i_lat[:]), inc=c_gld, k=16)
        pr.run("gpsimd", lambda e: e.dma_start(out=bsht[:], in_=i_lbst[:]), inc=c_gld, k=16)

        pr.wait("scalar", c_ld)
        pr.run("scalar", lambda e: e.activation(nega[:], alog[:], AF.Exp), inc=c_act)
        pr.wait("vector", c_act)
        pr.run("vector", lambda e: e.tensor_scalar_mul(nega[:], nega[:], -1.0), inc=c_dve)
        pr.wait("tensor", c_ld)
        pr.run("tensor", lambda e: e.matmul(bk[4][:, 0:64], ones1r[:], drow[:], start=True, stop=True), inc=c_pe)
        pr.wait("vector", c_pe)
        pr.run("vector", lambda e: e.tensor_copy(drowbc[:], bk[4][:, 0:64]), inc=c_dve)
        pr.run("vector", lambda e: e.memset(hst[:], 0.0), inc=c_dve)

        A_LORA = int(os.environ.get("A_LORA", "1"))
        lsc = ExitStack()
        aeff = lsc.enter_context(nc.sbuf_tensor("aeff", [128, 32 * 32], BF16))
        asht = lsc.enter_context(nc.sbuf_tensor("asht", [128, 32 * 32], BF16))
        beff = lsc.enter_context(nc.sbuf_tensor("beff", [32, 4096], BF16))
        bsht = lsc.enter_context(nc.sbuf_tensor("bsht", [32, 4096], BF16))
        gecol = lsc.enter_context(nc.sbuf_tensor("gecol", [128, 4 * 32], BF16))
        with ExitStack() as lctx:
          if A_LORA:
            lab = [lctx.enter_context(nc.sbuf_tensor(f"lab{kt}", [128, 4096], BF16)) for kt in range(4)]
            pr.wait("gpsimd", c_dve)
            for kt in range(4):
                pr.run("gpsimd", lambda e, kt=kt: e.dma_start(out=lab[kt][:], in_=i_la[kt]), inc=c_gld, k=16)
            pr.wait("tensor", c_gld)
            pr.wait("tensor", c_dve)
            for kc in range(32):
                for kt in range(4):
                    pr.run("tensor", lambda e, kc=kc, kt=kt: e.matmul(
                        bk[0][:, 0:32], lab[kt][:, kc * 128:(kc + 1) * 128],
                        gecol[:, kt * 32:(kt + 1) * 32],
                        start=(kt == 0), stop=(kt == 3)), inc=(c_pe if kt == 3 else None))
                pr.wait("vector", c_pe)
                pr.run("vector", lambda e, kc=kc: e.tensor_copy(aeff[:, kc * 32:(kc + 1) * 32], bk[0][:, 0:32]), inc=c_dve)
                pr.wait("tensor", c_dve)
            pr.wait("gpsimd", c_pe)
            for kt in range(4):
                pr.run("gpsimd", lambda e, kt=kt: e.dma_start(out=lab[kt][:], in_=i_lbt[kt]), inc=c_gld, k=16)
            pr.wait("tensor", c_gld)
            for pc in range(8):
                for kt in range(4):
                    pr.run("tensor", lambda e, pc=pc, kt=kt: e.matmul(
                        bk[0][0:32, :], gecol[:, kt * 32:(kt + 1) * 32],
                        lab[kt][:, pc * 512:(pc + 1) * 512],
                        start=(kt == 0), stop=(kt == 3)), inc=(c_pe if kt == 3 else None))
                pr.wait("vector", c_pe)
                pr.run("vector", lambda e, pc=pc: e.tensor_copy(beff[:, pc * 512:(pc + 1) * 512], bk[0][0:32, :]), inc=c_dve)
                pr.wait("tensor", c_dve)

        with ExitStack() as p1:
            x_sb = p1.enter_context(nc.sbuf_tensor("x_sb", [128, 32 * 512], F32))
            xn_sb = p1.enter_context(nc.sbuf_tensor("xn_sb", [128, 32 * 512], BF16))
            wblk = [p1.enter_context(nc.sbuf_tensor(f"wblk{i}", [128, 32 * 128], BF16)) for i in range(2)]
            kblk = wblk
            zbuf = [p1.enter_context(nc.sbuf_tensor(f"zbuf{i}", [128, 512], BF16)) for i in range(2)]
            xcb = [p1.enter_context(nc.sbuf_tensor(f"xcb{i}", [128, 512], BF16)) for i in range(2)]
            gb = [p1.enter_context(nc.sbuf_tensor(f"gb{i}", [128, 512], F32)) for i in range(2)]
            xwin = p1.enter_context(nc.sbuf_tensor("xwin", [128, 516], BF16))
            diag = p1.enter_context(nc.sbuf_tensor("diag", [128, 4 * 128], BF16))
            x2t = [p1.enter_context(nc.sbuf_tensor(f"x2t{i}", [128, 512], F32)) for i in range(2)]
            mrow = p1.enter_context(nc.sbuf_tensor("mrow", [1, 512], F32))
            rrow = p1.enter_context(nc.sbuf_tensor("rrow", [1, 512], F32))
            srow = p1.enter_context(nc.sbuf_tensor("srow", [1, 512], F32))
            mbc = p1.enter_context(nc.sbuf_tensor("mbc", [128, 512], F32))
            rbc = p1.enter_context(nc.sbuf_tensor("rbc", [128, 512], F32))
            tmpf = p1.enter_context(nc.sbuf_tensor("tmpf", [128, 512], F32))
            axb = p1.enter_context(nc.sbuf_tensor("axb", [32, 512], BF16))
            axsb = p1.enter_context(nc.sbuf_tensor("axsb", [32, 512], BF16))
            spA = p1.enter_context(nc.sbuf_tensor("spA", [64, 512], F32))
            spB = p1.enter_context(nc.sbuf_tensor("spB", [64, 512], F32))
            spC = p1.enter_context(nc.sbuf_tensor("spC", [64, 512], F32))
            dtsc = p1.enter_context(nc.sbuf_tensor("dtsc", [64, T], F32))

            def ln_xn(ncols, xsrc, xn_out):
                pr.wait("sync", c_dve)
                pr.wait("sync", c_pe)
                for kc in range(32):
                    pr.run("sync", lambda e, kc=kc: e.dma_start(
                        out=x_sb[:, kc * 512:kc * 512 + ncols],
                        in_=i_xt[kc * 128:(kc + 1) * 128, xsrc:xsrc + ncols]), inc=c_ld, k=16)
                pr.wait("scalar", c_ld)
                pr.wait("scalar", c_pe)
                pr.wait("tensor", c_ld)
                pr.wait("tensor", c_act)
                pr.wait("tensor", c_dve)
                for kc in range(32):
                    pr.run("tensor", lambda e, kc=kc: e.matmul(
                        bk[3][0:1, 0:ncols], onescol[:], x_sb[:, kc * 512:kc * 512 + ncols],
                        start=(kc == 0), stop=(kc == 31)), inc=None)
                sq, x2v = {}, {}
                for kc in range(32):
                    if kc >= 2:
                        pr.wait("scalar", c_pe, x2v[kc - 2])
                    sq[kc] = pr.run("scalar", lambda e, kc=kc: e.activation(
                        x2t[kc % 2][:, 0:ncols], x_sb[:, kc * 512:kc * 512 + ncols], AF.Square), inc=c_act)
                    pr.wait("tensor", c_act, sq[kc])
                    x2v[kc] = pr.run("tensor", lambda e, kc=kc: e.matmul(
                        bk[4][0:1, 0:ncols], onescol[:], x2t[kc % 2][:, 0:ncols],
                        start=(kc == 0), stop=(kc == 31)), inc=c_pe)
                pr.wait("vector", c_pe)
                pr.run("vector", lambda e: e.tensor_scalar_mul(mrow[0:1, 0:ncols], bk[3][0:1, 0:ncols], 1.0 / HD), inc=c_dve)
                pr.run("vector", lambda e: e.tensor_scalar_mul(srow[0:1, 0:ncols], bk[4][0:1, 0:ncols], 1.0 / HD), inc=c_dve)
                pr.wait("vector", c_dve)
                pr.run("vector", lambda e: e.tensor_mul(rrow[0:1, 0:ncols], mrow[0:1, 0:ncols], mrow[0:1, 0:ncols]), inc=c_dve)
                pr.wait("vector", c_dve)
                pr.run("vector", lambda e: e.tensor_sub(srow[0:1, 0:ncols], srow[0:1, 0:ncols], rrow[0:1, 0:ncols]), inc=c_dve)
                pr.wait("scalar", c_dve)
                pr.run("scalar", lambda e: e.activation(srow[0:1, 0:ncols], srow[0:1, 0:ncols], AF.Sqrt, bias=epsc[:]), inc=c_act)
                pr.wait("vector", c_act)
                pr.run("vector", lambda e: e.reciprocal(rrow[0:1, 0:ncols], srow[0:1, 0:ncols]), inc=c_dve)
                pr.wait("tensor", c_dve)
                pr.run("tensor", lambda e: e.matmul(bk[3][:, 0:ncols], ones1r[:], mrow[0:1, 0:ncols], start=True, stop=True), inc=None)
                pr.run("tensor", lambda e: e.matmul(bk[4][:, 0:ncols], ones1r[:], rrow[0:1, 0:ncols], start=True, stop=True), inc=c_pe)
                pr.wait("vector", c_pe)
                pr.run("vector", lambda e: e.tensor_copy(mbc[:, 0:ncols], bk[3][:, 0:ncols]), inc=c_dve)
                pr.run("vector", lambda e: e.tensor_copy(rbc[:, 0:ncols], bk[4][:, 0:ncols]), inc=c_dve)
                pr.wait("vector", c_dve)
                for kc in range(32):
                    pr.run("vector", lambda e, kc=kc: e.tensor_sub(
                        tmpf[:, 0:ncols], x_sb[:, kc * 512:kc * 512 + ncols], mbc[:, 0:ncols]), inc=c_dve)
                    pr.wait("vector", c_dve)
                    pr.run("vector", lambda e, kc=kc: e.tensor_mul(
                        xn_out(kc)[:, 0:ncols], tmpf[:, 0:ncols], rbc[:, 0:ncols]), inc=c_dve)

            A_NG = int(os.environ.get("A_NG", "2"))
            A_NMC = int(os.environ.get("A_NMC", "131"))
            A_FFN = int(os.environ.get("A_FFN", "1"))
            passes = [(128, None, 0)] + [(512, g, 128 + g * 512) for g in range(A_NG)]
            if A_NG == 0:
                passes = []
            wblk_guard = {0: 0, 1: 0}
            for (ncols, g, xoff) in passes:
                halo_pass = g is None
                ln_xn(ncols, xoff, lambda kc: xn_sb[:, kc * 512:kc * 512 + ncols])

                pe_mc, ev_mc = {}, {}
                for Mc in range(A_NMC if int(os.environ.get("A_PROJ", "1")) else 0):
                    wb = wblk[Mc % 2]
                    pr.wait("gpsimd", c_pe, wblk_guard[Mc % 2])
                    gld_v = pr.run("gpsimd", lambda e, Mc=Mc, wb=wb: e.dma_start(
                        out=wb[:], in_=i_win[Mc]), inc=c_gld, k=16)
                    pr.wait("tensor", c_gld, gld_v)
                    ev = ev_mc.get(Mc - 3)
                    if ev is not None:
                        pr.wait("tensor", ev[0], ev[1])
                    elif Mc < 3:
                        pr.wait("tensor", c_dve)
                        pr.wait("tensor", c_act)
                    pb = bk[Mc % 3]
                    for kc in range(32):
                        pr.run("tensor", lambda e, kc=kc, wb=wb, pb=pb, ncols=ncols: e.matmul(
                            pb[:, 0:ncols], wb[:, kc * 128:(kc + 1) * 128],
                            xn_sb[:, kc * 512:kc * 512 + ncols],
                            start=(kc == 0), stop=(kc == 31)),
                            inc=(None if kc < 31 else c_pe))
                    pev = pr.n(c_pe)
                    pe_mc[Mc] = pev
                    wblk_guard[Mc % 2] = pev

                    if Mc < 64:
                        if halo_pass:
                            continue
                        zb = zbuf[Mc % 2]
                        pr.wait("scalar", c_pe, pev)
                        pr.wait("scalar", c_st, ev_mc.get(("st", Mc - 2), 0))
                        av = pr.run("scalar", lambda e, zb=zb, pb=pb: e.activation(zb[:], pb[:], AF.Silu), inc=c_act)
                        ev_mc[Mc] = (c_act, av)
                        pr.wait("gpsimd", c_act, av)
                        ev_mc[("st", Mc)] = pr.run("gpsimd", lambda e, Mc=Mc, g=g, zb=zb: e.dma_start(
                            out=o_siluz[Mc, :, g * 512:(g + 1) * 512], in_=zb[:]), inc=c_st, k=16)
                    elif Mc < 130:
                        j = Mc - 64
                        pr.wait("vector", c_pe, pev)
                        pr.run("vector", lambda e, j=j: e.tensor_copy(xwin[:, 0:4], halobuf[:, j * 4:(j + 1) * 4]), inc=None)
                        pr.run("vector", lambda e, pb=pb, ncols=ncols: e.tensor_copy(
                            xwin[:, 4:4 + ncols], pb[:, 0:ncols]), inc=None)
                        dvs = pr.run("vector", lambda e, j=j, pb=pb, ncols=ncols: e.tensor_copy(
                            halobuf[:, j * 4:(j + 1) * 4], pb[:, ncols - 4:ncols]), inc=c_dve)
                        ev_mc[Mc] = (c_dve, dvs)
                        for k in range(4):
                            pr.run("vector", lambda e, j=j, k=k: e.tensor_scalar_mul(
                                diag[:, k * 128:(k + 1) * 128], ident[:], cw[:, j * 4 + k:j * 4 + k + 1]),
                                inc=(None if k < 3 else c_dve))
                        dvev = pr.n(c_dve)
                        if halo_pass:
                            continue
                        cb = bk[3 + (j % 2)]
                        pr.wait("tensor", c_dve, dvev)
                        pr.wait("tensor", c_act, ev_mc.get(("cv", j - 2), 0))
                        for k in range(4):
                            pr.run("tensor", lambda e, k=k, cb=cb, ncols=ncols: e.matmul(
                                cb[:, 0:ncols], diag[:, k * 128:(k + 1) * 128], xwin[:, k + 1:k + 1 + ncols],
                                start=(k == 0), stop=(k == 3)), inc=(c_pe if k == 3 else None))
                        xc = xcb[j % 2]
                        pr.wait("scalar", c_pe)
                        pr.wait("scalar", c_st, ev_mc.get(("cst", j - 2), 0))
                        cv = pr.run("scalar", lambda e, xc=xc, cb=cb: e.activation(xc[:], cb[:], AF.Silu), inc=c_act)
                        ev_mc[("cv", j)] = cv
                        pr.wait("gpsimd", c_act, cv)
                        ev_mc[("cst", j)] = pr.run("gpsimd", lambda e, j=j, g=g, xc=xc: e.dma_start(
                            out=o_xbcc[j, :, g * 512:(g + 1) * 512], in_=xc[:]), inc=c_st, k=16)
                    else:
                        if halo_pass:
                            continue
                        ev_mc[Mc] = (c_act, pr.n(c_act) + 2)
                        pr.wait("scalar", c_pe, pev)
                        pr.run("scalar", lambda e, pb=pb: e.activation(spA[:], pb[0:64, :], AF.Abs, bias=dtb[:]), inc=c_act)
                        pr.run("scalar", lambda e, pb=pb: e.activation(spB[:], pb[0:64, :], AF.Relu, bias=dtb[:]), inc=c_act)
                        pr.wait("scalar", c_act)
                        pr.run("scalar", lambda e: e.activation(spC[:], spA[:], AF.Exp, scale=-1.0), inc=c_act)
                        pr.wait("scalar", c_act)
                        pr.run("scalar", lambda e: e.activation(spA[:], spC[:], AF.Ln, bias=onescol[0:64, :]), inc=c_act)
                        pr.wait("vector", c_act)
                        pr.run("vector", lambda e, g=g: e.tensor_add(
                            dtt[:, g * 512:(g + 1) * 512], spB[:], spA[:]), inc=c_dve)

                if not A_FFN or halo_pass:
                    continue
                pr.wait("tensor", c_dve)
                pr.wait("tensor", c_act)
                for kc in range(32):
                    pr.run("tensor", lambda e, kc=kc: e.matmul(
                        bk[3][0:32, :], aeff[:, kc * 32:(kc + 1) * 32], xn_sb[:, kc * 512:(kc + 1) * 512],
                        start=(kc == 0), stop=(kc == 31)), inc=None)
                for kc in range(32):
                    pr.run("tensor", lambda e, kc=kc: e.matmul(
                        bk[4][0:32, :], asht[:, kc * 32:(kc + 1) * 32], xn_sb[:, kc * 512:(kc + 1) * 512],
                        start=(kc == 0), stop=(kc == 31)), inc=(None if kc < 31 else c_pe))
                pr.wait("vector", c_pe)
                pr.run("vector", lambda e: e.tensor_copy(axb[:], bk[3][0:32, :]), inc=None)
                pr.run("vector", lambda e: e.tensor_copy(axsb[:], bk[4][0:32, :]), inc=c_dve)
                fpe, fev, fst = {}, {}, {}
                pr.wait("tensor", c_dve)
                pr.wait("tensor", c_act)
                for Mc in range(32):
                    kb = kblk[Mc % 2]
                    pr.wait("gpsimd", c_pe, wblk_guard[Mc % 2])
                    gv = pr.run("gpsimd", lambda e, Mc=Mc, kb=kb: e.dma_start(
                        out=kb[:], in_=i_kern[Mc]), inc=c_gld, k=16)
                    pr.wait("tensor", c_gld, gv)
                    if Mc >= 3:
                        pr.wait("tensor", fev[Mc - 3][0], fev[Mc - 3][1])
                    pb = bk[Mc % 3]
                    for kc in range(32):
                        pr.run("tensor", lambda e, kc=kc, kb=kb, pb=pb: e.matmul(
                            pb[:], kb[:, kc * 128:(kc + 1) * 128], xn_sb[:, kc * 512:(kc + 1) * 512],
                            start=(kc == 0), stop=False), inc=None)
                    pr.run("tensor", lambda e, Mc=Mc, pb=pb: e.matmul(
                        pb[:], beff[:, Mc * 128:(Mc + 1) * 128], axb[:], start=False, stop=False), inc=None)
                    pr.run("tensor", lambda e, Mc=Mc, pb=pb: e.matmul(
                        pb[:], bsht[:, Mc * 128:(Mc + 1) * 128], axsb[:], start=False, stop=True), inc=c_pe)
                    pev = pr.n(c_pe)
                    fpe[Mc] = pev
                    wblk_guard[Mc % 2] = pev
                    gbx = gb[Mc % 2]
                    pr.wait("scalar", c_pe, pev)
                    pr.wait("scalar", c_st, fst.get(Mc - 2, 0))
                    av = pr.run("scalar", lambda e, Mc=Mc, gbx=gbx, pb=pb: e.activation(
                        gbx[:], pb[:], AF.Silu, bias=biasfm[:, Mc:Mc + 1]), inc=c_act)
                    fev[Mc] = (c_act, av)
                    pr.wait("gpsimd", c_act, av)
                    fst[Mc] = pr.run("gpsimd", lambda e, Mc=Mc, g=g, gbx=gbx: e.dma_start(
                        out=o_g[Mc, :, g * 512:(g + 1) * 512], in_=gbx[:]), inc=c_st, k=16)

            pr.wait("vector", c_dve)
            pr.wait("vector", c_act)
            pr.run("vector", lambda e: e.tensor_scalar_mul(dtsc[:], dtt[:], nega[:]), inc=c_dve)
            pr.wait("vector", c_dve)
            pr.run("vector", lambda e: e.tensor_tensor_scan(lamg[:], dtsc[:], dtsc[:], 0.0, ALU.add, ALU.bypass), inc=c_dve)
            pr.wait("gpsimd", c_dve)
            pr.run("gpsimd", lambda e: e.dma_start(out=o_lamg[:], in_=lamg[:]), inc=c_st, k=16)
        lsc.close()
        pr.barrier(QMAP)

        with ExitStack() as p2:
            xcc = [p2.enter_context(nc.sbuf_tensor(f"xcc{i}", [128, 64 * 128], BF16)) for i in range(2)]
            Bt = p2.enter_context(nc.sbuf_tensor("Bt", [128, T], BF16))
            Ct = p2.enter_context(nc.sbuf_tensor("Ct", [128, T], BF16))
            dtx = p2.enter_context(nc.sbuf_tensor("dtx", [128, 8192], BF16))
            mask = p2.enter_context(nc.sbuf_tensor("mask", [128, 8192], BF16))
            cbtm = p2.enter_context(nc.sbuf_tensor("cbtm", [128, 8192], BF16))
            cwt = p2.enter_context(nc.sbuf_tensor("cwt", [128, 8192], BF16))
            ylcb = p2.enter_context(nc.sbuf_tensor("ylcb", [128, 8192], BF16))
            lrowp = [p2.enter_context(nc.sbuf_tensor(f"lrowp{i}", [1, 512], F32)) for i in range(2)]
            ltrow = p2.enter_context(nc.sbuf_tensor("ltrow", [1, 64], F32))
            argp = [p2.enter_context(nc.sbuf_tensor(f"argp{i}", [128, 512], F32)) for i in range(2)]
            expp = p2.enter_context(nc.sbuf_tensor("expp", [128, 512], BF16))
            lamfm = p2.enter_context(nc.sbuf_tensor("lamfm", [64, 128], F32))
            lamTM = p2.enter_context(nc.sbuf_tensor("lamTM", [128, 64], F32))
            dttTM = p2.enter_context(nc.sbuf_tensor("dttTM", [128, 64], F32))
            ltotbc = p2.enter_context(nc.sbuf_tensor("ltotbc", [128, 64], F32))
            endw = p2.enter_context(nc.sbuf_tensor("endw", [128, 64], F32))
            atotbc = p2.enter_context(nc.sbuf_tensor("atotbc", [128, 64], F32))
            cbtraw = p2.enter_context(nc.sbuf_tensor("cbtraw", [128, 128], BF16))
            btm = p2.enter_context(nc.sbuf_tensor("btm", [128, 128], BF16))
            bwb = [p2.enter_context(nc.sbuf_tensor(f"bwb{i}", [128, 128], BF16)) for i in range(2)]

            pr.run("sync", lambda e: e.dma_start(out=Bt[:], in_=o_xbcc[64]), inc=c_ld, k=16)
            pr.run("sync", lambda e: e.dma_start(out=Ct[:], in_=o_xbcc[65]), inc=c_ld, k=16)
            btct_v = pr.n(c_ld)

            def load_xcc(c):
                buf = xcc[c % 2]
                pr.wait("sync", c_dve)
                pr.wait("sync", c_pe)
                pr.wait("gpsimd", c_dve)
                pr.wait("gpsimd", c_pe)
                for m in range(64):
                    eng, cc = ("sync", c_ld) if m % 2 == 0 else ("gpsimd", c_gld)
                    pr.run(eng, lambda e, m=m, c=c, buf=buf: e.dma_start(
                        out=buf[:, m * 128:(m + 1) * 128], in_=o_xbcc[m, :, c * 128:(c + 1) * 128]),
                        inc=cc, k=16)

            A_NCH = int(os.environ.get("A_NCH", str(NCH)))
            if A_NCH > 0:
                load_xcc(0)
            for c in range(A_NCH):
                buf = xcc[c % 2]
                ld_mark = (pr.n(c_ld), pr.n(c_gld))
                if c + 1 < NCH:
                    load_xcc(c + 1)

                if c == 0:
                    pr.run("vector", lambda e: e.tensor_copy(lamfm[:], lamg[:, 0:128]), inc=c_dve)
                else:
                    pr.run("vector", lambda e, c=c: e.tensor_scalar(
                        lamfm[:], lamg[:, c * 128:(c + 1) * 128], lamg[:, c * 128 - 1:c * 128], None,
                        ALU.subtract), inc=c_dve)
                pr.wait("gpsimd", c_dve)
                pr.run("gpsimd", lambda e: e.dma_start(
                    out=d_row1[:].rearrange("(h t) -> h t", h=64), in_=lamfm[:]), inc=c_st, k=16)
                pr.run("gpsimd", lambda e: e.dma_start(
                    out=d_row2[:].rearrange("(h o) -> h o", h=64),
                    in_=lamfm[:, 127:128]), inc=c_st, k=16)
                pr.wait("sync", c_st)
                pr.run("sync", lambda e: e.dma_start(out=ltrow[:], in_=d_row2[:].unsqueeze(0)), inc=c_ld, k=16)
                row_v = pr.n(c_ld)

                A_P2 = int(os.environ.get("A_P2", "9"))
                if A_P2 < 2:
                    continue
                pr.wait("tensor", c_dve)
                pr.run("tensor", lambda e: e.transpose(bk[4][0:128, 0:64], lamfm[:], identf[0:64, 0:64]), inc=c_pe)
                pr.wait("vector", c_pe)
                pr.run("vector", lambda e: e.tensor_copy(lamTM[:], bk[4][0:128, 0:64]), inc=c_dve)
                pr.wait("tensor", c_dve)
                pr.run("tensor", lambda e, c=c: e.transpose(bk[4][0:128, 0:64], dtt[:, c * 128:(c + 1) * 128], identf[0:64, 0:64]), inc=c_pe)
                pr.wait("vector", c_pe)
                pr.run("vector", lambda e: e.tensor_copy(dttTM[:], bk[4][0:128, 0:64]), inc=c_dve)
                pr.wait("tensor", c_ld, row_v)
                pr.wait("tensor", c_dve)
                pr.run("tensor", lambda e: e.matmul(bk[4][:, 0:64], ones1r[:], ltrow[:], start=True, stop=True), inc=c_pe)
                pr.wait("vector", c_pe)
                pr.run("vector", lambda e: e.tensor_copy(ltotbc[:], bk[4][:, 0:64]), inc=c_dve)
                pr.wait("vector", c_dve)
                pr.run("vector", lambda e: e.scalar_tensor_tensor(
                    endw[:], lamTM[:], -1.0, ltotbc[:], ALU.mult, ALU.add), inc=c_dve)
                pr.wait("scalar", c_dve)
                pr.run("scalar", lambda e: e.activation(endw[:], endw[:], AF.Exp), inc=None)
                pr.run("scalar", lambda e: e.activation(atotbc[:], ltotbc[:], AF.Exp), inc=c_act)
                pr.wait("tensor", c_ld, btct_v)
                pr.wait("tensor", c_dve)
                pr.run("tensor", lambda e, c=c: e.matmul(
                    bk[4][:, 0:128], Bt[:, c * 128:(c + 1) * 128], Ct[:, c * 128:(c + 1) * 128],
                    start=True, stop=True), inc=c_pe)
                pr.wait("vector", c_pe)
                pr.run("vector", lambda e: e.tensor_copy(cbtraw[:], bk[4][:, 0:128]), inc=c_dve)

                if A_P2 < 3:
                    continue
                for p in range(16):
                    pb = bk[p % 2]
                    lrp = lrowp[p % 2]
                    pr.wait("sync", c_pe)
                    pr.run("sync", lambda e, p=p, lrp=lrp: e.dma_start(
                        out=lrp[:], in_=d_row1[p * 512:(p + 1) * 512].unsqueeze(0)), inc=c_ld, k=16)
                    pr.wait("tensor", c_dve)
                    pr.wait("tensor", c_act)
                    pr.wait("tensor", c_ld)
                    pr.run("tensor", lambda e, lrp=lrp, pb=pb: e.matmul(
                        pb[:], ones1r[:], lrp[:], start=True, stop=True), inc=c_pe)
                    pev = pr.n(c_pe)
                    ap = argp[p % 2]
                    pr.wait("vector", c_pe, pev)
                    for j in range(4):
                        h = p * 4 + j
                        pr.run("vector", lambda e, j=j, h=h, ap=ap, pb=pb: e.scalar_tensor_tensor(
                            ap[:, j * 128:(j + 1) * 128], pb[:, j * 128:(j + 1) * 128],
                            lamTM[:, h:h + 1], mp[:], ALU.subtract, ALU.add),
                            inc=(None if j < 3 else c_dve))
                    pr.wait("scalar", c_pe, pev)
                    pr.wait("scalar", c_dve)
                    pr.run("scalar", lambda e, pb=pb: e.activation(expp[:], pb[:], AF.Exp), inc=None)
                    pr.run("scalar", lambda e, p=p, ap=ap: e.activation(
                        mask[:, p * 512:(p + 1) * 512], ap[:], AF.Exp), inc=c_act)
                    pr.wait("vector", c_act)
                    pr.run("vector", lambda e, p=p, c=c: e.tensor_tensor(
                        cwt[:, p * 512:(p + 1) * 512].rearrange("q (a b) -> q a b", a=4),
                        expp[:].rearrange("q (a b) -> q a b", a=4),
                        Ct[:, c * 128:(c + 1) * 128].unsqueeze(1).broadcast_to([128, 4, 128]),
                        ALU.mult), inc=c_dve)

                pr.wait("vector", c_dve)
                pr.run("vector", lambda e: e.tensor_tensor(
                    cbtm[:].rearrange("q (a b) -> q a b", a=64),
                    mask[:].rearrange("q (a b) -> q a b", a=64),
                    cbtraw[:].unsqueeze(1).broadcast_to([128, 64, 128]), ALU.mult), inc=c_dve)

                if A_P2 < 4:
                    continue
                pr.wait("tensor", c_ld, ld_mark[0])
                pr.wait("tensor", c_gld, ld_mark[1])
                pr.wait("tensor", c_dve)
                for m in range(64):
                    pt = pT[m % 2]
                    if m >= 2:
                        pr.wait("tensor", c_dve, dtx_ev[m - 2])
                    pr.run("tensor", lambda e, m=m, buf=buf, pt=pt: e.transpose(
                        pt[:, 0:128], buf[:, m * 128:(m + 1) * 128], ident[:]), inc=c_pe)
                    pr.wait("vector", c_pe)
                    if m == 0:
                        dtx_ev = {}
                    dtx_ev[m] = pr.run("vector", lambda e, m=m, pt=pt: e.tensor_scalar_mul(
                        dtx[:, m * 128:(m + 1) * 128], pt[:, 0:128], dttTM[:, m:m + 1]), inc=c_dve)
                pr.wait("tensor", c_dve)
                pr.run("tensor", lambda e, c=c: e.transpose(pT[0][:, 0:128], Bt[:, c * 128:(c + 1) * 128], ident[:]), inc=c_pe)
                pr.wait("vector", c_pe)
                pr.run("vector", lambda e: e.tensor_copy(btm[:], pT[0][:, 0:128]), inc=c_dve)

                if A_P2 < 5:
                    continue
                pr.wait("tensor", c_dve)
                pr.wait("tensor", c_act)
                pr.wait("vector", c_dve)
                pe_h, dve_h = {}, {}
                for h in range(64):
                    pyb = bk[h % 2]
                    psb = bk[2 + (h % 2)]
                    pr.wait("vector", c_pe, pe_h.get(h - 2, 0))
                    pr.run("vector", lambda e, h=h: e.tensor_scalar_mul(
                        bwb[h % 2][:], btm[:], endw[:, h:h + 1]), inc=c_dve)
                    bwv = pr.n(c_dve)
                    pr.wait("tensor", c_dve, bwv)
                    pr.run("tensor", lambda e, h=h, pyb=pyb: e.matmul(
                        pyb[:, 0:128], dtx[:, h * 128:(h + 1) * 128], cbtm[:, h * 128:(h + 1) * 128],
                        start=True, stop=False), inc=None)
                    pr.run("tensor", lambda e, h=h, pyb=pyb: e.matmul(
                        pyb[:, 0:128], hst[:, h * 128:(h + 1) * 128], cwt[:, h * 128:(h + 1) * 128],
                        start=False, stop=True), inc=None)
                    pr.run("tensor", lambda e, h=h, psb=psb: e.matmul(
                        psb[:, 0:128], bwb[h % 2][:], dtx[:, h * 128:(h + 1) * 128],
                        start=True, stop=True), inc=c_pe)
                    pe_h[h] = pr.n(c_pe)
                    pr.wait("vector", c_pe, pe_h[h])
                    pr.run("vector", lambda e, h=h, psb=psb: e.scalar_tensor_tensor(
                        hst[:, h * 128:(h + 1) * 128], hst[:, h * 128:(h + 1) * 128],
                        atotbc[:, h:h + 1], psb[:, 0:128], ALU.mult, ALU.add), inc=None)
                    pr.run("vector", lambda e, h=h, buf=buf, pyb=pyb: e.scalar_tensor_tensor(
                        ylcb[:, h * 128:(h + 1) * 128], buf[:, h * 128:(h + 1) * 128],
                        drowbc[:, h:h + 1], pyb[:, 0:128], ALU.mult, ALU.add), inc=c_dve)
                    dve_h[h] = pr.n(c_dve)

                if A_P2 < 6:
                    continue
                pr.wait("gpsimd", c_dve)
                pr.run("gpsimd", lambda e, c=c: e.dma_start(out=o_ylc[c], in_=ylcb[:]), inc=c_st, k=16)
                pr.wait("vector", c_st)

            pr.wait("gpsimd", c_dve)
            pr.run("gpsimd", lambda e: e.dma_start(out=o_scor[:], in_=hst[:]), inc=c_st, k=16)
            pr.wait("gpsimd", c_st)
            pr.wait("sync", c_ld)

        pr.replay()
    return nc


def stage_A(inp):
    x = np.asarray(inp["L0_attn_hidden_states"], np.float32).reshape(S, HD)
    xT = np.ascontiguousarray(x.T)
    w_in = np.asarray(inp["L0_attn_W_in"], np.float32)
    w_pad = np.zeros((HD, 131 * 128), np.float32)
    w_pad[:, :PROJ_DIM] = w_in
    win_b = _blocked(w_pad).astype(BF)
    kern_b = _blocked(np.asarray(inp["L0_ffn_kernel"], np.float32)).astype(BF)
    bias_fm = np.ascontiguousarray(np.asarray(inp["L0_ffn_bias"], np.float32).reshape(32, 128).T)
    cwr = np.ascontiguousarray(
        np.asarray(inp["L0_attn_conv_w"], np.float32).reshape(66, 128, 4).transpose(1, 0, 2).reshape(128, 264))
    dtb = np.asarray(inp["L0_attn_dt_bias"], np.float32).reshape(64, 1)
    alog = np.asarray(inp["L0_attn_A_log"], np.float32).reshape(64, 1)
    drow = np.asarray(inp["L0_attn_D_param"], np.float32).reshape(1, 64)
    la = np.ascontiguousarray(np.asarray(inp["L0_ffn_lora_A"], np.float32).reshape(512, 4096).reshape(4, 128, 4096)).astype(BF)
    lbt = np.ascontiguousarray(
        np.asarray(inp["L0_ffn_lora_B"], np.float32).transpose(0, 2, 1).reshape(512, 4096).reshape(4, 128, 4096)).astype(BF)
    lat = np.ascontiguousarray(
        np.asarray(inp["L0_ffn_lora_A_shared"], np.float32).T.reshape(32, 128, 32)
        .transpose(1, 0, 2).reshape(128, 1024)).astype(BF)
    lbst = np.ascontiguousarray(np.asarray(inp["L0_ffn_lora_B_shared"], np.float32).T).astype(BF)
    ident = np.eye(128).astype(BF)
    identf = np.eye(128).astype(np.float32)
    mpat = np.where(np.arange(128)[None, :] >= np.arange(128)[:, None], 0.0, NEGBIG).astype(np.float32)
    ones = np.ones((128, 128), np.float32)
    gate = np.asarray(inp["L0_ffn_gate"], np.float32)

    shared = dict(i_win=win_b, i_kern=kern_b, i_bias=bias_fm, i_cw=cwr, i_dtb=dtb,
                  i_alog=alog, i_drow=drow, i_la=la, i_lbt=lbt, i_lat=lat, i_lbst=lbst,
                  i_ident=ident, i_identf=identf, i_mp=mpat, i_ones=ones,
                  i_onescol=np.ones((128, 1), np.float32),
                  i_misc=np.array([[EPS, 1.0, 0.0, 0.0]], np.float32))
    per_core = []
    for c in range(NCORES):
        t0 = c * T
        xt = np.zeros((HD, 128 + T), np.float32)
        lo = max(t0 - 128, 0)
        xt[:, 128 - (t0 - lo):] = xT[:, lo:t0 + T]
        ge = np.zeros((512, 32), np.float32)
        for nl in range(16):
            for r in range(32):
                ge[nl * 32 + r, r] = gate[c, nl]
        per_core.append(dict(i_xt=np.ascontiguousarray(xt),
                             i_ge=np.ascontiguousarray(
                                 ge.reshape(4, 128, 32).transpose(1, 0, 2).reshape(128, 128)).astype(BF)))
    return shared, per_core


def run_A(inp):
    nc = build_A()
    shared, per_core = stage_A(inp)
    in_maps = [dict(shared, **pc) for pc in per_core]
    res = run_bass_kernel_spmd(nc, in_maps, core_ids=list(range(NCORES)))
    return res.results




def build_B():
    nc = bass.Bass("TRN2", target_bir_lowering=False, debug=False, detect_race_conditions=False)
    i_xtb = nc.dram_tensor("i_xtb", [32, 128, T], F32, kind="ExternalInput")
    i_ylc = nc.dram_tensor("i_ylc", [NCH, 128, 8192], BF16, kind="ExternalInput")
    i_siluz = nc.dram_tensor("i_siluz", [64, 128, T], BF16, kind="ExternalInput")
    i_ct = nc.dram_tensor("i_ct", [128, T], BF16, kind="ExternalInput")
    i_lamg = nc.dram_tensor("i_lamg", [64, T], F32, kind="ExternalInput")
    i_sall = nc.dram_tensor("i_sall", [7, 128, 8192], BF16, kind="ExternalInput")
    i_ltv = nc.dram_tensor("i_ltv", [64, 7], F32, kind="ExternalInput")
    i_g = nc.dram_tensor("i_g", [32, 128, T], F32, kind="ExternalInput")
    i_wout = nc.dram_tensor("i_wout", [32, 128, 8192], BF16, kind="ExternalInput")
    i_nw = nc.dram_tensor("i_nw", [128, 64], F32, kind="ExternalInput")
    i_qkb = nc.dram_tensor("i_qkb", [128, 64], F32, kind="ExternalInput")
    i_qvb = nc.dram_tensor("i_qvb", [1, 4096], BF16, kind="ExternalInput")
    i_wqk = nc.dram_tensor("i_wqk", [64, 128, 4096], BF16, kind="ExternalInput")
    i_wv = nc.dram_tensor("i_wv", [8, 128, 32 * 512], BF16, kind="ExternalInput")
    i_ones = nc.dram_tensor("i_ones", [128, 128], F32, kind="ExternalInput")
    i_onescol = nc.dram_tensor("i_onescol", [128, 1], F32, kind="ExternalInput")
    i_oc_bf = nc.dram_tensor("i_oc_bf", [128, 1], BF16, kind="ExternalInput")
    i_o1r_bf = nc.dram_tensor("i_o1r_bf", [1, 128], BF16, kind="ExternalInput")
    i_misc = nc.dram_tensor("i_misc", [1, 4], F32, kind="ExternalInput")

    o_x1 = nc.dram_tensor("o_x1", [32, 128, T], F32, kind="ExternalOutput")
    o_xn2 = nc.dram_tensor("o_xn2", [32, 128, T], BF16, kind="ExternalOutput")
    o_q = nc.dram_tensor("o_q", [32, 128, T], BF16, kind="ExternalOutput")
    o_k = nc.dram_tensor("o_k", [32, 128, T], BF16, kind="ExternalOutput")
    o_v = nc.dram_tensor("o_v", [8, 128, 4096], BF16, kind="ExternalOutput")

    d_row3 = nc.dram_tensor("d_row3", [448], F32, kind="Internal")
    d_row1 = nc.dram_tensor("d_row1", [8192], F32, kind="Internal")
    d_rowM = nc.dram_tensor("d_rowM", [T], F32, kind="Internal")
    d_rowR = nc.dram_tensor("d_rowR", [T], F32, kind="Internal")

    ctx = ExitStack()
    with ctx:
        pr = Prog(nc, ctx)
        sbt = lambda nm, sh, dt: ctx.enter_context(nc.sbuf_tensor(nm, sh, dt))
        pst = lambda nm, sh, dt=F32: ctx.enter_context(nc.psum_tensor(nm, sh, dt))

        bkA = pst("bkA", [128, 512])
        bkB = pst("bkB", [128, 512])
        bkC = pst("bkC", [128, 512])
        bkD = pst("bkD", [128, 512])
        bkE = pst("bkE", [128, 512])
        bkS1 = pst("bkS1", [1, 512])
        bkS2 = pst("bkS2", [1, 512])

        ones1r = sbt("ones1r", [1, 128], F32)
        onescol = sbt("onescol", [128, 1], F32)
        oc_bf = sbt("oc_bf", [128, 1], BF16)
        o1r_bf = sbt("o1r_bf", [1, 128], BF16)
        epsc = sbt("epsc", [1, 1], F32)
        nwc = sbt("nwc", [128, 64], F32)
        qkbc = sbt("qkbc", [128, 64], F32)
        Ct = sbt("Ct", [128, T], BF16)
        lamg = sbt("lamg", [64, T], F32)
        hinb = sbt("hinb", [128, 8192], BF16)

        c_ld = pr.ctr("c_ld")
        c_gld = pr.ctr("c_gld")
        c_st = pr.ctr("c_st")
        c_pe = pr.ctr("c_pe")
        c_dve = pr.ctr("c_dve")
        c_act = pr.ctr("c_act")
        QMAP = {"sync": [c_ld], "gpsimd": [c_gld, c_st]}

        for o, i in ((ones1r[:], i_ones[0:1, :]), (onescol[:], i_onescol[:]),
                     (oc_bf[:], i_oc_bf[:]), (o1r_bf[:], i_o1r_bf[:]),
                     (epsc[:], i_misc[0:1, 0:1]), (nwc[:], i_nw[:]),
                     (qkbc[:], i_qkb[:]), (Ct[:], i_ct[:]), (lamg[:], i_lamg[:])):
            pr.run("sync", lambda e, oo=o, ii=i: e.dma_start(out=oo, in_=ii), inc=c_ld, k=16)

        with ExitStack() as b0:
            acc = b0.enter_context(nc.sbuf_tensor("acc", [128, 8192], F32))
            sjb = b0.enter_context(nc.sbuf_tensor("sjb", [128, 8192], BF16))
            ltv = b0.enter_context(nc.sbuf_tensor("ltv", [64, 7], F32))
            rr = b0.enter_context(nc.sbuf_tensor("rr", [64, 7], F32))
            coefr = b0.enter_context(nc.sbuf_tensor("coefr", [1, 448], F32))
            coefbc = b0.enter_context(nc.sbuf_tensor("coefbc", [128, 448], F32))
            pr.run("sync", lambda e: e.dma_start(out=ltv[:], in_=i_ltv[:]), inc=c_ld, k=16)
            pr.wait("vector", c_ld)
            pr.run("vector", lambda e: e.memset(rr[:, 6:7], 0.0), inc=c_dve)
            for j in range(5, -1, -1):
                pr.wait("vector", c_dve)
                pr.run("vector", lambda e, j=j: e.tensor_add(rr[:, j:j + 1], rr[:, j + 1:j + 2], ltv[:, j + 1:j + 2]), inc=c_dve)
            pr.wait("scalar", c_dve)
            pr.run("scalar", lambda e: e.activation(rr[:], rr[:], AF.Exp), inc=c_act)
            pr.wait("gpsimd", c_act)
            pr.run("gpsimd", lambda e: e.dma_start(out=d_row3[:].rearrange("(h j) -> h j", h=64), in_=rr[:]), inc=c_st, k=16)
            pr.wait("sync", c_st)
            pr.run("sync", lambda e: e.dma_start(out=coefr[:], in_=d_row3[:].unsqueeze(0)), inc=c_ld, k=16)
            pr.wait("tensor", c_ld)
            pr.run("tensor", lambda e: e.matmul(bkE[:, 0:448], ones1r[:], coefr[:], start=True, stop=True), inc=c_pe)
            pr.wait("vector", c_pe)
            pr.run("vector", lambda e: e.tensor_copy(coefbc[:], bkE[:, 0:448]), inc=c_dve)
            pr.run("vector", lambda e: e.memset(acc[:], 0.0), inc=c_dve)
            pr.wait("vector", c_dve)
            for j in range(7):
                pr.wait("sync", c_dve, pr.n(c_dve) if j else 0)
                pr.run("sync", lambda e, j=j: e.dma_start(out=sjb[:], in_=i_sall[j]), inc=c_ld, k=16)
                pr.wait("vector", c_ld)
                for h in range(64):
                    pr.run("vector", lambda e, j=j, h=h: e.scalar_tensor_tensor(
                        acc[:, h * 128:(h + 1) * 128], sjb[:, h * 128:(h + 1) * 128],
                        coefbc[:, h * 7 + j:h * 7 + j + 1], acc[:, h * 128:(h + 1) * 128],
                        ALU.mult, ALU.add), inc=(c_dve if h == 63 else None))
                pr.wait("vector", c_dve)
            pr.run("vector", lambda e: e.tensor_copy(hinb[:], acc[:]), inc=c_dve)

        pr.barrier(QMAP)

        with ExitStack() as b1:
            ylc = b1.enter_context(nc.sbuf_tensor("ylc", [128, 8192], BF16))
            szc = b1.enter_context(nc.sbuf_tensor("szc", [128, 8192], BF16))
            cwg = b1.enter_context(nc.sbuf_tensor("cwg", [128, 8192], BF16))
            ybuf = b1.enter_context(nc.sbuf_tensor("ybuf", [128, 8192], BF16))
            ynspan = b1.enter_context(nc.sbuf_tensor("ynspan", [128, 64 * 512], BF16))
            elgfm = b1.enter_context(nc.sbuf_tensor("elgfm", [64, 128], F32))
            elrp = [b1.enter_context(nc.sbuf_tensor(f"elrp{i}", [1, 512], F32)) for i in range(2)]
            tmp32 = b1.enter_context(nc.sbuf_tensor("tmp32", [128, 128], F32))
            ysq = [b1.enter_context(nc.sbuf_tensor(f"ysq{i}", [128, 128], BF16)) for i in range(2)]
            vrowp = b1.enter_context(nc.sbuf_tensor("vrowp", [1, 128], F32))
            rstd = b1.enter_context(nc.sbuf_tensor("rstd", [1, 128], F32))
            rstdbc = b1.enter_context(nc.sbuf_tensor("rstdbc", [128, 128], F32))
            wob = [b1.enter_context(nc.sbuf_tensor(f"wob{i}", [128, 8192], BF16)) for i in range(2)]
            _gsl = b1.enter_context(nc.sbuf_tensor("gsl", [128, 512], F32))
            _xsl = b1.enter_context(nc.sbuf_tensor("xsl", [128, 512], F32))
            gsl = [_gsl, _gsl]
            xsl = [_xsl, _xsl]
            x1b = [b1.enter_context(nc.sbuf_tensor(f"x1b{i}", [128, 512], F32)) for i in range(2)]
            x1sq = [b1.enter_context(nc.sbuf_tensor(f"x1sq{i}", [128, 512], BF16)) for i in range(2)]
            m2loc = b1.enter_context(nc.sbuf_tensor("m2loc", [1, 512], F32))
            trow = b1.enter_context(nc.sbuf_tensor("trow", [1, 512], F32))
            urow = b1.enter_context(nc.sbuf_tensor("urow", [1, 512], F32))

            for c in range(NCH):
                pr.wait("sync", c_dve)
                pr.run("sync", lambda e, c=c: e.dma_start(out=ylc[:], in_=i_ylc[c]), inc=c_ld, k=16)
                for m in range(64):
                    eng, cc = ("sync", c_ld) if m % 2 == 0 else ("gpsimd", c_gld)
                    pr.wait(eng, c_dve)
                    pr.run(eng, lambda e, m=m, c=c: e.dma_start(
                        out=szc[:, m * 128:(m + 1) * 128], in_=i_siluz[m, :, c * 128:(c + 1) * 128]), inc=cc, k=16)
                ld_mark = (pr.n(c_ld), pr.n(c_gld))
                pr.wait("scalar", c_dve)
                pr.run("scalar", lambda e, c=c: e.activation(elgfm[:], lamg[:, c * 128:(c + 1) * 128], AF.Exp), inc=c_act)
                pr.wait("gpsimd", c_act)
                pr.run("gpsimd", lambda e: e.dma_start(out=d_row1[:].rearrange("(h t) -> h t", h=64), in_=elgfm[:]), inc=c_st, k=16)
                for p in range(16):
                    lrp = elrp[p % 2]
                    pr.wait("sync", c_pe)
                    pr.wait("sync", c_st)
                    pr.run("sync", lambda e, p=p, lrp=lrp: e.dma_start(
                        out=lrp[:], in_=d_row1[p * 512:(p + 1) * 512].unsqueeze(0)), inc=c_ld, k=16)
                    pb = bkE
                    pr.wait("tensor", c_ld)
                    pr.wait("tensor", c_dve)
                    pr.run("tensor", lambda e, lrp=lrp, pb=pb: e.matmul(
                        pb[:], ones1r[:], lrp[:], start=True, stop=True), inc=c_pe)
                    pr.wait("vector", c_pe)
                    pr.run("vector", lambda e, p=p, c=c, pb=pb: e.tensor_tensor(
                        cwg[:, p * 512:(p + 1) * 512].rearrange("q (a b) -> q a b", a=4),
                        pb[:].rearrange("q (a b) -> q a b", a=4),
                        Ct[:, c * 128:(c + 1) * 128].unsqueeze(1).broadcast_to([128, 4, 128]),
                        ALU.mult), inc=c_dve)
                pr.wait("tensor", c_ld, ld_mark[0])
                pr.wait("tensor", c_gld, ld_mark[1])
                pr.wait("tensor", c_dve)
                pe_h, dve_h, act_h = {}, {}, {}
                for h in range(64):
                    pyb = (bkA, bkB)[h % 2]
                    pr.run("tensor", lambda e, h=h, pyb=pyb: e.matmul(
                        pyb[:, 0:128], hinb[:, h * 128:(h + 1) * 128], cwg[:, h * 128:(h + 1) * 128],
                        start=True, stop=True), inc=c_pe)
                    pe_h[h] = pr.n(c_pe)
                    pr.wait("vector", c_pe, pe_h[h])
                    pr.run("vector", lambda e, h=h, pyb=pyb: e.tensor_add(
                        tmp32[:], ylc[:, h * 128:(h + 1) * 128], pyb[:, 0:128]), inc=c_dve)
                    pr.wait("vector", c_dve)
                    pr.run("vector", lambda e, h=h: e.tensor_mul(
                        ybuf[:, h * 128:(h + 1) * 128], tmp32[:], szc[:, h * 128:(h + 1) * 128]), inc=c_dve)
                    dve_h[h] = pr.n(c_dve)
                    pr.wait("scalar", c_dve, dve_h[h])
                    pr.wait("scalar", c_pe, act_h.get(h - 2, 0))
                    pr.run("scalar", lambda e, h=h: e.activation(
                        ysq[h % 2][:], ybuf[:, h * 128:(h + 1) * 128], AF.Square), inc=c_act)
                    pr.wait("tensor", c_act)
                    vv = pr.run("tensor", lambda e, h=h: e.matmul(
                        bkS1[0:1, 0:128], oc_bf[:], ysq[h % 2][:],
                        start=(h == 0), stop=(h == 63)), inc=c_pe)
                    act_h[h] = vv
                    if h >= 1:
                        pr.wait("tensor", c_dve, dve_h[h - 1])
                pr.wait("scalar", c_pe)
                pr.run("scalar", lambda e: e.activation(vrowp[:], bkS1[0:1, 0:128], AF.Ln, bias=epsc[:], scale=1.0 / 8192.0), inc=c_act)
                pr.wait("vector", c_act)
                pr.run("vector", lambda e: e.tensor_scalar_mul(vrowp[:], vrowp[:], -0.5), inc=c_dve)
                pr.wait("scalar", c_dve)
                pr.run("scalar", lambda e: e.activation(rstd[:], vrowp[:], AF.Exp), inc=c_act)
                pr.wait("tensor", c_act)
                pr.run("tensor", lambda e: e.matmul(bkE[:, 0:128], ones1r[:], rstd[:], start=True, stop=True), inc=c_pe)
                pr.wait("vector", c_pe)
                pr.run("vector", lambda e: e.tensor_copy(rstdbc[:], bkE[:, 0:128]), inc=c_dve)
                pr.wait("vector", c_dve)
                for h in range(64):
                    pr.run("vector", lambda e, h=h, c=c: e.scalar_tensor_tensor(
                        ynspan[:, h * 512 + (c % 4) * 128:h * 512 + (c % 4) * 128 + 128],
                        ybuf[:, h * 128:(h + 1) * 128], nwc[:, h:h + 1], rstdbc[:],
                        ALU.mult, ALU.mult), inc=(c_dve if h == 63 else None))

                if c % 4 == 3:
                    s = c // 4
                    wpe, wev, wld = {}, {}, {}
                    pr.wait("tensor", c_dve)
                    pr.wait("tensor", c_act)
                    for Mc in range(32):
                        wb = wob[Mc % 2]
                        pr.wait("gpsimd", c_pe, wpe.get(Mc - 2, 0))
                        gvv = pr.run("gpsimd", lambda e, Mc=Mc, wb=wb: e.dma_start(out=wb[:], in_=i_wout[Mc]), inc=c_gld, k=16)
                        pr.wait("sync", c_dve, wev.get(Mc - 1, 0))
                        pr.run("sync", lambda e, Mc=Mc, s=s: e.dma_start(
                            out=gsl[Mc % 2][:], in_=i_g[Mc, :, s * 512:(s + 1) * 512]), inc=c_ld, k=16)
                        wld[Mc] = pr.run("sync", lambda e, Mc=Mc, s=s: e.dma_start(
                            out=xsl[Mc % 2][:], in_=i_xtb[Mc, :, s * 512:(s + 1) * 512]), inc=c_ld, k=16)
                        pw = (bkC, bkD)[Mc % 2]
                        pr.wait("tensor", c_gld, gvv)
                        if Mc >= 2:
                            pr.wait("tensor", c_dve, wev[Mc - 2])
                        for kc in range(64):
                            pr.run("tensor", lambda e, kc=kc, wb=wb, pw=pw: e.matmul(
                                pw[:], wb[:, kc * 128:(kc + 1) * 128], ynspan[:, kc * 512:(kc + 1) * 512],
                                start=(kc == 0), stop=(kc == 63)), inc=(None if kc < 63 else c_pe))
                        pev = pr.n(c_pe)
                        wpe[Mc] = pev
                        pr.wait("vector", c_pe, pev)
                        pr.wait("vector", c_ld, wld[Mc])
                        pr.run("vector", lambda e, Mc=Mc, pw=pw: e.tensor_mul(x1b[Mc % 2][:], pw[:], gsl[Mc % 2][:]), inc=c_dve)
                        pr.wait("vector", c_dve)
                        pr.run("vector", lambda e, Mc=Mc: e.tensor_add(x1b[Mc % 2][:], x1b[Mc % 2][:], xsl[Mc % 2][:]), inc=c_dve)
                        x1v = pr.n(c_dve)
                        wev[Mc] = x1v
                        pr.wait("gpsimd", c_dve, x1v)
                        pr.run("gpsimd", lambda e, Mc=Mc, s=s: e.dma_start(
                            out=o_x1[Mc, :, s * 512:(s + 1) * 512], in_=x1b[Mc % 2][:]), inc=c_st, k=16)
                        pr.wait("scalar", c_dve, x1v)
                        pr.wait("scalar", c_pe)
                        pr.run("scalar", lambda e, Mc=Mc: e.activation(x1sq[Mc % 2][:], x1b[Mc % 2][:], AF.Square), inc=c_act)
                        pr.wait("tensor", c_dve, x1v)
                        pr.run("tensor", lambda e, Mc=Mc: e.matmul(
                            bkS1[0:1, 0:512], onescol[:], x1b[Mc % 2][:],
                            start=(Mc == 0), stop=(Mc == 31)), inc=None)
                        pr.wait("tensor", c_act)
                        pr.run("tensor", lambda e, Mc=Mc: e.matmul(
                            bkS2[0:1, 0:512], oc_bf[:], x1sq[Mc % 2][:],
                            start=(Mc == 0), stop=(Mc == 31)), inc=(None if Mc < 31 else c_pe))
                    pr.wait("vector", c_pe)
                    pr.run("vector", lambda e: e.tensor_scalar_mul(m2loc[:], bkS1[0:1, 0:512], 1.0 / HD), inc=c_dve)
                    pr.run("vector", lambda e: e.tensor_scalar_mul(trow[:], bkS2[0:1, 0:512], 1.0 / HD), inc=c_dve)
                    pr.wait("vector", c_dve)
                    pr.run("vector", lambda e: e.tensor_mul(urow[:], m2loc[:], m2loc[:]), inc=c_dve)
                    pr.wait("vector", c_dve)
                    pr.run("vector", lambda e: e.tensor_sub(trow[:], trow[:], urow[:]), inc=c_dve)
                    pr.wait("scalar", c_dve)
                    pr.run("scalar", lambda e: e.activation(trow[:], trow[:], AF.Sqrt, bias=epsc[:]), inc=c_act)
                    pr.wait("vector", c_act)
                    pr.run("vector", lambda e: e.reciprocal(urow[:], trow[:]), inc=c_dve)
                    pr.wait("gpsimd", c_dve)
                    pr.run("gpsimd", lambda e, s=s: e.dma_start(
                        out=d_rowM[s * 512:(s + 1) * 512].unsqueeze(0), in_=m2loc[:]), inc=c_st, k=16)
                    pr.run("gpsimd", lambda e, s=s: e.dma_start(
                        out=d_rowR[s * 512:(s + 1) * 512].unsqueeze(0), in_=urow[:]), inc=c_st, k=16)

        pr.barrier(QMAP)

        with ExitStack() as b2:
            xn2sb = b2.enter_context(nc.sbuf_tensor("xn2sb", [128, 32 * T], BF16))
            m2bc = b2.enter_context(nc.sbuf_tensor("m2bc", [128, T], F32))
            r2bc = b2.enter_context(nc.sbuf_tensor("r2bc", [128, T], F32))
            x1t = [b2.enter_context(nc.sbuf_tensor(f"x1t{i}", [128, T], F32)) for i in range(2)]
            t2b = b2.enter_context(nc.sbuf_tensor("t2b", [128, T], F32))
            qkblk = [b2.enter_context(nc.sbuf_tensor(f"qkblk{i}", [128, 4096], BF16)) for i in range(2)]
            qbuf = [b2.enter_context(nc.sbuf_tensor(f"qbuf{i}", [128, 512], BF16)) for i in range(2)]
            vwcol = b2.enter_context(nc.sbuf_tensor("vwcol", [128, 32 * 512], BF16))
            vbr = b2.enter_context(nc.sbuf_tensor("vbr", [1, 4096], BF16))
            vbuf = [b2.enter_context(nc.sbuf_tensor(f"vbuf{i}", [128, 512], BF16)) for i in range(2)]
            m2row = b2.enter_context(nc.sbuf_tensor("m2row", [1, T], F32))
            r2row = b2.enter_context(nc.sbuf_tensor("r2row", [1, T], F32))
            pr.run("sync", lambda e: e.dma_start(out=m2row[:], in_=d_rowM[:].unsqueeze(0)), inc=c_ld, k=16)
            pr.run("sync", lambda e: e.dma_start(out=r2row[:], in_=d_rowR[:].unsqueeze(0)), inc=c_ld, k=16)
            pr.wait("tensor", c_ld)

            for half in range(2):
                pr.wait("tensor", c_dve)
                pr.run("tensor", lambda e, half=half: e.matmul(
                    bkE[:], ones1r[:], m2row[0:1, half * 512:(half + 1) * 512], start=True, stop=True), inc=c_pe)
                pr.wait("vector", c_pe)
                pr.run("vector", lambda e, half=half: e.tensor_copy(m2bc[:, half * 512:(half + 1) * 512], bkE[:]), inc=c_dve)
                pr.wait("tensor", c_dve)
                pr.run("tensor", lambda e, half=half: e.matmul(
                    bkE[:], ones1r[:], r2row[0:1, half * 512:(half + 1) * 512], start=True, stop=True), inc=c_pe)
                pr.wait("vector", c_pe)
                pr.run("vector", lambda e, half=half: e.tensor_copy(r2bc[:, half * 512:(half + 1) * 512], bkE[:]), inc=c_dve)
            pr.wait("vector", c_dve)
            for kc in range(32):
                pr.wait("sync", c_dve)
                pr.run("sync", lambda e, kc=kc: e.dma_start(out=x1t[kc % 2][:], in_=o_x1[kc]), inc=c_ld, k=16)
                pr.wait("vector", c_ld)
                pr.run("vector", lambda e, kc=kc: e.tensor_sub(t2b[:], x1t[kc % 2][:], m2bc[:]), inc=c_dve)
                pr.wait("vector", c_dve)
                pr.run("vector", lambda e, kc=kc: e.tensor_mul(xn2sb[:, kc * T:(kc + 1) * T], t2b[:], r2bc[:]), inc=c_dve)
                pr.wait("gpsimd", c_dve)
                pr.run("gpsimd", lambda e, kc=kc: e.dma_start(out=o_xn2[kc], in_=xn2sb[:, kc * T:(kc + 1) * T]), inc=c_st, k=16)
            qpe, qev = {}, {}
            for Mc in range(64):
                qk = qkblk[Mc % 2]
                pr.wait("gpsimd", c_pe, qpe.get(Mc - 2, 0))
                gvv = pr.run("gpsimd", lambda e, Mc=Mc, qk=qk: e.dma_start(out=qk[:], in_=i_wqk[Mc]), inc=c_gld, k=16)
                pr.wait("tensor", c_gld, gvv)
                for half in range(2):
                    pq = (bkC, bkD)[half]
                    pr.wait("tensor", c_act, qev.get((Mc - 1, half), 0))
                    for kc in range(32):
                        pr.run("tensor", lambda e, kc=kc, qk=qk, pq=pq, half=half: e.matmul(
                            pq[:], qk[:, kc * 128:(kc + 1) * 128],
                            xn2sb[:, kc * T + half * 512:kc * T + half * 512 + 512],
                            start=(kc == 0), stop=(kc == 31)), inc=(None if kc < 31 else c_pe))
                    pev = pr.n(c_pe)
                    qpe[Mc] = pev
                    qb = qbuf[half]
                    pr.wait("vector", c_pe, pev)
                    pr.wait("vector", c_st, qev.get(("st", Mc - 1, half), 0))
                    qev[(Mc, half)] = pr.run("vector", lambda e, Mc=Mc, qb=qb, pq=pq: e.tensor_scalar_add(
                        qb[:], pq[:], qkbc[:, Mc:Mc + 1]), inc=c_act)
                    pr.wait("gpsimd", c_act, qev[(Mc, half)])
                    if Mc < 32:
                        qev[("st", Mc, half)] = pr.run("gpsimd", lambda e, Mc=Mc, half=half, qb=qb: e.dma_start(
                            out=o_q[Mc, :, half * 512:(half + 1) * 512], in_=qb[:]), inc=c_st, k=16)
                    else:
                        qev[("st", Mc, half)] = pr.run("gpsimd", lambda e, Mc=Mc, half=half, qb=qb: e.dma_start(
                            out=o_k[Mc - 32, :, half * 512:(half + 1) * 512], in_=qb[:]), inc=c_st, k=16)
            pr.run("gpsimd", lambda e: e.dma_start(out=vbr[:], in_=i_qvb[:]), inc=c_gld, k=16)
            for vc in range(8):
                pr.wait("gpsimd", c_pe)
                pr.run("gpsimd", lambda e, vc=vc: e.dma_start(out=vwcol[:], in_=i_wv[vc]), inc=c_gld, k=16)
                pr.wait("tensor", c_gld)
                for tc in range(8):
                    pv = (bkC, bkD)[tc % 2]
                    pr.wait("tensor", c_act)
                    for kc in range(32):
                        pr.run("tensor", lambda e, kc=kc, tc=tc, pv=pv: e.matmul(
                            pv[:], xn2sb[:, kc * T + tc * 128:kc * T + tc * 128 + 128],
                            vwcol[:, kc * 512:(kc + 1) * 512],
                            start=(kc == 0), stop=False), inc=None)
                    pr.run("tensor", lambda e, vc=vc, pv=pv: e.matmul(
                        pv[:], o1r_bf[:], vbr[0:1, vc * 512:(vc + 1) * 512],
                        start=False, stop=True), inc=c_pe)
                    pev = pr.n(c_pe)
                    vb = vbuf[tc % 2]
                    pr.wait("scalar", c_pe, pev)
                    pr.wait("scalar", c_st)
                    pr.run("scalar", lambda e, vb=vb, pv=pv: e.activation(vb[:], pv[:], AF.Copy), inc=c_act)
                    pr.wait("gpsimd", c_act)
                    pr.run("gpsimd", lambda e, tc=tc, vc=vc, vb=vb: e.dma_start(
                        out=o_v[tc, :, vc * 512:(vc + 1) * 512], in_=vb[:]), inc=c_st, k=16)
            pr.wait("gpsimd", c_st)
            pr.wait("sync", c_ld)
        pr.replay()
    return nc


def stage_B(inp, a_outs):
    xT = np.ascontiguousarray(np.asarray(inp["L0_attn_hidden_states"], np.float32).reshape(S, HD).T)
    wout_b = _blocked(np.asarray(inp["L0_attn_W_out"], np.float32)).astype(BF)
    qkvw = np.asarray(inp["L1_attn_attn_qkvw"], np.float32)
    wqk_b = _blocked(qkvw[:, 0:8192]).astype(BF)
    wv = np.ascontiguousarray(
        qkvw[:, 8192:12288].reshape(32, 128, 8, 512).transpose(2, 1, 0, 3).reshape(8, 128, 32 * 512)).astype(BF)
    qkvb = np.asarray(inp["L1_attn_attn_qkvb"], np.float32)
    qkb = np.ascontiguousarray(qkvb[0:8192].reshape(64, 128).T)
    qvb = np.ascontiguousarray(qkvb[8192:12288].reshape(1, 4096)).astype(BF)
    nw = np.ascontiguousarray(np.asarray(inp["L0_attn_norm_weight"], np.float32).reshape(64, 128).T)
    ones = np.ones((128, 128), np.float32)
    shared = dict(i_wout=wout_b, i_nw=nw, i_qkb=qkb, i_qvb=qvb, i_wqk=wqk_b, i_wv=wv,
                  i_ones=ones, i_onescol=np.ones((128, 1), np.float32),
                  i_oc_bf=np.ones((128, 1), BF), i_o1r_bf=np.ones((1, 128), BF),
                  i_misc=np.array([[EPS, 1.0, 0.0, 0.0]], np.float32))
    per_core = []
    ltots = [np.asarray(a_outs[c]["o_lamg"], np.float32)[:, T - 1] for c in range(NCORES)]
    for c in range(NCORES):
        sall = np.zeros((7, 128, 8192), BF)
        ltv = np.zeros((64, 7), np.float32)
        for j in range(min(c, 7)):
            sall[j] = np.asarray(a_outs[j]["o_scor"])
        for m in range(1, 7):
            if m <= c - 1:
                ltv[:, m] = ltots[m]
        per_core.append(dict(
            i_xtb=np.ascontiguousarray(xT[:, c * T:(c + 1) * T].reshape(32, 128, T)),
            i_ylc=np.asarray(a_outs[c]["o_ylc"]),
            i_siluz=np.asarray(a_outs[c]["o_siluz"]),
            i_ct=np.asarray(a_outs[c]["o_xbcc"])[65],
            i_lamg=np.asarray(a_outs[c]["o_lamg"], np.float32),
            i_sall=sall, i_ltv=ltv,
            i_g=np.asarray(a_outs[c]["o_g"], np.float32)))
    return shared, per_core


def build_C():
    nc = bass.Bass("TRN2", target_bir_lowering=False, debug=False, detect_race_conditions=False)
    i_q = nc.dram_tensor("i_q", [32, 128, T], BF16, kind="ExternalInput")
    i_kf = nc.dram_tensor("i_kf", [32, 128, 8192], BF16, kind="ExternalInput")
    i_vf = nc.dram_tensor("i_vf", [32, 128, 8192], BF16, kind="ExternalInput")
    i_x1 = nc.dram_tensor("i_x1", [32, 128, T], F32, kind="ExternalInput")
    i_xn2 = nc.dram_tensor("i_xn2", [32, 128, T], BF16, kind="ExternalInput")
    i_wo = nc.dram_tensor("i_wo", [32, 128, 4096], BF16, kind="ExternalInput")
    i_w1t = nc.dram_tensor("i_w1t", [32, 128, 4096], BF16, kind="ExternalInput")
    i_ob = nc.dram_tensor("i_ob", [128, 32], F32, kind="ExternalInput")
    i_wsc = nc.dram_tensor("i_wsc", [128, 32], F32, kind="ExternalInput")
    i_wbc = nc.dram_tensor("i_wbc", [128, 32], F32, kind="ExternalInput")
    i_ones = nc.dram_tensor("i_ones", [128, 128], F32, kind="ExternalInput")
    i_oc_bf = nc.dram_tensor("i_oc_bf", [128, 1], BF16, kind="ExternalInput")
    i_smsh = nc.dram_tensor("i_smsh", [128, 1], F32, kind="ExternalInput")

    o_out = nc.dram_tensor("o_out", [32, 128, T], F32, kind="ExternalOutput")

    ctx = ExitStack()
    with ctx:
        pr = Prog(nc, ctx)
        c_ld = pr.ctr("c_ld")
        c_gld = pr.ctr("c_gld")
        c_st = pr.ctr("c_st")
        c_pe = pr.ctr("c_pe")
        c_dve = pr.ctr("c_dve")
        c_act = pr.ctr("c_act")
        QMAP = {"sync": [c_ld], "gpsimd": [c_gld, c_st]}
        sbt = lambda nm, sh, dt: ctx.enter_context(nc.sbuf_tensor(nm, sh, dt))

        ones1r = sbt("ones1r", [1, 128], F32)
        oc_bf = sbt("oc_bf", [128, 1], BF16)
        smsh = sbt("smsh", [128, 1], F32)
        obc = sbt("obc", [128, 32], F32)
        wscc = sbt("wscc", [128, 32], F32)
        wbcc = sbt("wbcc", [128, 32], F32)
        ctxb = sbt("ctxb", [128, 32 * T], BF16)

        for o, i in ((ones1r[:], i_ones[0:1, :]), (oc_bf[:], i_oc_bf[:]), (smsh[:], i_smsh[:]),
                     (obc[:], i_ob[:]), (wscc[:], i_wsc[:]), (wbcc[:], i_wbc[:])):
            pr.run("sync", lambda e, oo=o, ii=i: e.dma_start(out=oo, in_=ii), inc=c_ld, k=16)

        with ExitStack() as ca:
            pSc = [ca.enter_context(nc.psum_tensor(f"pSc{i}", [128, 1024], F32)) for i in range(2)]
            pCx = ca.enter_context(nc.psum_tensor("pCx", [128, 1024], F32))
            pZ = ca.enter_context(nc.psum_tensor("pZ", [1, 1024], F32))
            khb = [ca.enter_context(nc.sbuf_tensor(f"khb{i}", [128, 8192], BF16)) for i in range(2)]
            vhb = [ca.enter_context(nc.sbuf_tensor(f"vhb{i}", [128, 8192], BF16)) for i in range(2)]
            qhb = [ca.enter_context(nc.sbuf_tensor(f"qhb{i}", [128, T], BF16)) for i in range(2)]
            probs = [ca.enter_context(nc.sbuf_tensor(f"probs{i}", [128, 1024], BF16)) for i in range(2)]
            zr = ca.enter_context(nc.sbuf_tensor("zr", [1, 1024], F32))
            zbc = ca.enter_context(nc.sbuf_tensor("zbc", [128, 1024], F32))

            def load_head(h):
                b = h % 2
                pr.wait("sync", c_pe)
                pr.wait("gpsimd", c_pe)
                pr.run("sync", lambda e, h=h, b=b: e.dma_start(out=khb[b][:], in_=i_kf[h]), inc=c_ld, k=16)
                pr.run("gpsimd", lambda e, h=h, b=b: e.dma_start(out=vhb[b][:], in_=i_vf[h]), inc=c_gld, k=16)
                pr.run("sync", lambda e, h=h, b=b: e.dma_start(out=qhb[b][:], in_=i_q[h]), inc=c_ld, k=16)

            load_head(0)
            for h in range(32):
                b = h % 2
                marks = (pr.n(c_ld), pr.n(c_gld))
                if h + 1 < 32:
                    load_head(h + 1)
                pr.wait("tensor", c_ld, marks[0])
                pr.wait("tensor", c_gld, marks[1])
                act_k, pe_k, sc_k = {}, {}, {}

                def emit_scores(kc):
                    sc = pSc[kc % 2]
                    pr.wait("tensor", c_act, act_k.get(kc - 2, 0))
                    for half in range(2):
                        pr.run("tensor", lambda e, kc=kc, b=b, sc=sc, half=half: e.matmul(
                            sc[:, half * 512:(half + 1) * 512],
                            khb[b][:, kc * 128:(kc + 1) * 128],
                            qhb[b][:, half * 512:(half + 1) * 512],
                            start=True, stop=True), inc=(None if half == 0 else c_pe))
                    sc_k[kc] = pr.n(c_pe)

                emit_scores(0)
                for kc in range(64):
                    sc = pSc[kc % 2]
                    if kc + 1 < 64:
                        emit_scores(kc + 1)
                    pr.wait("scalar", c_pe, sc_k[kc])
                    pr.wait("scalar", c_pe, pe_k.get(kc - 2, 0))
                    for half in range(2):
                        pr.run("scalar", lambda e, kc=kc, sc=sc, half=half: e.activation(
                            probs[kc % 2][:, half * 512:(half + 1) * 512],
                            sc[:, half * 512:(half + 1) * 512],
                            AF.Exp, bias=smsh[:], scale=SM_SCALE), inc=(None if half == 0 else c_act))
                    act_k[kc] = pr.n(c_act)
                    pr.wait("tensor", c_act, act_k[kc])
                    for half in range(2):
                        pr.run("tensor", lambda e, kc=kc, b=b, half=half: e.matmul(
                            pCx[:, half * 512:(half + 1) * 512],
                            vhb[b][:, kc * 128:(kc + 1) * 128],
                            probs[kc % 2][:, half * 512:(half + 1) * 512],
                            start=(kc == 0), stop=(kc == 63)), inc=None)
                        pr.run("tensor", lambda e, kc=kc, half=half: e.matmul(
                            pZ[0:1, half * 512:(half + 1) * 512], oc_bf[:],
                            probs[kc % 2][:, half * 512:(half + 1) * 512],
                            start=(kc == 0), stop=(kc == 63)), inc=(None if half == 0 else c_pe))
                    pe_k[kc] = pr.n(c_pe)
                pr.wait("vector", c_pe)
                pr.run("vector", lambda e: e.reciprocal(zr[:], pZ[0:1, :]), inc=c_dve)
                pr.wait("tensor", c_dve)
                for half in range(2):
                    pr.run("tensor", lambda e, half=half: e.matmul(
                        pSc[0][:, half * 512:(half + 1) * 512], ones1r[:],
                        zr[0:1, half * 512:(half + 1) * 512], start=True, stop=True),
                        inc=(None if half == 0 else c_pe))
                pr.wait("vector", c_pe)
                pr.run("vector", lambda e: e.tensor_copy(zbc[:], pSc[0][:]), inc=c_dve)
                pr.wait("vector", c_dve)
                pr.run("vector", lambda e, h=h: e.tensor_mul(
                    ctxb[:, h * T:(h + 1) * T], pCx[:], zbc[:]), inc=c_dve)
                pr.wait("tensor", c_dve)

        pr.barrier(QMAP)

        with ExitStack() as cb:
            pA = cb.enter_context(nc.psum_tensor("pA", [128, 512], F32))
            pB = cb.enter_context(nc.psum_tensor("pB", [128, 512], F32))
            pX = cb.enter_context(nc.psum_tensor("pX", [1, 1024], F32))
            pBc = cb.enter_context(nc.psum_tensor("pBc", [128, 1024], F32))
            xn2sb = cb.enter_context(nc.sbuf_tensor("xn2sb", [128, 32 * T], BF16))
            woblk = [cb.enter_context(nc.sbuf_tensor(f"woblk{i}", [128, 4096], BF16)) for i in range(2)]
            w1blk = [cb.enter_context(nc.sbuf_tensor(f"w1blk{i}", [128, 4096], BF16)) for i in range(2)]
            sgnb = cb.enter_context(nc.sbuf_tensor("sgnb", [128, 4096], BF16))
            xsumr = cb.enter_context(nc.sbuf_tensor("xsumr", [1, 1024], F32))
            xsbc = cb.enter_context(nc.sbuf_tensor("xsbc", [128, 1024], F32))
            x1sl = [cb.enter_context(nc.sbuf_tensor(f"x1sl{i}", [128, 512], F32)) for i in range(2)]
            g1b = cb.enter_context(nc.sbuf_tensor("g1b", [128, 512], F32))
            t1c = cb.enter_context(nc.sbuf_tensor("t1c", [128, 512], F32))
            t2c = cb.enter_context(nc.sbuf_tensor("t2c", [128, 512], F32))
            outb = [cb.enter_context(nc.sbuf_tensor(f"outb{i}", [128, 512], F32)) for i in range(2)]

            for kc in range(32):
                pr.run("sync", lambda e, kc=kc: e.dma_start(
                    out=xn2sb[:, kc * T:(kc + 1) * T], in_=i_xn2[kc]), inc=c_ld, k=16)
            pr.wait("tensor", c_ld)
            for kc in range(32):
                for half in range(2):
                    pr.run("tensor", lambda e, kc=kc, half=half: e.matmul(
                        pX[0:1, half * 512:(half + 1) * 512], oc_bf[:],
                        xn2sb[:, kc * T + half * 512:kc * T + half * 512 + 512],
                        start=(kc == 0), stop=(kc == 31)), inc=(None if (kc < 31 or half == 0) else c_pe))
            pr.wait("vector", c_pe)
            pr.run("vector", lambda e: e.tensor_copy(xsumr[:], pX[0:1, :]), inc=c_dve)
            pr.wait("tensor", c_dve)
            for half in range(2):
                pr.run("tensor", lambda e, half=half: e.matmul(
                    pBc[:, half * 512:(half + 1) * 512], ones1r[:],
                    xsumr[0:1, half * 512:(half + 1) * 512], start=True, stop=True),
                    inc=(None if half == 0 else c_pe))
            pr.wait("vector", c_pe)
            pr.run("vector", lambda e: e.tensor_copy(xsbc[:], pBc[:]), inc=c_dve)
            pr.wait("vector", c_dve)

            for th in range(2):
                for nc_ in range(32):
                    wo = woblk[nc_ % 2]
                    w1 = w1blk[nc_ % 2]
                    pr.wait("gpsimd", c_pe)
                    pr.run("gpsimd", lambda e, nc_=nc_, wo=wo: e.dma_start(out=wo[:], in_=i_wo[nc_]), inc=c_gld, k=16)
                    pr.run("gpsimd", lambda e, nc_=nc_, w1=w1: e.dma_start(out=w1[:], in_=i_w1t[nc_]), inc=c_gld, k=16)
                    pr.wait("sync", c_dve)
                    pr.run("sync", lambda e, nc_=nc_, th=th: e.dma_start(
                        out=x1sl[nc_ % 2][:], in_=i_x1[nc_, :, th * 512:(th + 1) * 512]), inc=c_ld, k=16)
                    pr.wait("scalar", c_gld)
                    pr.wait("scalar", c_pe)
                    pr.run("scalar", lambda e, w1=w1: e.activation(sgnb[:], w1[:], AF.Sign), inc=c_act)
                    pr.wait("tensor", c_gld)
                    pr.wait("tensor", c_dve)
                    for kc in range(32):
                        pr.run("tensor", lambda e, kc=kc, wo=wo, th=th: e.matmul(
                            pA[:], wo[:, kc * 128:(kc + 1) * 128],
                            ctxb[:, kc * T + th * 512:kc * T + th * 512 + 512],
                            start=(kc == 0), stop=(kc == 31)), inc=(None if kc < 31 else c_pe))
                    pr.wait("tensor", c_act)
                    for kc in range(32):
                        pr.run("tensor", lambda e, kc=kc, th=th: e.matmul(
                            pB[:], sgnb[:, kc * 128:(kc + 1) * 128],
                            xn2sb[:, kc * T + th * 512:kc * T + th * 512 + 512],
                            start=(kc == 0), stop=(kc == 31)), inc=(None if kc < 31 else c_pe))
                    pev = pr.n(c_pe)
                    pr.wait("vector", c_pe, pev)
                    pr.wait("vector", c_ld)
                    pr.run("vector", lambda e, nc_=nc_: e.tensor_scalar_mul(t1c[:], pB[:], wscc[:, nc_:nc_ + 1]), inc=c_dve)
                    pr.wait("vector", c_dve)
                    pr.run("vector", lambda e, nc_=nc_, th=th: e.scalar_tensor_tensor(
                        t2c[:], xsbc[:, th * 512:(th + 1) * 512], wbcc[:, nc_:nc_ + 1], t1c[:],
                        ALU.mult, ALU.add), inc=c_dve)
                    pr.wait("scalar", c_dve)
                    pr.run("scalar", lambda e: e.activation(g1b[:], t2c[:], AF.Silu), inc=c_act)
                    pr.wait("vector", c_act)
                    pr.run("vector", lambda e, nc_=nc_: e.tensor_scalar_add(t1c[:], pA[:], obc[:, nc_:nc_ + 1]), inc=c_dve)
                    pr.wait("vector", c_dve)
                    pr.run("vector", lambda e: e.tensor_mul(t2c[:], t1c[:], g1b[:]), inc=c_dve)
                    pr.wait("vector", c_dve)
                    pr.run("vector", lambda e, nc_=nc_: e.tensor_add(outb[nc_ % 2][:], t2c[:], x1sl[nc_ % 2][:]), inc=c_dve)
                    pr.wait("gpsimd", c_dve)
                    pr.run("gpsimd", lambda e, nc_=nc_, th=th: e.dma_start(
                        out=o_out[nc_, :, th * 512:(th + 1) * 512], in_=outb[nc_ % 2][:]), inc=c_st, k=16)
            pr.wait("gpsimd", c_st)
            pr.wait("sync", c_ld)
        pr.replay()
    return nc


def stage_C(inp, b_outs):
    wo_b = _blocked(np.asarray(inp["L1_attn_attn_ow"], np.float32)).astype(BF)
    w1t = np.ascontiguousarray(np.asarray(inp["L1_ffn_weight"], np.float32).T)
    w1t_b = _blocked(w1t).astype(BF)
    ob = np.ascontiguousarray(np.asarray(inp["L1_attn_attn_ob"], np.float32).reshape(32, 128).T)
    wsc = np.ascontiguousarray(np.asarray(inp["L1_ffn_wscale"], np.float32).reshape(32, 128).T)
    wbc = np.ascontiguousarray(np.asarray(inp["L1_ffn_wbias"], np.float32).reshape(32, 128).T)
    kf = np.concatenate([np.asarray(b_outs[c]["o_k"]) for c in range(NCORES)], axis=2)
    vg = np.concatenate([np.asarray(b_outs[c]["o_v"]) for c in range(NCORES)], axis=0)
    vf = np.ascontiguousarray(vg.reshape(64, 128, 32, 128).transpose(2, 1, 0, 3).reshape(32, 128, 8192))
    shared = dict(i_kf=kf, i_vf=vf, i_wo=wo_b, i_w1t=w1t_b, i_ob=ob, i_wsc=wsc, i_wbc=wbc,
                  i_ones=np.ones((128, 128), np.float32),
                  i_oc_bf=np.ones((128, 1), BF),
                  i_smsh=np.full((128, 1), SM_SHIFT, np.float32))
    per_core = []
    for c in range(NCORES):
        per_core.append(dict(i_q=np.asarray(b_outs[c]["o_q"]),
                             i_x1=np.asarray(b_outs[c]["o_x1"], np.float32),
                             i_xn2=np.asarray(b_outs[c]["o_xn2"])))
    return shared, per_core


def run_B(inp, a_outs):
    nc = build_B()
    shared, per_core = stage_B(inp, a_outs)
    res = run_bass_kernel_spmd(nc, [dict(shared, **pc) for pc in per_core], core_ids=list(range(NCORES)))
    return res.results


def run_C(inp, b_outs):
    nc = build_C()
    shared, per_core = stage_C(inp, b_outs)
    res = run_bass_kernel_spmd(nc, [dict(shared, **pc) for pc in per_core], core_ids=list(range(NCORES)))
    return res.results


def kernel(**inputs):
    a_outs = run_A(inputs)
    b_outs = run_B(inputs, a_outs)
    c_outs = run_C(inputs, b_outs)
    out = np.concatenate(
        [np.asarray(c_outs[c]["o_out"], np.float32).reshape(HD, T) for c in range(NCORES)], axis=1)
    return np.ascontiguousarray(out.T).reshape(1, S, HD).astype(np.float32)
